# version 03
# baseline (speedup 1.0000x reference)
import numpy as np
from contextlib import ExitStack

import concourse.bass as bass
import concourse.mybir as mybir
from concourse.bass_utils import run_bass_kernel_spmd

F32 = mybir.dt.float32
F16 = mybir.dt.float16
BF16 = mybir.dt.bfloat16
I32 = mybir.dt.int32
AF = mybir.ActivationFunctionType
OP = mybir.AluOpType
AX = mybir.AxisListType.X

B, S, H, FF, E = 2, 8192, 1024, 3584, 8
CAP = 2560
SH = 2048
NT = S // 128
NG = NT // 4
QT = SH // 128
EPAD = 640
EPT = EPAD // 128
HT = H // 128
FT = FF // 128
LN2 = float(np.log(2.0))
INV_LN2 = float(1.0 / np.log(2.0))
C1 = 0.5 * INV_LN2
BIGF = float(1 << 16)
PHASES = set("ABCDEFG")
G_GATHERS = True
DEBUG_OUTS = False


class PB:

    ENG = ("sync", "tensor", "vector", "scalar", "gpsimd")

    def __init__(self, nc, sem, gsem):
        self.nc = nc
        self.sem = sem
        self.gsem = gsem
        self.cnt = 0
        self.gcnt = 0
        self.last_eng = None
        self.last_dma = False
        self.ops = {e: [] for e in self.ENG}

    def _emit(self, eng, fn, inc, ginc=0):
        wait_at = self.cnt if self.cnt else None
        gwait_at = self.gcnt if self.gcnt else None
        sem, gsem = self.sem, self.gsem

        def closure(engine, wait_at=wait_at, gwait_at=gwait_at, fn=fn,
                    inc=inc, ginc=ginc):
            if gwait_at:
                engine.wait_ge(gsem, gwait_at)
            if wait_at:
                engine.wait_ge(sem, wait_at)
            inst = fn()
            if ginc:
                inst.then_inc(gsem, ginc)
            else:
                inst.then_inc(sem, inc)

        self.ops[eng].append(closure)
        self.cnt += inc
        self.gcnt += ginc
        self.last_eng = eng
        self.last_dma = (inc >= 16) or ginc

    def t(self, fn):
        self._emit("tensor", fn, 1)

    def v(self, fn):
        self._emit("vector", fn, 1)

    def s(self, fn):
        self._emit("scalar", fn, 1)

    def dma(self, fn):
        self._emit("sync", fn, 16)

    def gdma(self, fn):
        self._emit("gpsimd", fn, 0, ginc=16)

    def g(self, fn):
        self._emit("gpsimd", fn, 1)

    def finish(self, block):
        total = self.cnt
        gtotal = self.gcnt
        sem, gsem = self.sem, self.gsem
        for name in self.ENG:
            ops = self.ops[name]

            def run(engine, ops=ops, name=name):
                for op in ops:
                    op(engine)
                if name == "sync":
                    if gtotal:
                        engine.wait_ge(gsem, gtotal)
                    engine.wait_ge(sem, total)

            getattr(block, name)(run)


def build_program():
    nc = bass.Bass("TRN2", target_bir_lowering=False,
                   detect_race_conditions=False)

    xb = nc.dram_tensor("xb", [S, H], F32, kind="ExternalInput").ap()
    xq = nc.dram_tensor("xq", [SH, H], F32, kind="ExternalInput").ap()
    tok = nc.dram_tensor("tok", [128, QT], I32, kind="ExternalInput").ap()
    wqd = nc.dram_tensor("wq", [H, H], F32, kind="ExternalInput").ap()
    wkd = nc.dram_tensor("wk", [H, H], F32, kind="ExternalInput").ap()
    wvd = nc.dram_tensor("wv", [H, H], F32, kind="ExternalInput").ap()
    bqv = nc.dram_tensor("bq", [H], F32, kind="ExternalInput").ap()
    bkv = nc.dram_tensor("bk", [H], F32, kind="ExternalInput").ap()
    bvv = nc.dram_tensor("bv", [H], F32, kind="ExternalInput").ap()
    gw = nc.dram_tensor("gw", [E, H], F32, kind="ExternalInput").ap()
    w1 = nc.dram_tensor("w1", [E, H, FF], F32, kind="ExternalInput").ap()
    w2 = nc.dram_tensor("w2", [E, FF, H], F32, kind="ExternalInput").ap()
    w3 = nc.dram_tensor("w3", [E, H, FF], F32, kind="ExternalInput").ap()
    out = nc.dram_tensor("out", [SH, H], F32, kind="ExternalOutput").ap()

    DK = "ExternalOutput" if DEBUG_OUTS else "Internal"
    kT_s = nc.dram_tensor("kT_s", [H, S], F16, kind=DK).ap()
    v_s = nc.dram_tensor("v_s", [S, H], F16, kind=DK).ap()
    attn_s = nc.dram_tensor("attn_s", [SH, H], F32, kind=DK).ap()
    moe_s = nc.dram_tensor("moe_s", [E * EPAD, H], F32, kind=DK).ap()
    didx_s = nc.dram_tensor("didx_s", [E * EPAD, 1], I32, kind=DK).ap()
    rec_s = nc.dram_tensor("rec_s", [S, 4], F32, kind=DK).ap()
    maskh = nc.dram_tensor("maskh", [E, 2 * S], F32, kind="Internal").ap()
    toth = nc.dram_tensor("toth", [128], F32, kind="Internal").ap()
    offh = nc.dram_tensor("offh", [128], F32, kind="Internal").ap()
    posh = nc.dram_tensor("posh", [2 * S], F32, kind="Internal").ap()

    w1v = w1.rearrange("e h f -> (e h) f")
    w3v = w3.rearrange("e h f -> (e h) f")
    w2v = w2.rearrange("e f h -> (e f) h")

    st = ExitStack()
    with st:
        sem = st.enter_context(nc.semaphore("chain"))
        gsem = st.enter_context(nc.semaphore("gchain"))
        block = st.enter_context(nc.Block())
        pb = PB(nc, sem, gsem)
        V, SC, T, G = nc.vector, nc.scalar, nc.tensor, nc.gpsimd

        def sb(name, shape, dt=F32, stack=None):
            return (stack or st).enter_context(nc.sbuf_tensor(name, shape, dt))

        def psum(name, shape, stack=None, dt=F32):
            return (stack or st).enter_context(
                nc.psum_tensor(name, shape, dt))

        PSF = st.enter_context(nc.psum_tensor("PSF", [128, 3584], F32))
        PS16 = st.enter_context(nc.psum_tensor("PS16", [128, 1024], F16))
        psA = PSF[:, 0:512]
        psT = PSF[:, 3072:3200]
        psT0 = psT
        psR = PSF[0:8, 0:1024]
        psRs = PSF[0:8, 1024:1152]
        psTT = PSF[:, 3200:3208]
        psD = PSF[:, 2048:2560]
        psMT = PSF[0:8, 3072:3200]
        psP = PSF[0:16, 0:1024]
        psB8 = PSF[:, 3200:3208]
        psE = PSF[:, 2048:2560]
        psTT2 = PS16[:, 0:128]
        psO = [PSF[:, 0:1024], PSF[:, 1024:2048]]
        psH = PSF[:, 2560:3072]
        psHb = PSF[:, 3072:3200]
        psO2 = [PSF[:, 0:512], PSF[:, 512:1024], PSF[:, 1024:1536],
                PSF[:, 1536:2048], PSF[:, 2048:2560]]

        ident = sb("ident", [128, 128])
        identH = sb("identH", [128, 128], F16)
        identI = sb("identI", [128, 128], I32)
        ones1 = sb("ones1", [1, 128])
        consts = sb("consts", [128, 8])
        C_EPS, C_135, C_N505, C_SM, C_SP, C_LNEPS, C_M24 = range(7)
        biasq = sb("biasq", [128, HT])
        biask = sb("biask", [128, HT])
        bvB = sb("bvB", [128, H])
        gwT1 = sb("gwT1", [128, HT * E], BF16)
        gwT2 = sb("gwT2", [128, HT * E], BF16)
        tokS = sb("tokS", [128, QT], I32)
        didxS = sb("didxS", [128, E * EPT], I32)
        sidxI1 = sb("sidxI1", [128, QT], I32)
        sidxI2 = sb("sidxI2", [128, QT], I32)
        sidxG1 = sb("sidxG1", [128, QT], I32)
        sidxG2 = sb("sidxG2", [128, QT], I32)
        pmS = sb("pmS", [128, 2 * QT])
        rowsc = sb("rowsc", [128, 16])

        PW = 2048
        qs_a = sb("qs_a", [128, PW])
        qs_b = sb("qs_b", [128, PW])
        qs_c = sb("qs_c", [128, PW])
        qs_d = sb("qs_d", [128, PW])
        qs_i = sb("qs_i", [128, PW], I32)

        sBC = ExitStack()
        logP = sb("logP", [128, NT * 72], stack=sBC)
        oh1 = sb("oh1", [128, NT * E], stack=sBC)
        oh2 = sb("oh2", [128, NT * E], stack=sBC)
        lgb = sb("lgb", [128, NT * E], stack=sBC)
        sA5 = sb("sA5", [128, NT * E], stack=sBC)
        sB5 = sb("sB5", [128, NT * E], stack=sBC)
        iotaE = sb("iotaE", [128, NT * E], stack=sBC)
        m1 = sb("m1", [128, NT], stack=sBC)
        m2 = sb("m2", [128, NT], stack=sBC)
        i1f = sb("i1f", [128, NT], stack=sBC)
        i2f = sb("i2f", [128, NT], stack=sBC)
        p1x = sb("p1x", [128, NT], stack=sBC)
        p2x = sb("p2x", [128, NT], stack=sBC)
        sA6 = sb("sA6", [128, NT], stack=sBC)
        sB6 = sb("sB6", [128, NT], stack=sBC)

        def e3(ap):
            return ap.rearrange("p (t e) -> p t e", e=8)

        def quantize(src, dst, width):
            a = qs_a[:, 0:width]
            b2 = qs_b[:, 0:width]
            c = qs_c[:, 0:width]
            d = qs_d[:, 0:width]
            ii = qs_i[:, 0:width]
            pb.s(lambda: SC.activation(a, src, AF.Square))
            pb.s(lambda: SC.activation(
                b2, a, AF.Ln, bias=consts[:, C_EPS:C_EPS + 1]))
            pb.s(lambda: SC.activation(
                d, b2, AF.Relu, scale=C1, bias=consts[:, C_135:C_135 + 1]))
            pb.v(lambda: V.tensor_copy(ii, d))
            pb.v(lambda: V.tensor_copy(a, ii))
            pb.v(lambda: V.scalar_tensor_tensor(
                b2, a, -1.0, d, OP.mult, OP.add))
            pb.v(lambda: V.tensor_single_scalar(b2, b2, 0.5, OP.is_ge))
            pb.v(lambda: V.scalar_tensor_tensor(
                a, b2, 1.0, a, OP.mult, OP.add))
            pb.s(lambda: SC.activation(
                b2, a, AF.Exp, scale=-LN2, bias=consts[:, C_SM:C_SM + 1]))
            pb.s(lambda: SC.activation(
                c, a, AF.Exp, scale=LN2, bias=consts[:, C_SP:C_SP + 1]))
            pb.v(lambda: V.tensor_mul(a, src, b2))
            pb.v(lambda: V.tensor_single_scalar(d, a, 4095.5, OP.add))
            pb.v(lambda: V.tensor_copy(ii, d))
            pb.v(lambda: V.tensor_copy(d, ii))
            pb.v(lambda: V.scalar_tensor_tensor(
                b2, d, -1.0, a, OP.mult, OP.add))
            pb.v(lambda: V.tensor_single_scalar(
                b2, b2, -4095.5, OP.is_ge))
            pb.v(lambda: V.scalar_tensor_tensor(
                d, b2, 1.0, d, OP.mult, OP.add))
            pb.v(lambda: V.tensor_single_scalar(d, d, -4096.0, OP.add))
            pb.v(lambda: V.tensor_mul(dst, d, c))

        def layernorm(xt, xnt):
            rs = rowsc[:, 0:1]
            ss = rowsc[:, 1:2]
            pb.v(lambda: V.reduce_sum(rs, xt, AX))
            pb.v(lambda: V.tensor_single_scalar(rs, rs, 1.0 / H, OP.mult))
            pb.v(lambda: V.tensor_scalar(xnt, xt, rs, None, OP.subtract))
            pb.s(lambda: SC.activation(
                qs_a[:, 0:H], xnt, AF.Square, accum_out=ss))
            pb.s(lambda: SC.activation(
                ss, ss, AF.Sqrt, scale=1.0 / H,
                bias=consts[:, C_LNEPS:C_LNEPS + 1]))
            pb.v(lambda: V.reciprocal(ss, ss))
            pb.v(lambda: V.tensor_scalar(xnt, xnt, ss, None, OP.mult))

        def ln_transpose_split(i_dram, src_dram, xtile, xntile, f32c, psT,
                               dhi, dlo, ii):
            pb.dma(lambda: nc.sync.dma_start(
                out=xtile[:],
                in_=src_dram[128 * i_dram:128 * (i_dram + 1), :]))
            layernorm(xtile[:], xntile[:])
            for k in range(HT):
                hi = dhi[:, 512 * k + 128 * ii:512 * k + 128 * (ii + 1)]
                lo = dlo[:, 512 * k + 128 * ii:512 * k + 128 * (ii + 1)]
                pb.t(lambda k=k: T.transpose(
                    psT[:], xntile[:, 128 * k:128 * (k + 1)], ident[:]))
                pb.v(lambda: V.tensor_copy(f32c[:], psT[:]))
                pb.v(lambda hi=hi: V.tensor_copy(hi, f32c[:]))
                pb.v(lambda hi=hi, lo=lo: V.scalar_tensor_tensor(
                    lo, hi, -1.0, f32c[:], OP.mult, OP.add))

        def prep_wT(wsrc, hi, lo, wst, psT):
            pb.dma(lambda: nc.sync.dma_start(
                out=wst.rearrange("p (ot h) -> p ot h", h=H),
                in_=wsrc.rearrange("(ot p) h -> p ot h", p=128)))
            for ot in range(HT):
                for ht in range(HT):
                    pb.t(lambda ot=ot, ht=ht: T.transpose(
                        psT[:], wst[:, 1024 * ot + 128 * ht:
                                    1024 * ot + 128 * (ht + 1)], ident[:]))
                    pb.v(lambda: V.tensor_copy(qs_a[:, 0:128], psT[:]))
                    dh = hi[:, 1024 * ht + 128 * ot:1024 * ht + 128 * (ot + 1)]
                    dl = lo[:, 1024 * ht + 128 * ot:1024 * ht + 128 * (ot + 1)]
                    pb.v(lambda dh=dh: V.tensor_copy(dh, qs_a[:, 0:128]))
                    pb.v(lambda dh=dh, dl=dl: V.scalar_tensor_tensor(
                        dl, dh, -1.0, qs_a[:, 0:128], OP.mult, OP.add))

        pb.g(lambda: G.iota(identI[:], [[-1, 128]], base=0,
                            channel_multiplier=1))
        pb.v(lambda: V.tensor_copy(ident[:], identI[:]))
        pb.v(lambda: V.tensor_single_scalar(
            ident[:], ident[:], 0.0, OP.is_equal))
        pb.v(lambda: V.tensor_copy(identH[:], ident[:]))
        pb.v(lambda: V.memset(ones1[:], 1.0))
        for idx, val in [(C_EPS, 2.0 ** -40), (C_135, 13.5), (C_N505, -50.5),
                         (C_SM, 24.0 * LN2), (C_SP, -24.0 * LN2),
                         (C_LNEPS, 1e-5), (C_M24, -24.0 * LN2)]:
            pb.v(lambda idx=idx, val=val: V.memset(
                consts[:, idx:idx + 1], float(val)))
        pb.dma(lambda: nc.sync.dma_start(
            out=biasq[:], in_=bqv.rearrange("(j p) -> p j", p=128),
            allow_slow_non_contiguous=True))
        pb.dma(lambda: nc.sync.dma_start(
            out=biask[:], in_=bkv.rearrange("(j p) -> p j", p=128),
            allow_slow_non_contiguous=True))
        pb.dma(lambda: nc.sync.dma_start(out=tokS[:], in_=tok[:]))
        pb.g(lambda: G.iota(qs_i[:, 0:NT * E], [[0, NT], [1, E]],
                            base=0, channel_multiplier=0))
        pb.v(lambda: V.tensor_copy(iotaE[:], qs_i[:, 0:NT * E]))

        with ExitStack() as sA:
            bvrow = sb("bvrow", [1, H], stack=sA)
            gws = sb("gws", [8, H], stack=sA)
            gtf = sb("gtf", [128, 8], stack=sA)
            pb.dma(lambda: nc.sync.dma_start(
                out=bvrow[:], in_=bvv.rearrange("(o h) -> o h", o=1)))
            for half in range(2):
                pb.t(lambda half=half: T.matmul(
                    psA[:], ones1[:], bvrow[:, 512 * half:512 * (half + 1)],
                    start=True, stop=True))
                pb.v(lambda half=half: V.tensor_copy(
                    bvB[:, 512 * half:512 * (half + 1)], psA[:]))
            pb.dma(lambda: nc.sync.dma_start(out=gws[:], in_=gw[:]))
            for k in range(HT):
                pb.t(lambda k=k: T.transpose(
                    psA[:, 0:8], gws[:, 128 * k:128 * (k + 1)],
                    ident[0:8, 0:8]))
                pb.v(lambda: V.tensor_copy(gtf[:], psA[:, 0:8]))
                pb.v(lambda k=k: V.tensor_copy(
                    gwT1[:, 8 * k:8 * (k + 1)], gtf[:]))
                pb.v(lambda k=k: V.scalar_tensor_tensor(
                    gwT2[:, 8 * k:8 * (k + 1)], gwT1[:, 8 * k:8 * (k + 1)],
                    -1.0, gtf[:], OP.mult, OP.add))

        def _phB():
            with ExitStack() as sBo:
                wkT_hi = sb("wkT_hi", [128, H * HT], BF16, stack=sBo)
                wkT_lo = sb("wkT_lo", [128, H * HT], BF16, stack=sBo)
                wvT_hi = sb("wvT_hi", [128, H * HT], BF16, stack=sBo)
                wvT_lo = sb("wvT_lo", [128, H * HT], BF16, stack=sBo)
                with ExitStack() as sBw:
                    wst = sb("wst", [128, H * HT], stack=sBw)
                    prep_wT(wkd, wkT_hi, wkT_lo, wst, psT0)
                    prep_wT(wvd, wvT_hi, wvT_lo, wst, psT0)
                xcT_hi = sb("xcT_hi", [128, HT * 512], BF16, stack=sBo)
                xcT_lo = sb("xcT_lo", [128, HT * 512], BF16, stack=sBo)
                xtile = sb("xtile", [128, H], stack=sBo)
                xntile = sb("xntile", [128, H], stack=sBo)
                f32c = sb("f32c", [128, 128], stack=sBo)
                evb = sb("evb", [128, 2048], stack=sBo)
                hf16 = sb("hf16", [128, 2048], F16, stack=sBo)
                rsc = sb("rsc", [8, 1152], stack=sBo)

                for g in range(NG):
                    for ii in range(4):
                        i = 4 * g + ii
                        ln_transpose_split(i, xb, xtile, xntile, f32c, psT,
                                           xcT_hi, xcT_lo, ii)
                        for k in range(HT):
                            pb.t(lambda k=k, ii=ii: T.matmul(
                                psR[:, 128 * k:128 * (k + 1)],
                                gwT1[:, 8 * k:8 * (k + 1)],
                                xcT_hi[:, 512 * k + 128 * ii:
                                       512 * k + 128 * (ii + 1)],
                                start=True, stop=True))
                        nsm = 0
                        for k in range(HT):
                            for (ga, xa) in ((gwT2, xcT_hi), (gwT1, xcT_lo),
                                             (gwT2, xcT_lo)):
                                pb.t(lambda k=k, ii=ii, ga=ga, xa=xa, nsm=nsm:
                                     T.matmul(
                                         psRs[:], ga[:, 8 * k:8 * (k + 1)],
                                         xa[:, 512 * k + 128 * ii:
                                            512 * k + 128 * (ii + 1)],
                                         start=(nsm == 0), stop=(nsm == 23)))
                                nsm += 1
                        pb.v(lambda: V.tensor_copy(rsc[:, 0:1024], psR[:]))
                        pb.v(lambda: V.tensor_copy(rsc[:, 1024:1152], psRs[:]))
                        for pl in range(9):
                            pb.t(lambda pl=pl: T.transpose(
                                psTT[:], rsc[:, 128 * pl:128 * (pl + 1)],
                                ident[0:8, 0:8]))
                            pb.v(lambda i=i, pl=pl: V.tensor_copy(
                                logP[:, 72 * i + 8 * pl:72 * i + 8 * (pl + 1)],
                                psTT[:]))
                    for oth in range(2):
                        for otl in range(4):
                            ot = 4 * oth + otl
                            nmm = 0
                            for ht in range(HT):
                                for (wa, xa) in ((wkT_hi, xcT_hi),
                                                 (wkT_hi, xcT_lo),
                                                 (wkT_lo, xcT_hi)):
                                    pb.t(lambda ot=ot, ht=ht, wa=wa, xa=xa,
                                         nmm=nmm: T.matmul(
                                             psD[:],
                                             wa[:, 1024 * ht + 128 * ot:
                                                1024 * ht + 128 * (ot + 1)],
                                             xa[:, 512 * ht:512 * (ht + 1)],
                                             start=(nmm == 0), stop=(nmm == 23)))
                                    nmm += 1
                            pb.v(lambda ot=ot, otl=otl: V.tensor_scalar(
                                evb[:, 512 * otl:512 * (otl + 1)], psD[:],
                                biask[:, ot:ot + 1], None, OP.add))
                        quantize(evb[:, 0:2048], hf16[:, 0:2048], 2048)
                        pb.dma(lambda g=g, oth=oth: nc.sync.dma_start(
                            out=kT_s.rearrange("(ot p) s -> p ot s", p=128)[
                                :, 4 * oth:4 * (oth + 1), 512 * g:512 * (g + 1)],
                            in_=hf16.rearrange("p (ot s) -> p ot s", s=512)))
                    for iih in range(2):
                        for iil in range(2):
                            ii = 2 * iih + iil
                            for oc in range(2):
                                nmm = 0
                                for ht in range(HT):
                                    for (xa, wa) in ((xcT_hi, wvT_hi),
                                                     (xcT_hi, wvT_lo),
                                                     (xcT_lo, wvT_hi)):
                                        pb.t(lambda ii=ii, oc=oc, ht=ht, xa=xa,
                                             wa=wa, nmm=nmm: T.matmul(
                                                 psD[:],
                                                 xa[:, 512 * ht + 128 * ii:
                                                    512 * ht + 128 * (ii + 1)],
                                                 wa[:, 1024 * ht + 512 * oc:
                                                    1024 * ht + 512 * (oc + 1)],
                                                 start=(nmm == 0),
                                                 stop=(nmm == 23)))
                                        nmm += 1
                                pb.v(lambda iil=iil, oc=oc: V.tensor_add(
                                    evb[:, 1024 * iil + 512 * oc:
                                        1024 * iil + 512 * (oc + 1)],
                                    psD[:], bvB[:, 512 * oc:512 * (oc + 1)]))
                        quantize(evb[:, 0:2048], hf16[:, 0:2048], 2048)
                        pb.dma(lambda g=g, iih=iih: nc.sync.dma_start(
                            out=v_s[512 * g + 256 * iih:
                                    512 * g + 256 * (iih + 1), :].rearrange(
                                "(ii p) h -> p ii h", p=128),
                            in_=hf16.rearrange("p (ii h) -> p ii h", h=1024)))

        if "B" in PHASES:
            _phB()
        def _phC():
            with ExitStack() as sC:
                eftS = sb("eftS", [128, NT * E], stack=sC)
                eftC = sb("eftC", [128, NT * E], stack=sC)
                eftX = sb("eftX", [128, NT * E], stack=sC)
                eftT = sb("eftT", [128, NT * E], stack=sC)
                eftZ = sb("eftZ", [128, NT * E], stack=sC)
                mbuf = sb("mbuf", [8, 2048], stack=sC)
                m8 = sb("m8", [8, 16], stack=sC)
                scanL = sb("scanL", [128, 1024], stack=sC)
                cg = sb("cg", [128, 1024], stack=sC)
                cmb = sb("cmb", [128, 1024], stack=sC)
                tot8 = sb("tot8", [8, 16], stack=sC)
                offs8 = sb("offs8", [8, 16], stack=sC)
                offCol = sb("offCol", [128, 1], stack=sC)
                selM = sb("selM", [128, 16], stack=sC)
                pos16 = sb("pos16", [16, 1024], stack=sC)
                posTok = sb("posTok", [128, 128], stack=sC)
                baseB = sb("baseB", [128, 32], stack=sC)
                bRow = sb("bRow", [1, 8], stack=sC)
                recS = sb("recS", [128, NT * 4], stack=sC)
                recAll = sb("recAll", [128, QT * 4], stack=sC)
                initI = sb("initI", [128, E * EPT], I32, stack=sC)
                iotaPf = sb("iotaPf", [128, 1], stack=sC)
                iota16 = sb("iota16", [128, 16], stack=sC)

                logv = logP.rearrange("p (t pl e) -> p t pl e", pl=9, e=8)

                pb.v(lambda: V.tensor_copy(e3(eftS[:, :]), logv[:, :, 0, :]))
                pb.v(lambda: V.memset(eftC[:], 0.0))
                for pl in range(1, 9):
                    pb.v(lambda pl=pl: V.tensor_copy(
                        e3(eftX[:, :]), logv[:, :, pl, :]))
                    pb.v(lambda: V.tensor_add(eftT[:], eftS[:], eftX[:]))
                    pb.v(lambda: V.tensor_sub(eftZ[:], eftT[:], eftS[:]))
                    pb.v(lambda: V.tensor_sub(sA5[:], eftT[:], eftZ[:]))
                    pb.v(lambda: V.tensor_sub(sA5[:], eftS[:], sA5[:]))
                    pb.v(lambda: V.tensor_add(eftC[:], eftC[:], sA5[:]))
                    pb.v(lambda: V.tensor_sub(eftZ[:], eftX[:], eftZ[:]))
                    pb.v(lambda: V.tensor_add(eftC[:], eftC[:], eftZ[:]))
                    pb.v(lambda: V.tensor_copy(eftS[:], eftT[:]))
                pb.v(lambda: V.tensor_add(lgb[:], eftS[:], eftC[:]))

                bc = [128, NT, 8]
                pb.v(lambda: V.reduce_max(m1[:].unsqueeze(2), e3(lgb[:, :]), AX))
                pb.v(lambda: V.tensor_tensor(
                    e3(sA5[:, :]), e3(lgb[:, :]),
                    m1[:].unsqueeze(2).broadcast_to(bc), OP.is_ge))
                pb.v(lambda: V.tensor_single_scalar(
                    sB5[:], iotaE[:], -999.0, OP.add))
                pb.v(lambda: V.tensor_mul(sA5[:], sA5[:], sB5[:]))
                pb.v(lambda: V.tensor_single_scalar(sA5[:], sA5[:], 999.0, OP.add))
                pb.v(lambda: V.tensor_reduce(
                    i1f[:].unsqueeze(2), e3(sA5[:, :]), AX, OP.min))
                pb.v(lambda: V.tensor_tensor(
                    e3(oh1[:, :]), e3(iotaE[:, :]),
                    i1f[:].unsqueeze(2).broadcast_to(bc), OP.is_equal))
                pb.v(lambda: V.scalar_tensor_tensor(
                    sB5[:], oh1[:], -1e30, lgb[:], OP.mult, OP.add))
                pb.v(lambda: V.reduce_max(m2[:].unsqueeze(2), e3(sB5[:, :]), AX))
                pb.v(lambda: V.tensor_tensor(
                    e3(sA5[:, :]), e3(sB5[:, :]),
                    m2[:].unsqueeze(2).broadcast_to(bc), OP.is_ge))
                pb.v(lambda: V.tensor_single_scalar(
                    sB5[:], iotaE[:], -999.0, OP.add))
                pb.v(lambda: V.tensor_mul(sA5[:], sA5[:], sB5[:]))
                pb.v(lambda: V.tensor_single_scalar(sA5[:], sA5[:], 999.0, OP.add))
                pb.v(lambda: V.tensor_reduce(
                    i2f[:].unsqueeze(2), e3(sA5[:, :]), AX, OP.min))
                pb.v(lambda: V.tensor_tensor(
                    e3(oh2[:, :]), e3(iotaE[:, :]),
                    i2f[:].unsqueeze(2).broadcast_to(bc), OP.is_equal))
                pb.v(lambda: V.tensor_tensor(
                    e3(sA5[:, :]), e3(lgb[:, :]),
                    m1[:].unsqueeze(2).broadcast_to(bc), OP.subtract))
                pb.s(lambda: SC.activation(sA5[:], sA5[:], AF.Exp))
                pb.v(lambda: V.reduce_sum(sA6[:].unsqueeze(2), e3(sA5[:, :]), AX))
                pb.v(lambda: V.reciprocal(sA6[:], sA6[:]))
                pb.v(lambda: V.tensor_tensor(
                    e3(sA5[:, :]), e3(sA5[:, :]),
                    sA6[:].unsqueeze(2).broadcast_to(bc), OP.mult))
                pb.v(lambda: V.tensor_mul(sB5[:], sA5[:], oh1[:]))
                pb.v(lambda: V.reduce_sum(p1x[:].unsqueeze(2), e3(sB5[:, :]), AX))
                pb.v(lambda: V.tensor_mul(sB5[:], sA5[:], oh2[:]))
                pb.v(lambda: V.reduce_sum(p2x[:].unsqueeze(2), e3(sB5[:, :]), AX))

                mb3 = mbuf.rearrange("e (w k) -> e w k", k=2)
                for grp in range(8):
                    for iw in range(8):
                        i = 8 * grp + iw
                        for k, oh in ((0, oh1), (1, oh2)):
                            pb.t(lambda i=i, oh=oh: T.transpose(
                                psMT[:], oh[:, 8 * i:8 * (i + 1)], ident[:]))
                            pb.v(lambda iw=iw, k=k: V.tensor_copy(
                                mb3[:, 128 * iw:128 * (iw + 1), k:k + 1],
                                psMT[:].unsqueeze(2)))
                    pb.dma(lambda grp=grp: nc.sync.dma_start(
                        out=maskh[:, 2048 * grp:2048 * (grp + 1)], in_=mbuf[:]))
                pb.dma(lambda: nc.sync.dma_start(
                    out=cmb[:], in_=maskh.rearrange("e (c w) -> (e c) w", c=16)))
                pb.v(lambda: V.tensor_tensor_scan(
                    scanL[:], cmb[:], cmb[:], 0.0, OP.add, OP.bypass))
                pb.dma(lambda: nc.sync.dma_start(
                    out=toth.rearrange("(p o) -> p o", o=1),
                    in_=scanL[:, 1023:1024]))
                pb.dma(lambda: nc.sync.dma_start(
                    out=tot8[:], in_=toth.rearrange("(e c) -> e c", c=16)))
                pb.v(lambda: V.tensor_copy(m8[:], tot8[:]))
                pb.v(lambda: V.tensor_tensor_scan(
                    tot8[:], m8[:], m8[:], 0.0, OP.add, OP.bypass))
                pb.v(lambda: V.memset(offs8[:, 0:1], 0.0))
                pb.v(lambda: V.tensor_copy(offs8[:, 1:16], tot8[:, 0:15]))
                pb.dma(lambda: nc.sync.dma_start(
                    out=offh.rearrange("(e c) -> e c", c=16), in_=offs8[:]))
                pb.dma(lambda: nc.sync.dma_start(
                    out=offCol[:], in_=offh.rearrange("(p o) -> p o", o=1)))
                pb.v(lambda: V.tensor_scalar(
                    cg[:], scanL[:], offCol[:, 0:1], None, OP.add))
                pb.v(lambda: V.tensor_mul(cg[:], cg[:], cmb[:]))
                pb.g(lambda: G.iota(qs_i[:, 0:1], [[0, 1]], base=0,
                                    channel_multiplier=1))
                pb.v(lambda: V.tensor_copy(iotaPf[:], qs_i[:, 0:1]))
                pb.g(lambda: G.iota(qs_i[:, 0:16], [[1, 16]], base=0,
                                    channel_multiplier=0))
                pb.v(lambda: V.tensor_copy(iota16[:], qs_i[:, 0:16]))
                t0, t1 = rowsc[:, 8:9], rowsc[:, 9:10]
                pb.v(lambda: V.tensor_scalar(
                    t0, iotaPf[:], 1.0 / 16.0, -0.5, OP.mult, OP.add))
                pb.v(lambda: V.tensor_copy(qs_i[:, 0:1], t0))
                pb.v(lambda: V.tensor_copy(t1, qs_i[:, 0:1]))
                pb.v(lambda: V.tensor_sub(t0, t0, t1))
                pb.v(lambda: V.tensor_single_scalar(t0, t0, 0.5, OP.is_ge))
                pb.v(lambda: V.scalar_tensor_tensor(
                    t1, t0, 1.0, t1, OP.mult, OP.add))
                pb.v(lambda: V.scalar_tensor_tensor(
                    t0, t1, -16.0, iotaPf[:], OP.mult, OP.add))
                pb.v(lambda: V.tensor_scalar(
                    selM[:], iota16[:], t0, None, OP.is_equal))
                for j in range(2):
                    pb.t(lambda j=j: T.matmul(
                        psP[:, 512 * j:512 * (j + 1)], selM[:],
                        cg[:, 512 * j:512 * (j + 1)], start=True, stop=True))
                pb.v(lambda: V.tensor_copy(pos16[:], psP[:]))
                pb.dma(lambda: nc.sync.dma_start(
                    out=posh.rearrange("(c w) -> c w", w=1024), in_=pos16[:]))
                pb.dma(lambda: nc.sync.dma_start(
                    out=posTok.rearrange("p (t k) -> p t k", k=2),
                    in_=posh.rearrange("(t p k) -> p t k", p=128, k=2),
                    allow_slow_non_contiguous=True))
                for q in range(4):
                    pb.t(lambda q=q: T.transpose(
                        psMT[0:1, 0:8], offs8[:, 4 * q:4 * q + 1],
                        ident[0:8, 0:8]))
                    pb.v(lambda: V.tensor_copy(bRow[:], psMT[0:1, 0:8]))
                    pb.t(lambda: T.matmul(
                        psB8[:], ones1[:], bRow[:], start=True, stop=True))
                    pb.v(lambda q=q: V.tensor_copy(
                        baseB[:, 8 * q:8 * (q + 1)], psB8[:]))
                posv = posTok.rearrange("p (t k) -> p t k", k=2)
                recv = recS.rearrange("p (t r) -> p t r", r=4)
                for k, ohk, ikf, pkx in ((0, oh1, i1f, p1x), (1, oh2, i2f, p2x)):
                    for q in range(4):
                        pb.v(lambda q=q, ohk=ohk: V.tensor_tensor(
                            e3(sA5[:, :])[:, 16 * q:16 * (q + 1), :],
                            e3(ohk[:, :])[:, 16 * q:16 * (q + 1), :],
                            baseB[:, 8 * q:8 * (q + 1)].unsqueeze(1)
                            .broadcast_to([128, 16, 8]), OP.mult))
                    pb.v(lambda: V.reduce_sum(
                        sA6[:].unsqueeze(2), e3(sA5[:, :]), AX))
                    pb.v(lambda k=k: V.tensor_tensor(
                        sB6[:].unsqueeze(2), posv[:, :, k:k + 1],
                        sA6[:].unsqueeze(2), OP.subtract))
                    pb.v(lambda ikf=ikf: V.tensor_scalar(
                        sA6[:], ikf[:], 640.0, -1.0, OP.mult, OP.add))
                    pb.v(lambda: V.tensor_add(sA6[:], sA6[:], sB6[:]))
                    pb.v(lambda k=k: V.tensor_copy(
                        m2[:].unsqueeze(2), posv[:, :, k:k + 1]))
                    pb.v(lambda: V.tensor_single_scalar(
                        m2[:], m2[:], CAP + 0.5, OP.is_le))
                    pb.v(lambda: V.tensor_single_scalar(
                        sB6[:], sB6[:], EPAD + 0.5, OP.is_le))
                    pb.v(lambda: V.tensor_mul(m2[:], m2[:], sB6[:]))
                    pb.v(lambda: V.tensor_single_scalar(
                        sA6[:], sA6[:], BIGF, OP.add))
                    pb.v(lambda: V.scalar_tensor_tensor(
                        sA6[:], m2[:], -BIGF, sA6[:], OP.mult, OP.add))
                    pb.v(lambda pkx=pkx: V.tensor_mul(pkx[:], pkx[:], m2[:]))
                    pb.v(lambda k=k: V.tensor_copy(
                        recv[:, :, k:k + 1], sA6[:].unsqueeze(2)))
                    pb.v(lambda k=k, pkx=pkx: V.tensor_copy(
                        recv[:, :, 2 + k:3 + k], pkx[:].unsqueeze(2)))
                pb.dma(lambda: nc.sync.dma_start(
                    out=rec_s.rearrange("(t p) r -> p t r", p=128),
                    in_=recS.rearrange("p (t r) -> p t r", r=4)))
                pb.v(lambda: V.memset(initI[:], 0))
                pb.dma(lambda: nc.sync.dma_start(
                    out=didx_s.rearrange("(g p) o -> p g o", p=128),
                    in_=initI[:].unsqueeze(2),
                    allow_slow_non_contiguous=True))
                for t in range(QT):
                    pb.gdma(lambda t=t: G.indirect_dma_start(
                        recAll[:, 4 * t:4 * (t + 1)], None, rec_s[:, :],
                        bass.IndirectOffsetOnAxis(tokS[:, t:t + 1], 0)))
                recav = recAll.rearrange("p (t r) -> p t r", r=4)
                pb.v(lambda: V.tensor_copy(
                    sidxI1[:].unsqueeze(2), recav[:, :, 0:1]))
                pb.v(lambda: V.tensor_copy(
                    sidxI2[:].unsqueeze(2), recav[:, :, 1:2]))
                pb.v(lambda: V.tensor_copy(
                    pmS[:, 0:QT].unsqueeze(2), recav[:, :, 2:3]))
                pb.v(lambda: V.tensor_copy(
                    pmS[:, QT:2 * QT].unsqueeze(2), recav[:, :, 3:4]))
                for (pmsl, rsl, dst) in ((0, 0, sidxG1), (QT, 1, sidxG2)):
                    pb.v(lambda pmsl=pmsl: V.tensor_single_scalar(
                        qs_a[:, 0:QT], pmS[:, pmsl:pmsl + QT], 0.0, OP.is_gt))
                    pb.v(lambda rsl=rsl: V.tensor_tensor(
                        qs_b[:, 0:QT].unsqueeze(2), recav[:, :, rsl:rsl + 1],
                        qs_a[:, 0:QT].unsqueeze(2), OP.mult))
                    pb.v(lambda: V.tensor_scalar(
                        qs_b[:, 0:QT], qs_b[:, 0:QT], 0.0,
                        float(E * EPAD - 1), OP.max, OP.min))
                    pb.v(lambda dst=dst: V.tensor_copy(dst[:], qs_b[:, 0:QT]))
                for t in range(QT):
                    for sidx in (sidxI1, sidxI2):
                        pb.gdma(lambda t=t, sidx=sidx: G.indirect_dma_start(
                            didx_s[:, :],
                            bass.IndirectOffsetOnAxis(sidx[:, t:t + 1], 0),
                            tokS[:, t:t + 1], None,
                            bounds_check=E * EPAD - 1, oob_is_err=False))
                pb.dma(lambda: nc.sync.dma_start(
                    out=didxS[:],
                    in_=didx_s.rearrange("(g p) o -> p (g o)", p=128),
                    allow_slow_non_contiguous=True))

        if "C" in PHASES:
            _phC()
        sBC.close()

        qTt = sb("qT", [128, HT * SH], F16)
        def _phD():
            with ExitStack() as sD:
                wqT_hi = sb("wqT_hi", [128, H * HT], BF16, stack=sD)
                wqT_lo = sb("wqT_lo", [128, H * HT], BF16, stack=sD)
                with ExitStack() as sDw:
                    wst = sb("wstD", [128, H * HT], stack=sDw)
                    prep_wT(wqd, wqT_hi, wqT_lo, wst, psT0)
                xcT_hi = sb("xcT_hiD", [128, HT * 512], BF16, stack=sD)
                xcT_lo = sb("xcT_loD", [128, HT * 512], BF16, stack=sD)
                xtile = sb("xtileD", [128, H], stack=sD)
                xntile = sb("xntileD", [128, H], stack=sD)
                f32c = sb("f32cD", [128, 128], stack=sD)
                evb = sb("evbD", [128, 2048], stack=sD)

                for g in range(4):
                    for ii in range(4):
                        ln_transpose_split(4 * g + ii, xq, xtile, xntile, f32c,
                                           psT, xcT_hi, xcT_lo, ii)
                    for oth in range(2):
                        for otl in range(4):
                            ot = 4 * oth + otl
                            nmm = 0
                            for ht in range(HT):
                                for (wa, xa) in ((wqT_hi, xcT_hi),
                                                 (wqT_hi, xcT_lo),
                                                 (wqT_lo, xcT_hi)):
                                    pb.t(lambda ot=ot, ht=ht, wa=wa, xa=xa,
                                         nmm=nmm: T.matmul(
                                             psD[:],
                                             wa[:, 1024 * ht + 128 * ot:
                                                1024 * ht + 128 * (ot + 1)],
                                             xa[:, 512 * ht:512 * (ht + 1)],
                                             start=(nmm == 0), stop=(nmm == 23)))
                                    nmm += 1
                            pb.v(lambda ot=ot, otl=otl: V.tensor_scalar(
                                evb[:, 512 * otl:512 * (otl + 1)], psD[:],
                                biasq[:, ot:ot + 1], None, OP.add))
                        for otl in range(4):
                            ot = 4 * oth + otl
                            quantize(evb[:, 512 * otl:512 * (otl + 1)],
                                     qTt[:, SH * ot + 512 * g:
                                         SH * ot + 512 * (g + 1)], 512)

        if "D" in PHASES:
            _phD()
        def _phE():
            with ExitStack() as sE:
                GE = 2
                sq = [sb(f"sq{i}", [128, S], F16, stack=sE) for i in range(GE)]
                sfb = [sb(f"sfb{i}", [128, 2048], stack=sE) for i in range(GE)]
                kc = sb("kc", [128, HT * 512], F16, stack=sE)
                vc = sb("vc", [128, 4 * H], F16, stack=sE)
                zfb = sb("zfb", [128, 2048], stack=sE)
                pfb = [sb(f"pfb{i}", [128, 2048], F16, stack=sE)
                       for i in range(GE)]
                pTt = sb("pTt", [128, 128], F16, stack=sE)
                sums = sb("sums", [128, 2 * GE], stack=sE)
                oeb = sb("oeb", [128, H], stack=sE)
                kcv = kc.rearrange("p (ot s) -> p ot s", s=512)
                vcv = vc.rearrange("p (tt h) -> p tt h", h=H)

                for grp in range(QT // GE):
                    for band in range(4):
                        for tc4 in range(4):
                            tc = 4 * band + tc4
                            pb.dma(lambda tc=tc: nc.sync.dma_start(
                                out=kcv,
                                in_=kT_s.rearrange("(ot p) s -> p ot s", p=128)[
                                    :, :, 512 * tc:512 * (tc + 1)]))
                            for qi in range(GE):
                                q = GE * grp + qi
                                for ot in range(HT):
                                    pb.t(lambda q=q, ot=ot: T.matmul(
                                        psE[:],
                                        qTt[:, SH * ot + 128 * q:
                                            SH * ot + 128 * (q + 1)],
                                        kc[:, 512 * ot:512 * (ot + 1)],
                                        start=(ot == 0), stop=(ot == 7)))
                                pb.s(lambda qi=qi, tc4=tc4: SC.activation(
                                    sfb[qi][:, 512 * tc4:512 * (tc4 + 1)],
                                    psE[:], AF.Copy, scale=1.0 / 32.0))
                        for qi in range(GE):
                            quantize(sfb[qi][:, 0:2048],
                                     sq[qi][:, 2048 * band:2048 * (band + 1)],
                                     2048)
                    for qi in range(GE):
                        mq = rowsc[:, 2 + qi:3 + qi]
                        negm = rowsc[:, 4 + qi:5 + qi]
                        rowA = rowsc[:, 8 + qi:9 + qi]
                        rowB = rowsc[:, 12 + qi:13 + qi]
                        pb.v(lambda qi=qi, mq=mq: V.reduce_max(
                            mq, sq[qi][:, :], AX))
                        pb.v(lambda mq=mq, negm=negm: V.tensor_single_scalar(
                            negm, mq, -1.0, OP.mult))
                        for band in range(4):
                            pb.s(lambda qi=qi, band=band, negm=negm:
                                 SC.activation(
                                     qs_a[:, 0:2048],
                                     sq[qi][:, 2048 * band:2048 * (band + 1)],
                                     AF.Exp, bias=negm,
                                     accum_out=qs_b[:, band:band + 1]))
                        pb.v(lambda qi=qi: V.reduce_sum(
                            sums[:, qi:qi + 1], qs_b[:, 0:4], AX))
                        pb.v(lambda qi=qi: V.reciprocal(
                            sums[:, qi:qi + 1], sums[:, qi:qi + 1]))
                        pb.s(lambda qi=qi: SC.activation(
                            sums[:, GE + qi:GE + qi + 1], sums[:, qi:qi + 1],
                            AF.Ln))
                        pb.v(lambda qi=qi, rowA=rowA: V.tensor_scalar(
                            rowA, sums[:, GE + qi:GE + qi + 1], INV_LN2, 64.0,
                            OP.mult, OP.add))
                        pb.v(lambda mq=mq, rowA=rowA: V.scalar_tensor_tensor(
                            rowA, mq, -INV_LN2, rowA, OP.mult, OP.add))
                        pb.v(lambda qi=qi, rowB=rowB: V.tensor_single_scalar(
                            rowB, sums[:, GE + qi:GE + qi + 1], 24.0 * LN2,
                            OP.add))
                    for band in range(4):
                        for qi in range(GE):
                            negm = rowsc[:, 4 + qi:5 + qi]
                            rowA = rowsc[:, 8 + qi:9 + qi]
                            rowB = rowsc[:, 12 + qi:13 + qi]
                            sqs = sq[qi][:, 2048 * band:2048 * (band + 1)]
                            a, b2 = qs_a[:, :], qs_b[:, :]
                            c, d, ii2 = qs_c[:, :], qs_d[:, :], qs_i[:, :]
                            pb.s(lambda sqs=sqs, negm=negm: SC.activation(
                                zfb[:], sqs, AF.Exp, bias=negm))
                            pb.v(lambda sqs=sqs, rowA=rowA, a=a: V.tensor_scalar(
                                a, sqs, INV_LN2, rowA, OP.mult, OP.add))
                            pb.s(lambda a=a, d=d: SC.activation(
                                d, a, AF.Relu, bias=consts[:, C_N505:C_N505 + 1]))
                            pb.v(lambda d=d, ii2=ii2: V.tensor_copy(ii2, d))
                            pb.v(lambda b2=b2, ii2=ii2: V.tensor_copy(b2, ii2))
                            pb.v(lambda a=a, b2=b2, d=d: V.scalar_tensor_tensor(
                                a, b2, -1.0, d, OP.mult, OP.add))
                            pb.v(lambda a=a: V.tensor_single_scalar(
                                a, a, 0.5, OP.is_ge))
                            pb.v(lambda a=a, b2=b2: V.scalar_tensor_tensor(
                                b2, a, 1.0, b2, OP.mult, OP.add))
                            pb.s(lambda a=a, b2=b2, rowB=rowB: SC.activation(
                                a, b2, AF.Exp, scale=-LN2, bias=rowB))
                            pb.s(lambda c=c, b2=b2: SC.activation(
                                c, b2, AF.Exp, scale=LN2,
                                bias=consts[:, C_M24:C_M24 + 1]))
                            pb.v(lambda a=a, b2=b2: V.tensor_mul(b2, zfb[:], a))
                            pb.v(lambda b2=b2, d=d: V.tensor_single_scalar(
                                d, b2, -0.5, OP.add))
                            pb.v(lambda d=d, ii2=ii2: V.tensor_copy(ii2, d))
                            pb.v(lambda d=d, ii2=ii2: V.tensor_copy(d, ii2))
                            pb.v(lambda a=a, b2=b2, d=d: V.scalar_tensor_tensor(
                                a, d, -1.0, b2, OP.mult, OP.add))
                            pb.v(lambda a=a: V.tensor_single_scalar(
                                a, a, 0.5, OP.is_ge))
                            pb.v(lambda a=a, d=d: V.scalar_tensor_tensor(
                                d, a, 1.0, d, OP.mult, OP.add))
                            pb.v(lambda d=d, c=c, qi=qi: V.tensor_mul(
                                pfb[qi][:], d, c))
                        for sub in range(4):
                            trange = 512 * (4 * band + sub)
                            pb.dma(lambda trange=trange: nc.sync.dma_start(
                                out=vcv,
                                in_=v_s[trange:trange + 512, :].rearrange(
                                    "(tt p) h -> p tt h", p=128)))
                            for qi in range(GE):
                                for tt_ in range(4):
                                    tglob = 4 * (4 * band + sub) + tt_
                                    pb.t(lambda qi=qi, sub=sub, tt_=tt_:
                                         T.transpose(
                                             psTT2[:],
                                             pfb[qi][:, 512 * sub + 128 * tt_:
                                                     512 * sub + 128 * (tt_ + 1)],
                                             identH[:]))
                                    pb.v(lambda: V.tensor_copy(pTt[:], psTT2[:]))
                                    for hh in range(2):
                                        pb.t(lambda qi=qi, tt_=tt_, hh=hh,
                                             tglob=tglob: T.matmul(
                                                 psO[qi][:, 512 * hh:
                                                         512 * (hh + 1)],
                                                 pTt[:],
                                                 vc[:, H * tt_ + 512 * hh:
                                                    H * tt_ + 512 * (hh + 1)],
                                                 start=(tglob == 0),
                                                 stop=(tglob == 63)))
                    for qi in range(GE):
                        q = GE * grp + qi
                        pb.v(lambda qi=qi: V.tensor_copy(oeb[:], psO[qi][:]))
                        pb.dma(lambda q=q: nc.sync.dma_start(
                            out=attn_s[128 * q:128 * (q + 1), :], in_=oeb[:]))

        if "E" in PHASES:
            _phE()
        def _phF():
            with ExitStack() as sF:
                xdT_hi = sb("xdT_hi", [128, HT * EPAD], BF16, stack=sF)
                xdT_lo = sb("xdT_lo", [128, HT * EPAD], BF16, stack=sF)
                yT_hi = sb("yT_hi", [128, FT * EPAD], BF16, stack=sF)
                yT_lo = sb("yT_lo", [128, FT * EPAD], BF16, stack=sF)
                xtile = sb("xtileF", [128, H], stack=sF)
                xntile = sb("xntileF", [128, H], stack=sF)
                f32c = sb("f32cF", [128, 128], stack=sF)
                wcf = sb("wcf", [128, HT * 128], stack=sF)
                w1h = sb("w1h", [128, HT * 128], BF16, stack=sF)
                w1l = sb("w1l", [128, HT * 128], BF16, stack=sF)
                w3h = sb("w3h", [128, HT * 128], BF16, stack=sF)
                w3l = sb("w3l", [128, HT * 128], BF16, stack=sF)
                silb = sb("silb", [128, EPAD], stack=sF)
                h3b = sb("h3b", [128, EPAD], stack=sF)
                w2h = sb("w2h", [128, 512], BF16, stack=sF)
                w2l = sb("w2l", [128, 512], BF16, stack=sF)
                oeb = sb("oebF", [128, 512], stack=sF)

                for e in range(E):
                    for j in range(EPT):
                        pb.v(lambda: V.memset(xtile[:], 0.0))
                        pb.gdma(lambda e=e, j=j: G.indirect_dma_start(
                            xtile[:], None, xb[:, :],
                            bass.IndirectOffsetOnAxis(
                                didxS[:, EPT * e + j:EPT * e + j + 1], 0)))
                        layernorm(xtile[:], xntile[:])
                        for k in range(HT):
                            hi = xdT_hi[:, EPAD * k + 128 * j:
                                        EPAD * k + 128 * (j + 1)]
                            lo = xdT_lo[:, EPAD * k + 128 * j:
                                        EPAD * k + 128 * (j + 1)]
                            pb.t(lambda k=k: T.transpose(
                                psT[:], xntile[:, 128 * k:128 * (k + 1)],
                                ident[:]))
                            pb.v(lambda: V.tensor_copy(f32c[:], psT[:]))
                            pb.v(lambda hi=hi: V.tensor_copy(hi, f32c[:]))
                            pb.v(lambda hi=hi, lo=lo: V.scalar_tensor_tensor(
                                lo, hi, -1.0, f32c[:], OP.mult, OP.add))
                    with ExitStack() as sFh:
                        for ftile in range(FT):
                            for (wmat, whi, wlo) in ((w1v, w1h, w1l),
                                                     (w3v, w3h, w3l)):
                                pb.dma(lambda wmat=wmat, e=e, ftile=ftile:
                                       nc.sync.dma_start(
                                           out=wcf.rearrange(
                                               "p (ht f) -> p ht f", f=128),
                                           in_=wmat[H * e:H * (e + 1),
                                                    128 * ftile:128 * (ftile + 1)]
                                           .rearrange("(ht p) f -> p ht f",
                                                      p=128)))
                                pb.v(lambda whi=whi: V.tensor_copy(
                                    whi[:], wcf[:]))
                                pb.v(lambda whi=whi, wlo=wlo:
                                     V.scalar_tensor_tensor(
                                         wlo[:], whi[:], -1.0, wcf[:],
                                         OP.mult, OP.add))
                            for (whi, wlo, issil) in ((w1h, w1l, True),
                                                      (w3h, w3l, False)):
                                for (s0, sw, ps) in ((0, 512, psH),
                                                     (512, 128, psHb)):
                                    nmm = 0
                                    for ht in range(HT):
                                        for (wa, xa) in ((whi, xdT_hi),
                                                         (whi, xdT_lo),
                                                         (wlo, xdT_hi)):
                                            pb.t(lambda wa=wa, xa=xa, ht=ht,
                                                 s0=s0, sw=sw, ps=ps, nmm=nmm:
                                                 T.matmul(
                                                     ps[:, 0:sw],
                                                     wa[:, 128 * ht:
                                                        128 * (ht + 1)],
                                                     xa[:, EPAD * ht + s0:
                                                        EPAD * ht + s0 + sw],
                                                     start=(nmm == 0),
                                                     stop=(nmm == 23)))
                                            nmm += 1
                                if issil:
                                    pb.s(lambda psH=psH: SC.activation(
                                        silb[:, 0:512], psH[:], AF.Silu))
                                    pb.s(lambda psHb=psHb: SC.activation(
                                        silb[:, 512:640], psHb[:], AF.Silu))
                                else:
                                    pb.v(lambda psH=psH: V.tensor_copy(
                                        h3b[:, 0:512], psH[:]))
                                    pb.v(lambda psHb=psHb: V.tensor_copy(
                                        h3b[:, 512:640], psHb[:]))
                            pb.v(lambda: V.tensor_mul(silb[:], silb[:], h3b[:]))
                            yh = yT_hi[:, EPAD * ftile:EPAD * (ftile + 1)]
                            yl = yT_lo[:, EPAD * ftile:EPAD * (ftile + 1)]
                            pb.v(lambda yh=yh: V.tensor_copy(yh, silb[:]))
                            pb.v(lambda yh=yh, yl=yl: V.scalar_tensor_tensor(
                                yl, yh, -1.0, silb[:], OP.mult, OP.add))
                    with ExitStack() as sFw:
                        for hh in range(2):
                            for ft in range(FT):
                                pb.dma(lambda e=e, ft=ft, hh=hh:
                                       nc.sync.dma_start(
                                           out=wcf[:, 0:512],
                                           in_=w2v[FF * e + 128 * ft:
                                                   FF * e + 128 * (ft + 1),
                                                   512 * hh:512 * (hh + 1)]))
                                pb.v(lambda: V.tensor_copy(w2h[:], wcf[:, 0:512]))
                                pb.v(lambda: V.scalar_tensor_tensor(
                                    w2l[:], w2h[:], -1.0, wcf[:, 0:512],
                                    OP.mult, OP.add))
                                for j in range(EPT):
                                    for nmm3, (ya, wa) in enumerate(
                                            ((yT_hi, w2h), (yT_hi, w2l),
                                             (yT_lo, w2h))):
                                        pb.t(lambda ya=ya, wa=wa, ft=ft, j=j,
                                             nmm3=nmm3: T.matmul(
                                                 psO2[j][:],
                                                 ya[:, EPAD * ft + 128 * j:
                                                    EPAD * ft + 128 * (j + 1)],
                                                 wa[:],
                                                 start=(ft == 0 and nmm3 == 0),
                                                 stop=(ft == FT - 1
                                                       and nmm3 == 2)))
                            for j in range(EPT):
                                pb.v(lambda j=j, psO2=psO2: V.tensor_copy(
                                    oeb[:], psO2[j][:]))
                                pb.dma(lambda e=e, j=j, hh=hh: nc.sync.dma_start(
                                    out=moe_s[EPAD * e + 128 * j:
                                              EPAD * e + 128 * (j + 1),
                                              512 * hh:512 * (hh + 1)],
                                    in_=oeb[:]))

        if "F" in PHASES:
            _phF()
        def _phG():
            with ExitStack() as sG:
                r1b = sb("r1b", [128, H], stack=sG)
                r2b = sb("r2b", [128, H], stack=sG)
                xtile = sb("xtileG", [128, H], stack=sG)
                atile = sb("atileG", [128, H], stack=sG)
                ob = sb("ob", [128, H], stack=sG)

                for t in range(QT):
                    pb.v(lambda: V.memset(r1b[:], 0.0))
                    pb.v(lambda: V.memset(r2b[:], 0.0))
                    if G_GATHERS:
                        pb.gdma(lambda t=t: G.indirect_dma_start(
                            r1b[:], None, moe_s[:, :],
                            bass.IndirectOffsetOnAxis(sidxG1[:, t:t + 1], 0)))
                        pb.gdma(lambda t=t: G.indirect_dma_start(
                            r2b[:], None, moe_s[:, :],
                            bass.IndirectOffsetOnAxis(sidxG2[:, t:t + 1], 0)))
                    pb.dma(lambda t=t: nc.sync.dma_start(
                        out=xtile[:], in_=xq[128 * t:128 * (t + 1), :]))
                    pb.dma(lambda t=t: nc.sync.dma_start(
                        out=atile[:], in_=attn_s[128 * t:128 * (t + 1), :]))
                    pb.v(lambda: V.tensor_add(ob[:], xtile[:], atile[:]))
                    pb.v(lambda t=t: V.scalar_tensor_tensor(
                        ob[:], r1b[:], pmS[:, t:t + 1], ob[:], OP.mult, OP.add))
                    pb.v(lambda t=t: V.scalar_tensor_tensor(
                        ob[:], r2b[:], pmS[:, QT + t:QT + t + 1], ob[:],
                        OP.mult, OP.add))
                    pb.dma(lambda t=t: nc.sync.dma_start(
                        out=out[128 * t:128 * (t + 1), :], in_=ob[:]))

        if "G" in PHASES:
            _phG()
        pb.finish(block)
    return nc


_NC_CACHE = []


def kernel(L0_attn_x, L0_attn_Wq, L0_attn_bq, L0_attn_Wk, L0_attn_bk,
           L0_attn_Wv, L0_attn_bv, L0_ffn_gate_w, L0_ffn_w1, L0_ffn_w2,
           L0_ffn_w3):
    x = np.ascontiguousarray(np.asarray(L0_attn_x, dtype=np.float32))
    shared = {
        "wq": np.ascontiguousarray(L0_attn_Wq, dtype=np.float32),
        "wk": np.ascontiguousarray(L0_attn_Wk, dtype=np.float32),
        "wv": np.ascontiguousarray(L0_attn_Wv, dtype=np.float32),
        "bq": np.ascontiguousarray(L0_attn_bq, dtype=np.float32),
        "bk": np.ascontiguousarray(L0_attn_bk, dtype=np.float32),
        "bv": np.ascontiguousarray(L0_attn_bv, dtype=np.float32),
        "gw": np.ascontiguousarray(L0_ffn_gate_w, dtype=np.float32),
        "w1": np.ascontiguousarray(L0_ffn_w1, dtype=np.float32),
        "w2": np.ascontiguousarray(L0_ffn_w2, dtype=np.float32),
        "w3": np.ascontiguousarray(L0_ffn_w3, dtype=np.float32),
    }
    in_maps = []
    for c in range(8):
        b, q0 = c // 4, SH * (c % 4)
        tokids = np.ascontiguousarray(
            (q0 + np.arange(SH, dtype=np.int32)).reshape(QT, 128).T)
        m = dict(shared)
        m["xb"] = x[b]
        m["xq"] = np.ascontiguousarray(x[b, q0:q0 + SH])
        m["tok"] = tokids
        in_maps.append(m)

    if not _NC_CACHE:
        _NC_CACHE.append(build_program())
    nc = _NC_CACHE[0]
    res = run_bass_kernel_spmd(nc, in_maps, core_ids=list(range(8)))
    full = np.empty((B, S, H), dtype=np.float32)
    for c in range(8):
        b, q0 = c // 4, SH * (c % 4)
        full[b, q0:q0 + SH] = np.asarray(res.results[c]["out"])
    return full


# version 05
# speedup vs baseline: 1.4055x; 1.4055x over previous
import numpy as np
from contextlib import ExitStack

import concourse.bass as bass
import concourse.mybir as mybir
from concourse.bass_utils import run_bass_kernel_spmd

F32 = mybir.dt.float32
F16 = mybir.dt.float16
BF16 = mybir.dt.bfloat16
I32 = mybir.dt.int32
AF = mybir.ActivationFunctionType
OP = mybir.AluOpType
AX = mybir.AxisListType.X

B, S, H, FF, E = 2, 8192, 1024, 3584, 8
CAP = 2560
SH = 2048
NT = S // 128
NG = NT // 4
QT = SH // 128
EPAD = 640
EPT = EPAD // 128
HT = H // 128
FT = FF // 128
LN2 = float(np.log(2.0))
INV_LN2 = float(1.0 / np.log(2.0))
C1 = 0.5 * INV_LN2
BIGF = float(1 << 16)
PHASES = set("ABCDEFG")
G_GATHERS = True
DEBUG_OUTS = False


class PB:

    ENG = ("sync", "tensor", "vector", "scalar", "gpsimd")

    def __init__(self, nc, sem, gsem):
        self.force_hz = True
        self.nc = nc
        self.sem = sem
        self.gsem = gsem
        self.cnt = 0
        self.gcnt = 0
        self.run_eng = None
        self.run_ops = []
        self.run_inc = 0
        self.run_wait = None
        self.ops = {e: [] for e in self.ENG}

    def _flush(self):
        if not self.run_ops:
            return
        ops = self.run_ops
        inc = self.run_inc
        wait_at, gwait_at = self.run_wait
        sem, gsem = self.sem, self.gsem
        last = len(ops) - 1

        def closure(engine, ops=ops, inc=inc, wait_at=wait_at,
                    gwait_at=gwait_at, last=last):
            for i, fn in enumerate(ops):
                if i == 0:
                    if gwait_at:
                        engine.wait_ge(gsem, gwait_at)
                    if wait_at:
                        engine.wait_ge(sem, wait_at)
                inst = fn()
                if i == last:
                    inst.then_inc(sem, inc)

        self.ops[self.run_eng].append(closure)
        self.run_ops = []
        self.run_inc = 0
        self.run_eng = None
        self.run_wait = None

    def _emit(self, eng, fn, hazard=False):
        if eng != self.run_eng:
            self._flush()
            self.run_eng = eng
            self.run_wait = (self.cnt, self.gcnt)
        self.run_ops.append(fn)
        self.run_inc += 1
        self.cnt += 1
        if hazard or self.force_hz:
            self._flush()

    def _emit_dma(self, eng, fn, ginc):
        self._flush()
        wait_at, gwait_at = self.cnt, self.gcnt
        sem, gsem = self.sem, self.gsem

        def closure(engine, fn=fn, wait_at=wait_at, gwait_at=gwait_at,
                    ginc=ginc):
            if gwait_at:
                engine.wait_ge(gsem, gwait_at)
            if wait_at:
                engine.wait_ge(sem, wait_at)
            fn().then_inc(gsem if ginc else sem, 16)

        self.ops[eng].append(closure)
        if ginc:
            self.gcnt += 16
        else:
            self.cnt += 16

    def t(self, fn, hz=False):
        self._emit("tensor", fn, hz)

    def v(self, fn, hz=False):
        self._emit("vector", fn, hz)

    def s(self, fn, hz=False):
        self._emit("scalar", fn, hz)

    def g(self, fn, hz=False):
        self._emit("gpsimd", fn, hz)

    def dma(self, fn):
        self._emit_dma("sync", fn, 0)

    def gdma(self, fn):
        self._emit_dma("gpsimd", fn, 1)

    def finish(self, block):
        self._flush()
        total = self.cnt
        gtotal = self.gcnt
        sem, gsem = self.sem, self.gsem
        for name in self.ENG:
            ops = self.ops[name]

            def run(engine, ops=ops, name=name):
                for op in ops:
                    op(engine)
                if name == "sync":
                    if gtotal:
                        engine.wait_ge(gsem, gtotal)
                    engine.wait_ge(sem, total)

            getattr(block, name)(run)


def build_program():
    nc = bass.Bass("TRN2", target_bir_lowering=False,
                   detect_race_conditions=False)

    xb = nc.dram_tensor("xb", [S, H], F32, kind="ExternalInput").ap()
    xq = nc.dram_tensor("xq", [SH, H], F32, kind="ExternalInput").ap()
    tok = nc.dram_tensor("tok", [128, QT], I32, kind="ExternalInput").ap()
    wqd = nc.dram_tensor("wq", [H, H], F32, kind="ExternalInput").ap()
    wkd = nc.dram_tensor("wk", [H, H], F32, kind="ExternalInput").ap()
    wvd = nc.dram_tensor("wv", [H, H], F32, kind="ExternalInput").ap()
    bqv = nc.dram_tensor("bq", [H], F32, kind="ExternalInput").ap()
    bkv = nc.dram_tensor("bk", [H], F32, kind="ExternalInput").ap()
    bvv = nc.dram_tensor("bv", [H], F32, kind="ExternalInput").ap()
    gw = nc.dram_tensor("gw", [E, H], F32, kind="ExternalInput").ap()
    w1 = nc.dram_tensor("w1", [E, H, FF], F32, kind="ExternalInput").ap()
    w2 = nc.dram_tensor("w2", [E, FF, H], F32, kind="ExternalInput").ap()
    w3 = nc.dram_tensor("w3", [E, H, FF], F32, kind="ExternalInput").ap()
    out = nc.dram_tensor("out", [SH, H], F32, kind="ExternalOutput").ap()

    DK = "ExternalOutput" if DEBUG_OUTS else "Internal"
    kT_s = nc.dram_tensor("kT_s", [H, S], F16, kind=DK).ap()
    v_s = nc.dram_tensor("v_s", [S, H], F16, kind=DK).ap()
    attn_s = nc.dram_tensor("attn_s", [SH, H], F32, kind=DK).ap()
    moe_s = nc.dram_tensor("moe_s", [E * EPAD, H], F32, kind=DK).ap()
    didx_s = nc.dram_tensor("didx_s", [E * EPAD, 1], I32, kind=DK).ap()
    rec_s = nc.dram_tensor("rec_s", [S, 4], F32, kind=DK).ap()
    maskh = nc.dram_tensor("maskh", [E, 2 * S], F32, kind=DK).ap()
    toth = nc.dram_tensor("toth", [128], F32, kind=DK).ap()
    offh = nc.dram_tensor("offh", [128], F32, kind=DK).ap()
    posh = nc.dram_tensor("posh", [2 * S], F32, kind=DK).ap()

    w1v = w1.rearrange("e h f -> (e h) f")
    w3v = w3.rearrange("e h f -> (e h) f")
    w2v = w2.rearrange("e f h -> (e f) h")

    st = ExitStack()
    with st:
        sem = st.enter_context(nc.semaphore("chain"))
        gsem = st.enter_context(nc.semaphore("gchain"))
        block = st.enter_context(nc.Block())
        pb = PB(nc, sem, gsem)
        V, SC, T, G = nc.vector, nc.scalar, nc.tensor, nc.gpsimd

        def sb(name, shape, dt=F32, stack=None):
            return (stack or st).enter_context(nc.sbuf_tensor(name, shape, dt))

        def psum(name, shape, stack=None, dt=F32):
            return (stack or st).enter_context(
                nc.psum_tensor(name, shape, dt))

        PSF = st.enter_context(nc.psum_tensor("PSF", [128, 3584], F32))
        PS16 = st.enter_context(nc.psum_tensor("PS16", [128, 1024], F16))
        psA = PSF[:, 0:512]
        psT = PSF[:, 3072:3200]
        psT0 = psT
        psR = PSF[0:8, 0:1024]
        psRs = PSF[0:8, 1024:1152]
        psTT = PSF[:, 3200:3208]
        psD = PSF[:, 2048:2560]
        psMT = PSF[0:8, 3072:3200]
        psP = PSF[0:16, 0:1024]
        psB8 = PSF[:, 3200:3208]
        psE = PSF[:, 2048:2560]
        psTT2 = PS16[:, 0:128]
        psO = [PSF[:, 0:1024], PSF[:, 1024:2048]]
        psH = PSF[:, 2560:3072]
        psHb = PSF[:, 3072:3200]
        psO2 = [PSF[:, 0:512], PSF[:, 512:1024], PSF[:, 1024:1536],
                PSF[:, 1536:2048], PSF[:, 2048:2560]]

        ident = sb("ident", [128, 128])
        identH = sb("identH", [128, 128], F16)
        identI = sb("identI", [128, 128], I32)
        ones1 = sb("ones1", [1, 128])
        consts = sb("consts", [128, 8])
        C_EPS, C_135, C_N505, C_SM, C_SP, C_LNEPS, C_M24 = range(7)
        biasq = sb("biasq", [128, HT])
        biask = sb("biask", [128, HT])
        bvB = sb("bvB", [128, H])
        gwT1 = sb("gwT1", [128, HT * E], BF16)
        gwT2 = sb("gwT2", [128, HT * E], BF16)
        tokS = sb("tokS", [128, QT], I32)
        didxS = sb("didxS", [128, E * EPT], I32)
        sidxI1 = sb("sidxI1", [128, QT], I32)
        sidxI2 = sb("sidxI2", [128, QT], I32)
        sidxG1 = sb("sidxG1", [128, QT], I32)
        sidxG2 = sb("sidxG2", [128, QT], I32)
        pmS = sb("pmS", [128, 2 * QT])
        rowsc = sb("rowsc", [128, 16])

        PW = 2048
        qs_a = sb("qs_a", [128, PW])
        qs_b = sb("qs_b", [128, PW])
        qs_c = sb("qs_c", [128, PW])
        qs_d = sb("qs_d", [128, PW])
        qs_i = sb("qs_i", [128, PW], I32)

        sBC = ExitStack()
        logP = sb("logP", [128, NT * 72], stack=sBC)
        oh1 = sb("oh1", [128, NT * E], stack=sBC)
        oh2 = sb("oh2", [128, NT * E], stack=sBC)
        lgb = sb("lgb", [128, NT * E], stack=sBC)
        sA5 = sb("sA5", [128, NT * E], stack=sBC)
        sB5 = sb("sB5", [128, NT * E], stack=sBC)
        iotaE = sb("iotaE", [128, NT * E], stack=sBC)
        m1 = sb("m1", [128, NT], stack=sBC)
        m2 = sb("m2", [128, NT], stack=sBC)
        i1f = sb("i1f", [128, NT], stack=sBC)
        i2f = sb("i2f", [128, NT], stack=sBC)
        p1x = sb("p1x", [128, NT], stack=sBC)
        p2x = sb("p2x", [128, NT], stack=sBC)
        sA6 = sb("sA6", [128, NT], stack=sBC)
        sB6 = sb("sB6", [128, NT], stack=sBC)

        def e3(ap):
            return ap.rearrange("p (t e) -> p t e", e=8)

        def quantize(src, dst, width):
            prev = pb.force_hz
            pb.force_hz = False
            a = qs_a[:, 0:width]
            b2 = qs_b[:, 0:width]
            c = qs_c[:, 0:width]
            d = qs_d[:, 0:width]
            ii = qs_i[:, 0:width]
            pb.s(lambda: SC.activation(a, src, AF.Square))
            pb.s(lambda: SC.activation(
                b2, a, AF.Ln, bias=consts[:, C_EPS:C_EPS + 1]))
            pb.s(lambda: SC.activation(
                d, b2, AF.Relu, scale=C1, bias=consts[:, C_135:C_135 + 1]))
            pb.v(lambda: V.tensor_copy(ii, d))
            pb.v(lambda: V.tensor_copy(a, ii))
            pb.v(lambda: V.scalar_tensor_tensor(
                b2, a, -1.0, d, OP.mult, OP.add))
            pb.v(lambda: V.tensor_single_scalar(b2, b2, 0.5, OP.is_ge))
            pb.v(lambda: V.scalar_tensor_tensor(
                a, b2, 1.0, a, OP.mult, OP.add))
            pb.s(lambda: SC.activation(
                b2, a, AF.Exp, scale=-LN2, bias=consts[:, C_SM:C_SM + 1]))
            pb.s(lambda: SC.activation(
                c, a, AF.Exp, scale=LN2, bias=consts[:, C_SP:C_SP + 1]))
            pb.v(lambda: V.tensor_mul(a, src, b2))
            pb.v(lambda: V.tensor_single_scalar(d, a, 4095.5, OP.add))
            pb.v(lambda: V.tensor_copy(ii, d))
            pb.v(lambda: V.tensor_copy(d, ii))
            pb.v(lambda: V.scalar_tensor_tensor(
                b2, d, -1.0, a, OP.mult, OP.add))
            pb.v(lambda: V.tensor_single_scalar(
                b2, b2, -4095.5, OP.is_ge))
            pb.v(lambda: V.scalar_tensor_tensor(
                d, b2, 1.0, d, OP.mult, OP.add))
            pb.v(lambda: V.tensor_single_scalar(d, d, -4096.0, OP.add))
            pb.v(lambda: V.tensor_mul(dst, d, c))
            pb.force_hz = prev

        def layernorm(xt, xnt):
            prev_hz = pb.force_hz
            pb.force_hz = True
            rs = rowsc[:, 0:1]
            ss = rowsc[:, 1:2]
            pb.v(lambda: V.reduce_sum(rs, xt, AX), hz=True)
            pb.v(lambda: V.tensor_single_scalar(rs, rs, 1.0 / H, OP.mult))
            pb.v(lambda: V.tensor_scalar(xnt, xt, rs, None, OP.subtract))
            pb.s(lambda: SC.activation(
                qs_a[:, 0:H], xnt, AF.Square, accum_out=ss), hz=True)
            pb.s(lambda: SC.activation(
                ss, ss, AF.Sqrt, scale=1.0 / H,
                bias=consts[:, C_LNEPS:C_LNEPS + 1]))
            pb.v(lambda: V.reciprocal(ss, ss), hz=True)
            pb.v(lambda: V.tensor_scalar(xnt, xnt, ss, None, OP.mult))
            pb.force_hz = prev_hz

        def ln_transpose_split(i_dram, src_dram, xtile, xntile, f32c, psT,
                               dhi, dlo, ii):
            pb.dma(lambda: nc.sync.dma_start(
                out=xtile[:],
                in_=src_dram[128 * i_dram:128 * (i_dram + 1), :]))
            layernorm(xtile[:], xntile[:])
            for k in range(HT):
                hi = dhi[:, 512 * k + 128 * ii:512 * k + 128 * (ii + 1)]
                lo = dlo[:, 512 * k + 128 * ii:512 * k + 128 * (ii + 1)]
                pb.t(lambda k=k: T.transpose(
                    psT[:], xntile[:, 128 * k:128 * (k + 1)], ident[:]))
                pb.v(lambda: V.tensor_copy(f32c[:], psT[:]))
                pb.v(lambda hi=hi: V.tensor_copy(hi, f32c[:]))
                pb.v(lambda hi=hi, lo=lo: V.scalar_tensor_tensor(
                    lo, hi, -1.0, f32c[:], OP.mult, OP.add))

        def prep_wT(wsrc, hi, lo, wst, psT):
            pb.dma(lambda: nc.sync.dma_start(
                out=wst.rearrange("p (ot h) -> p ot h", h=H),
                in_=wsrc.rearrange("(ot p) h -> p ot h", p=128)))
            for ot in range(HT):
                for ht in range(HT):
                    pb.t(lambda ot=ot, ht=ht: T.transpose(
                        psT[:], wst[:, 1024 * ot + 128 * ht:
                                    1024 * ot + 128 * (ht + 1)], ident[:]))
                    pb.v(lambda: V.tensor_copy(qs_a[:, 0:128], psT[:]))
                    dh = hi[:, 1024 * ht + 128 * ot:1024 * ht + 128 * (ot + 1)]
                    dl = lo[:, 1024 * ht + 128 * ot:1024 * ht + 128 * (ot + 1)]
                    pb.v(lambda dh=dh: V.tensor_copy(dh, qs_a[:, 0:128]))
                    pb.v(lambda dh=dh, dl=dl: V.scalar_tensor_tensor(
                        dl, dh, -1.0, qs_a[:, 0:128], OP.mult, OP.add))

        pb.g(lambda: G.iota(identI[:], [[-1, 128]], base=0,
                            channel_multiplier=1))
        pb.v(lambda: V.tensor_copy(ident[:], identI[:]))
        pb.v(lambda: V.tensor_single_scalar(
            ident[:], ident[:], 0.0, OP.is_equal))
        pb.v(lambda: V.tensor_copy(identH[:], ident[:]))
        pb.v(lambda: V.memset(ones1[:], 1.0))
        for idx, val in [(C_EPS, 2.0 ** -40), (C_135, 13.5), (C_N505, -50.5),
                         (C_SM, 24.0 * LN2), (C_SP, -24.0 * LN2),
                         (C_LNEPS, 1e-5), (C_M24, -24.0 * LN2)]:
            pb.v(lambda idx=idx, val=val: V.memset(
                consts[:, idx:idx + 1], float(val)))
        pb.dma(lambda: nc.sync.dma_start(
            out=biasq[:], in_=bqv.rearrange("(j p) -> p j", p=128),
            allow_slow_non_contiguous=True))
        pb.dma(lambda: nc.sync.dma_start(
            out=biask[:], in_=bkv.rearrange("(j p) -> p j", p=128),
            allow_slow_non_contiguous=True))
        pb.dma(lambda: nc.sync.dma_start(out=tokS[:], in_=tok[:]))
        pb.g(lambda: G.iota(qs_i[:, 0:NT * E], [[0, NT], [1, E]],
                            base=0, channel_multiplier=0))
        pb.v(lambda: V.tensor_copy(iotaE[:], qs_i[:, 0:NT * E]))

        with ExitStack() as sA:
            bvrow = sb("bvrow", [1, H], stack=sA)
            gws = sb("gws", [8, H], stack=sA)
            gtf = sb("gtf", [128, 8], stack=sA)
            pb.dma(lambda: nc.sync.dma_start(
                out=bvrow[:], in_=bvv.rearrange("(o h) -> o h", o=1)))
            for half in range(2):
                pb.t(lambda half=half: T.matmul(
                    psA[:], ones1[:], bvrow[:, 512 * half:512 * (half + 1)],
                    start=True, stop=True))
                pb.v(lambda half=half: V.tensor_copy(
                    bvB[:, 512 * half:512 * (half + 1)], psA[:]))
            pb.dma(lambda: nc.sync.dma_start(out=gws[:], in_=gw[:]))
            for k in range(HT):
                pb.t(lambda k=k: T.transpose(
                    psA[:, 0:8], gws[:, 128 * k:128 * (k + 1)],
                    ident[0:8, 0:8]))
                pb.v(lambda: V.tensor_copy(gtf[:], psA[:, 0:8]))
                pb.v(lambda k=k: V.tensor_copy(
                    gwT1[:, 8 * k:8 * (k + 1)], gtf[:]))
                pb.v(lambda k=k: V.scalar_tensor_tensor(
                    gwT2[:, 8 * k:8 * (k + 1)], gwT1[:, 8 * k:8 * (k + 1)],
                    -1.0, gtf[:], OP.mult, OP.add))

        def _phB():
            with ExitStack() as sBo:
                wkT_hi = sb("wkT_hi", [128, H * HT], BF16, stack=sBo)
                wkT_lo = sb("wkT_lo", [128, H * HT], BF16, stack=sBo)
                wvT_hi = sb("wvT_hi", [128, H * HT], BF16, stack=sBo)
                wvT_lo = sb("wvT_lo", [128, H * HT], BF16, stack=sBo)
                with ExitStack() as sBw:
                    wst = sb("wst", [128, H * HT], stack=sBw)
                    prep_wT(wkd, wkT_hi, wkT_lo, wst, psT0)
                    prep_wT(wvd, wvT_hi, wvT_lo, wst, psT0)
                xcT_hi = sb("xcT_hi", [128, HT * 512], BF16, stack=sBo)
                xcT_lo = sb("xcT_lo", [128, HT * 512], BF16, stack=sBo)
                xtile = sb("xtile", [128, H], stack=sBo)
                xntile = sb("xntile", [128, H], stack=sBo)
                f32c = sb("f32c", [128, 128], stack=sBo)
                evb = sb("evb", [128, 2048], stack=sBo)
                hf16 = sb("hf16", [128, 2048], F16, stack=sBo)
                rsc = sb("rsc", [8, 1152], stack=sBo)

                for g in range(NG):
                    pb.force_hz = True
                    for ii in range(4):
                        i = 4 * g + ii
                        ln_transpose_split(i, xb, xtile, xntile, f32c, psT,
                                           xcT_hi, xcT_lo, ii)
                        for k in range(HT):
                            pb.t(lambda k=k, ii=ii: T.matmul(
                                psR[:, 128 * k:128 * (k + 1)],
                                gwT1[:, 8 * k:8 * (k + 1)],
                                xcT_hi[:, 512 * k + 128 * ii:
                                       512 * k + 128 * (ii + 1)],
                                start=True, stop=True))
                        nsm = 0
                        for k in range(HT):
                            for (ga, xa) in ((gwT2, xcT_hi), (gwT1, xcT_lo),
                                             (gwT2, xcT_lo)):
                                pb.t(lambda k=k, ii=ii, ga=ga, xa=xa, nsm=nsm:
                                     T.matmul(
                                         psRs[:], ga[:, 8 * k:8 * (k + 1)],
                                         xa[:, 512 * k + 128 * ii:
                                            512 * k + 128 * (ii + 1)],
                                         start=(nsm == 0), stop=(nsm == 23)))
                                nsm += 1
                        pb.v(lambda: V.tensor_copy(rsc[:, 0:1024], psR[:]))
                        pb.v(lambda: V.tensor_copy(rsc[:, 1024:1152], psRs[:]))
                        for pl in range(9):
                            pb.t(lambda pl=pl: T.transpose(
                                psTT[:], rsc[:, 128 * pl:128 * (pl + 1)],
                                ident[0:8, 0:8]))
                            pb.v(lambda i=i, pl=pl: V.tensor_copy(
                                logP[:, 72 * i + 8 * pl:72 * i + 8 * (pl + 1)],
                                psTT[:]))
                    for oth in range(2):
                        for otl in range(4):
                            ot = 4 * oth + otl
                            nmm = 0
                            for ht in range(HT):
                                for (wa, xa) in ((wkT_hi, xcT_hi),
                                                 (wkT_hi, xcT_lo),
                                                 (wkT_lo, xcT_hi)):
                                    pb.t(lambda ot=ot, ht=ht, wa=wa, xa=xa,
                                         nmm=nmm: T.matmul(
                                             psD[:],
                                             wa[:, 1024 * ht + 128 * ot:
                                                1024 * ht + 128 * (ot + 1)],
                                             xa[:, 512 * ht:512 * (ht + 1)],
                                             start=(nmm == 0), stop=(nmm == 23)))
                                    nmm += 1
                            pb.v(lambda ot=ot, otl=otl: V.tensor_scalar(
                                evb[:, 512 * otl:512 * (otl + 1)], psD[:],
                                biask[:, ot:ot + 1], None, OP.add))
                        quantize(evb[:, 0:2048], hf16[:, 0:2048], 2048)
                        pb.dma(lambda g=g, oth=oth: nc.sync.dma_start(
                            out=kT_s.rearrange("(ot p) s -> p ot s", p=128)[
                                :, 4 * oth:4 * (oth + 1), 512 * g:512 * (g + 1)],
                            in_=hf16.rearrange("p (ot s) -> p ot s", s=512)))
                    for iih in range(2):
                        for iil in range(2):
                            ii = 2 * iih + iil
                            for oc in range(2):
                                nmm = 0
                                for ht in range(HT):
                                    for (xa, wa) in ((xcT_hi, wvT_hi),
                                                     (xcT_hi, wvT_lo),
                                                     (xcT_lo, wvT_hi)):
                                        pb.t(lambda ii=ii, oc=oc, ht=ht, xa=xa,
                                             wa=wa, nmm=nmm: T.matmul(
                                                 psD[:],
                                                 xa[:, 512 * ht + 128 * ii:
                                                    512 * ht + 128 * (ii + 1)],
                                                 wa[:, 1024 * ht + 512 * oc:
                                                    1024 * ht + 512 * (oc + 1)],
                                                 start=(nmm == 0),
                                                 stop=(nmm == 23)))
                                        nmm += 1
                                pb.v(lambda iil=iil, oc=oc: V.tensor_add(
                                    evb[:, 1024 * iil + 512 * oc:
                                        1024 * iil + 512 * (oc + 1)],
                                    psD[:], bvB[:, 512 * oc:512 * (oc + 1)]))
                        quantize(evb[:, 0:2048], hf16[:, 0:2048], 2048)
                        pb.dma(lambda g=g, iih=iih: nc.sync.dma_start(
                            out=v_s[512 * g + 256 * iih:
                                    512 * g + 256 * (iih + 1), :].rearrange(
                                "(ii p) h -> p ii h", p=128),
                            in_=hf16.rearrange("p (ii h) -> p ii h", h=1024)))

        if "B" in PHASES:
            _phB()
        def _phC():
            pb.force_hz = True
            with ExitStack() as sC:
                eftS = sb("eftS", [128, NT * E], stack=sC)
                eftC = sb("eftC", [128, NT * E], stack=sC)
                eftX = sb("eftX", [128, NT * E], stack=sC)
                eftT = sb("eftT", [128, NT * E], stack=sC)
                eftZ = sb("eftZ", [128, NT * E], stack=sC)
                mbuf = sb("mbuf", [8, 2048], stack=sC)
                m8 = sb("m8", [8, 16], stack=sC)
                scanL = sb("scanL", [128, 1024], stack=sC)
                cg = sb("cg", [128, 1024], stack=sC)
                cmb = sb("cmb", [128, 1024], stack=sC)
                tot8 = sb("tot8", [8, 16], stack=sC)
                offs8 = sb("offs8", [8, 16], stack=sC)
                offCol = sb("offCol", [128, 1], stack=sC)
                selM = sb("selM", [128, 16], stack=sC)
                pos16 = sb("pos16", [16, 1024], stack=sC)
                posTok = sb("posTok", [128, 128], stack=sC)
                baseB = sb("baseB", [128, 32], stack=sC)
                bRow = sb("bRow", [1, 8], stack=sC)
                recS = sb("recS", [128, NT * 4], stack=sC)
                recAll = sb("recAll", [128, QT * 4], stack=sC)
                initI = sb("initI", [128, E * EPT], I32, stack=sC)
                iotaPf = sb("iotaPf", [128, 1], stack=sC)
                iota16 = sb("iota16", [128, 16], stack=sC)

                logv = logP.rearrange("p (t pl e) -> p t pl e", pl=9, e=8)

                pb.v(lambda: V.tensor_copy(e3(eftS[:, :]), logv[:, :, 0, :]))
                pb.v(lambda: V.memset(eftC[:], 0.0))
                for pl in range(1, 9):
                    pb.v(lambda pl=pl: V.tensor_copy(
                        e3(eftX[:, :]), logv[:, :, pl, :]))
                    pb.v(lambda: V.tensor_add(eftT[:], eftS[:], eftX[:]))
                    pb.v(lambda: V.tensor_sub(eftZ[:], eftT[:], eftS[:]))
                    pb.v(lambda: V.tensor_sub(sA5[:], eftT[:], eftZ[:]))
                    pb.v(lambda: V.tensor_sub(sA5[:], eftS[:], sA5[:]))
                    pb.v(lambda: V.tensor_add(eftC[:], eftC[:], sA5[:]))
                    pb.v(lambda: V.tensor_sub(eftZ[:], eftX[:], eftZ[:]))
                    pb.v(lambda: V.tensor_add(eftC[:], eftC[:], eftZ[:]))
                    pb.v(lambda: V.tensor_copy(eftS[:], eftT[:]))
                pb.v(lambda: V.tensor_add(lgb[:], eftS[:], eftC[:]))

                bc = [128, NT, 8]
                pb.v(lambda: V.reduce_max(m1[:].unsqueeze(2), e3(lgb[:, :]), AX), hz=True)
                pb.v(lambda: V.tensor_tensor(
                    e3(sA5[:, :]), e3(lgb[:, :]),
                    m1[:].unsqueeze(2).broadcast_to(bc), OP.is_ge))
                pb.v(lambda: V.tensor_single_scalar(
                    sB5[:], iotaE[:], -999.0, OP.add))
                pb.v(lambda: V.tensor_mul(sA5[:], sA5[:], sB5[:]))
                pb.v(lambda: V.tensor_single_scalar(sA5[:], sA5[:], 999.0, OP.add))
                pb.v(lambda: V.tensor_reduce(
                    i1f[:].unsqueeze(2), e3(sA5[:, :]), AX, OP.min), hz=True)
                pb.v(lambda: V.tensor_tensor(
                    e3(oh1[:, :]), e3(iotaE[:, :]),
                    i1f[:].unsqueeze(2).broadcast_to(bc), OP.is_equal))
                pb.v(lambda: V.scalar_tensor_tensor(
                    sB5[:], oh1[:], -1e30, lgb[:], OP.mult, OP.add))
                pb.v(lambda: V.reduce_max(m2[:].unsqueeze(2), e3(sB5[:, :]), AX), hz=True)
                pb.v(lambda: V.tensor_tensor(
                    e3(sA5[:, :]), e3(sB5[:, :]),
                    m2[:].unsqueeze(2).broadcast_to(bc), OP.is_ge))
                pb.v(lambda: V.tensor_single_scalar(
                    sB5[:], iotaE[:], -999.0, OP.add))
                pb.v(lambda: V.tensor_mul(sA5[:], sA5[:], sB5[:]))
                pb.v(lambda: V.tensor_single_scalar(sA5[:], sA5[:], 999.0, OP.add))
                pb.v(lambda: V.tensor_reduce(
                    i2f[:].unsqueeze(2), e3(sA5[:, :]), AX, OP.min), hz=True)
                pb.v(lambda: V.tensor_tensor(
                    e3(oh2[:, :]), e3(iotaE[:, :]),
                    i2f[:].unsqueeze(2).broadcast_to(bc), OP.is_equal))
                pb.v(lambda: V.tensor_tensor(
                    e3(sA5[:, :]), e3(lgb[:, :]),
                    m1[:].unsqueeze(2).broadcast_to(bc), OP.subtract))
                pb.s(lambda: SC.activation(sA5[:], sA5[:], AF.Exp))
                pb.v(lambda: V.reduce_sum(sA6[:].unsqueeze(2), e3(sA5[:, :]), AX), hz=True)
                pb.v(lambda: V.reciprocal(sA6[:], sA6[:]), hz=True)
                pb.v(lambda: V.tensor_tensor(
                    e3(sA5[:, :]), e3(sA5[:, :]),
                    sA6[:].unsqueeze(2).broadcast_to(bc), OP.mult))
                pb.v(lambda: V.tensor_mul(sB5[:], sA5[:], oh1[:]))
                pb.v(lambda: V.reduce_sum(p1x[:].unsqueeze(2), e3(sB5[:, :]), AX), hz=True)
                pb.v(lambda: V.tensor_mul(sB5[:], sA5[:], oh2[:]))
                pb.v(lambda: V.reduce_sum(p2x[:].unsqueeze(2), e3(sB5[:, :]), AX), hz=True)

                mb3 = mbuf.rearrange("e (w k) -> e w k", k=2)
                for grp in range(8):
                    for iw in range(8):
                        i = 8 * grp + iw
                        for k, oh in ((0, oh1), (1, oh2)):
                            pb.t(lambda i=i, oh=oh: T.transpose(
                                psMT[:], oh[:, 8 * i:8 * (i + 1)], ident[:]))
                            pb.v(lambda iw=iw, k=k: V.tensor_copy(
                                mb3[:, 128 * iw:128 * (iw + 1), k:k + 1],
                                psMT[:].unsqueeze(2)))
                    pb.dma(lambda grp=grp: nc.sync.dma_start(
                        out=maskh[:, 2048 * grp:2048 * (grp + 1)], in_=mbuf[:]))
                pb.dma(lambda: nc.sync.dma_start(
                    out=cmb[:], in_=maskh.rearrange("e (c w) -> (e c) w", c=16)))
                pb.v(lambda: V.tensor_tensor_scan(
                    scanL[:], cmb[:], cmb[:], 0.0, OP.add, OP.bypass), hz=True)
                pb.dma(lambda: nc.sync.dma_start(
                    out=toth.rearrange("(p o) -> p o", o=1),
                    in_=scanL[:, 1023:1024]))
                pb.dma(lambda: nc.sync.dma_start(
                    out=tot8[:], in_=toth.rearrange("(e c) -> e c", c=16)))
                pb.v(lambda: V.tensor_copy(m8[:], tot8[:]))
                pb.v(lambda: V.tensor_tensor_scan(
                    tot8[:], m8[:], m8[:], 0.0, OP.add, OP.bypass), hz=True)
                pb.v(lambda: V.memset(offs8[:, 0:1], 0.0))
                pb.v(lambda: V.tensor_copy(offs8[:, 1:16], tot8[:, 0:15]))
                pb.dma(lambda: nc.sync.dma_start(
                    out=offh.rearrange("(e c) -> e c", c=16), in_=offs8[:]))
                pb.dma(lambda: nc.sync.dma_start(
                    out=offCol[:], in_=offh.rearrange("(p o) -> p o", o=1)))
                pb.v(lambda: V.tensor_scalar(
                    cg[:], scanL[:], offCol[:, 0:1], None, OP.add))
                pb.v(lambda: V.tensor_mul(cg[:], cg[:], cmb[:]))
                pb.g(lambda: G.iota(qs_i[:, 0:1], [[0, 1]], base=0,
                                    channel_multiplier=1))
                pb.v(lambda: V.tensor_copy(iotaPf[:], qs_i[:, 0:1]))
                pb.g(lambda: G.iota(qs_i[:, 0:16], [[1, 16]], base=0,
                                    channel_multiplier=0))
                pb.v(lambda: V.tensor_copy(iota16[:], qs_i[:, 0:16]))
                t0, t1 = rowsc[:, 8:9], rowsc[:, 9:10]
                pb.v(lambda: V.tensor_scalar(
                    t0, iotaPf[:], 1.0 / 16.0, -0.5, OP.mult, OP.add))
                pb.v(lambda: V.tensor_copy(qs_i[:, 0:1], t0))
                pb.v(lambda: V.tensor_copy(t1, qs_i[:, 0:1]))
                pb.v(lambda: V.tensor_sub(t0, t0, t1))
                pb.v(lambda: V.tensor_single_scalar(t0, t0, 0.5, OP.is_ge))
                pb.v(lambda: V.scalar_tensor_tensor(
                    t1, t0, 1.0, t1, OP.mult, OP.add))
                pb.v(lambda: V.scalar_tensor_tensor(
                    t0, t1, -16.0, iotaPf[:], OP.mult, OP.add))
                pb.v(lambda: V.tensor_scalar(
                    selM[:], iota16[:], t0, None, OP.is_equal))
                for j in range(2):
                    pb.t(lambda j=j: T.matmul(
                        psP[:, 512 * j:512 * (j + 1)], selM[:],
                        cg[:, 512 * j:512 * (j + 1)], start=True, stop=True))
                pb.v(lambda: V.tensor_copy(pos16[:], psP[:]))
                pb.dma(lambda: nc.sync.dma_start(
                    out=posh.rearrange("(c w) -> c w", w=1024), in_=pos16[:]))
                pb.dma(lambda: nc.sync.dma_start(
                    out=posTok.rearrange("p (t k) -> p t k", k=2),
                    in_=posh.rearrange("(t p k) -> p t k", p=128, k=2),
                    allow_slow_non_contiguous=True))
                for q in range(4):
                    pb.t(lambda q=q: T.transpose(
                        psMT[0:1, 0:8], offs8[:, 4 * q:4 * q + 1],
                        ident[0:8, 0:8]))
                    pb.v(lambda: V.tensor_copy(bRow[:], psMT[0:1, 0:8]))
                    pb.t(lambda: T.matmul(
                        psB8[:], ones1[:], bRow[:], start=True, stop=True))
                    pb.v(lambda q=q: V.tensor_copy(
                        baseB[:, 8 * q:8 * (q + 1)], psB8[:]))
                posv = posTok.rearrange("p (t k) -> p t k", k=2)
                recv = recS.rearrange("p (t r) -> p t r", r=4)
                for k, ohk, ikf, pkx in ((0, oh1, i1f, p1x), (1, oh2, i2f, p2x)):
                    for q in range(4):
                        pb.v(lambda q=q, ohk=ohk: V.tensor_tensor(
                            e3(sA5[:, :])[:, 16 * q:16 * (q + 1), :],
                            e3(ohk[:, :])[:, 16 * q:16 * (q + 1), :],
                            baseB[:, 8 * q:8 * (q + 1)].unsqueeze(1)
                            .broadcast_to([128, 16, 8]), OP.mult))
                    pb.v(lambda: V.reduce_sum(
                        sA6[:].unsqueeze(2), e3(sA5[:, :]), AX), hz=True)
                    pb.v(lambda k=k: V.tensor_tensor(
                        sB6[:].unsqueeze(2), posv[:, :, k:k + 1],
                        sA6[:].unsqueeze(2), OP.subtract))
                    pb.v(lambda ikf=ikf: V.tensor_scalar(
                        sA6[:], ikf[:], 640.0, -1.0, OP.mult, OP.add))
                    pb.v(lambda: V.tensor_add(sA6[:], sA6[:], sB6[:]))
                    pb.v(lambda k=k: V.tensor_copy(
                        m2[:].unsqueeze(2), posv[:, :, k:k + 1]))
                    pb.v(lambda: V.tensor_single_scalar(
                        m2[:], m2[:], CAP + 0.5, OP.is_le))
                    pb.v(lambda: V.tensor_single_scalar(
                        sB6[:], sB6[:], EPAD + 0.5, OP.is_le))
                    pb.v(lambda: V.tensor_mul(m2[:], m2[:], sB6[:]))
                    pb.v(lambda: V.tensor_single_scalar(
                        sA6[:], sA6[:], BIGF, OP.add))
                    pb.v(lambda: V.scalar_tensor_tensor(
                        sA6[:], m2[:], -BIGF, sA6[:], OP.mult, OP.add))
                    pb.v(lambda pkx=pkx: V.tensor_mul(pkx[:], pkx[:], m2[:]))
                    pb.v(lambda k=k: V.tensor_copy(
                        recv[:, :, k:k + 1], sA6[:].unsqueeze(2)))
                    pb.v(lambda k=k, pkx=pkx: V.tensor_copy(
                        recv[:, :, 2 + k:3 + k], pkx[:].unsqueeze(2)))
                pb.dma(lambda: nc.sync.dma_start(
                    out=rec_s.rearrange("(t p) r -> p t r", p=128),
                    in_=recS.rearrange("p (t r) -> p t r", r=4)))
                pb.v(lambda: V.memset(initI[:], 0))
                pb.dma(lambda: nc.sync.dma_start(
                    out=didx_s.rearrange("(g p) o -> p g o", p=128),
                    in_=initI[:].unsqueeze(2),
                    allow_slow_non_contiguous=True))
                for t in range(QT):
                    pb.gdma(lambda t=t: G.indirect_dma_start(
                        recAll[:, 4 * t:4 * (t + 1)], None, rec_s[:, :],
                        bass.IndirectOffsetOnAxis(tokS[:, t:t + 1], 0)))
                recav = recAll.rearrange("p (t r) -> p t r", r=4)
                pb.v(lambda: V.tensor_copy(
                    sidxI1[:].unsqueeze(2), recav[:, :, 0:1]))
                pb.v(lambda: V.tensor_copy(
                    sidxI2[:].unsqueeze(2), recav[:, :, 1:2]))
                pb.v(lambda: V.tensor_copy(
                    pmS[:, 0:QT].unsqueeze(2), recav[:, :, 2:3]))
                pb.v(lambda: V.tensor_copy(
                    pmS[:, QT:2 * QT].unsqueeze(2), recav[:, :, 3:4]))
                for (pmsl, rsl, dst) in ((0, 0, sidxG1), (QT, 1, sidxG2)):
                    pb.v(lambda pmsl=pmsl: V.tensor_single_scalar(
                        qs_a[:, 0:QT], pmS[:, pmsl:pmsl + QT], 0.0, OP.is_gt))
                    pb.v(lambda rsl=rsl: V.tensor_tensor(
                        qs_b[:, 0:QT].unsqueeze(2), recav[:, :, rsl:rsl + 1],
                        qs_a[:, 0:QT].unsqueeze(2), OP.mult))
                    pb.v(lambda: V.tensor_scalar(
                        qs_b[:, 0:QT], qs_b[:, 0:QT], 0.0,
                        float(E * EPAD - 1), OP.max, OP.min))
                    pb.v(lambda dst=dst: V.tensor_copy(dst[:], qs_b[:, 0:QT]))
                for t in range(QT):
                    for sidx in (sidxI1, sidxI2):
                        pb.gdma(lambda t=t, sidx=sidx: G.indirect_dma_start(
                            didx_s[:, :],
                            bass.IndirectOffsetOnAxis(sidx[:, t:t + 1], 0),
                            tokS[:, t:t + 1], None,
                            bounds_check=E * EPAD - 1, oob_is_err=False))
                pb.dma(lambda: nc.sync.dma_start(
                    out=didxS[:],
                    in_=didx_s.rearrange("(g p) o -> p (g o)", p=128),
                    allow_slow_non_contiguous=True))

        if "C" in PHASES:
            _phC()
            pb.force_hz = False
        sBC.close()

        qTt = sb("qT", [128, HT * SH], F16)
        def _phD():
            with ExitStack() as sD:
                wqT_hi = sb("wqT_hi", [128, H * HT], BF16, stack=sD)
                wqT_lo = sb("wqT_lo", [128, H * HT], BF16, stack=sD)
                with ExitStack() as sDw:
                    wst = sb("wstD", [128, H * HT], stack=sDw)
                    prep_wT(wqd, wqT_hi, wqT_lo, wst, psT0)
                xcT_hi = sb("xcT_hiD", [128, HT * 512], BF16, stack=sD)
                xcT_lo = sb("xcT_loD", [128, HT * 512], BF16, stack=sD)
                xtile = sb("xtileD", [128, H], stack=sD)
                xntile = sb("xntileD", [128, H], stack=sD)
                f32c = sb("f32cD", [128, 128], stack=sD)
                evb = sb("evbD", [128, 2048], stack=sD)

                for g in range(4):
                    pb.force_hz = True
                    for ii in range(4):
                        ln_transpose_split(4 * g + ii, xq, xtile, xntile, f32c,
                                           psT, xcT_hi, xcT_lo, ii)
                    pb.force_hz = False
                    for oth in range(2):
                        for otl in range(4):
                            ot = 4 * oth + otl
                            nmm = 0
                            for ht in range(HT):
                                for (wa, xa) in ((wqT_hi, xcT_hi),
                                                 (wqT_hi, xcT_lo),
                                                 (wqT_lo, xcT_hi)):
                                    pb.t(lambda ot=ot, ht=ht, wa=wa, xa=xa,
                                         nmm=nmm: T.matmul(
                                             psD[:],
                                             wa[:, 1024 * ht + 128 * ot:
                                                1024 * ht + 128 * (ot + 1)],
                                             xa[:, 512 * ht:512 * (ht + 1)],
                                             start=(nmm == 0), stop=(nmm == 23)))
                                    nmm += 1
                            pb.v(lambda ot=ot, otl=otl: V.tensor_scalar(
                                evb[:, 512 * otl:512 * (otl + 1)], psD[:],
                                biasq[:, ot:ot + 1], None, OP.add))
                        for otl in range(4):
                            ot = 4 * oth + otl
                            quantize(evb[:, 512 * otl:512 * (otl + 1)],
                                     qTt[:, SH * ot + 512 * g:
                                         SH * ot + 512 * (g + 1)], 512)

        if "D" in PHASES:
            _phD()
        def _phE():
            with ExitStack() as sE:
                GE = 2
                sq = [sb(f"sq{i}", [128, S], F16, stack=sE) for i in range(GE)]
                sfb = [sb(f"sfb{i}", [128, 2048], stack=sE) for i in range(GE)]
                kc = sb("kc", [128, HT * 512], F16, stack=sE)
                vc = sb("vc", [128, 4 * H], F16, stack=sE)
                zfb = sb("zfb", [128, 2048], stack=sE)
                pfb = [sb(f"pfb{i}", [128, 2048], F16, stack=sE)
                       for i in range(GE)]
                pTt = sb("pTt", [128, 128], F16, stack=sE)
                sums = sb("sums", [128, 2 * GE], stack=sE)
                oeb = sb("oeb", [128, H], stack=sE)
                kcv = kc.rearrange("p (ot s) -> p ot s", s=512)
                vcv = vc.rearrange("p (tt h) -> p tt h", h=H)

                for grp in range(QT // GE):
                    pb.force_hz = False
                    for band in range(4):
                        for tc4 in range(4):
                            tc = 4 * band + tc4
                            pb.dma(lambda tc=tc: nc.sync.dma_start(
                                out=kcv,
                                in_=kT_s.rearrange("(ot p) s -> p ot s", p=128)[
                                    :, :, 512 * tc:512 * (tc + 1)]))
                            for qi in range(GE):
                                q = GE * grp + qi
                                for ot in range(HT):
                                    pb.t(lambda q=q, ot=ot: T.matmul(
                                        psE[:],
                                        qTt[:, SH * ot + 128 * q:
                                            SH * ot + 128 * (q + 1)],
                                        kc[:, 512 * ot:512 * (ot + 1)],
                                        start=(ot == 0), stop=(ot == 7)))
                                pb.s(lambda qi=qi, tc4=tc4: SC.activation(
                                    sfb[qi][:, 512 * tc4:512 * (tc4 + 1)],
                                    psE[:], AF.Copy, scale=1.0 / 32.0))
                        for qi in range(GE):
                            quantize(sfb[qi][:, 0:2048],
                                     sq[qi][:, 2048 * band:2048 * (band + 1)],
                                     2048)
                    pb.force_hz = True
                    for qi in range(GE):
                        mq = rowsc[:, 2 + qi:3 + qi]
                        negm = rowsc[:, 4 + qi:5 + qi]
                        rowA = rowsc[:, 8 + qi:9 + qi]
                        rowB = rowsc[:, 12 + qi:13 + qi]
                        pb.v(lambda qi=qi, mq=mq: V.reduce_max(
                            mq, sq[qi][:, :], AX), hz=True)
                        pb.v(lambda mq=mq, negm=negm: V.tensor_single_scalar(
                            negm, mq, -1.0, OP.mult))
                        for band in range(4):
                            pb.s(lambda qi=qi, band=band, negm=negm:
                                 SC.activation(
                                     qs_a[:, 0:2048],
                                     sq[qi][:, 2048 * band:2048 * (band + 1)],
                                     AF.Exp, bias=negm,
                                     accum_out=qs_b[:, band:band + 1]),
                                 hz=True)
                        pb.v(lambda qi=qi: V.reduce_sum(
                            sums[:, qi:qi + 1], qs_b[:, 0:4], AX), hz=True)
                        pb.v(lambda qi=qi: V.reciprocal(
                            sums[:, qi:qi + 1], sums[:, qi:qi + 1]), hz=True)
                        pb.s(lambda qi=qi: SC.activation(
                            sums[:, GE + qi:GE + qi + 1], sums[:, qi:qi + 1],
                            AF.Ln))
                        pb.v(lambda qi=qi, rowA=rowA: V.tensor_scalar(
                            rowA, sums[:, GE + qi:GE + qi + 1], INV_LN2, 64.0,
                            OP.mult, OP.add))
                        pb.v(lambda mq=mq, rowA=rowA: V.scalar_tensor_tensor(
                            rowA, mq, -INV_LN2, rowA, OP.mult, OP.add))
                        pb.v(lambda qi=qi, rowB=rowB: V.tensor_single_scalar(
                            rowB, sums[:, GE + qi:GE + qi + 1], 24.0 * LN2,
                            OP.add))
                    pb.force_hz = False
                    for band in range(4):
                        for qi in range(GE):
                            negm = rowsc[:, 4 + qi:5 + qi]
                            rowA = rowsc[:, 8 + qi:9 + qi]
                            rowB = rowsc[:, 12 + qi:13 + qi]
                            sqs = sq[qi][:, 2048 * band:2048 * (band + 1)]
                            a, b2 = qs_a[:, :], qs_b[:, :]
                            c, d, ii2 = qs_c[:, :], qs_d[:, :], qs_i[:, :]
                            pb.s(lambda sqs=sqs, negm=negm: SC.activation(
                                zfb[:], sqs, AF.Exp, bias=negm))
                            pb.v(lambda sqs=sqs, rowA=rowA, a=a: V.tensor_scalar(
                                a, sqs, INV_LN2, rowA, OP.mult, OP.add))
                            pb.s(lambda a=a, d=d: SC.activation(
                                d, a, AF.Relu, bias=consts[:, C_N505:C_N505 + 1]))
                            pb.v(lambda d=d, ii2=ii2: V.tensor_copy(ii2, d))
                            pb.v(lambda b2=b2, ii2=ii2: V.tensor_copy(b2, ii2))
                            pb.v(lambda a=a, b2=b2, d=d: V.scalar_tensor_tensor(
                                a, b2, -1.0, d, OP.mult, OP.add))
                            pb.v(lambda a=a: V.tensor_single_scalar(
                                a, a, 0.5, OP.is_ge))
                            pb.v(lambda a=a, b2=b2: V.scalar_tensor_tensor(
                                b2, a, 1.0, b2, OP.mult, OP.add))
                            pb.s(lambda a=a, b2=b2, rowB=rowB: SC.activation(
                                a, b2, AF.Exp, scale=-LN2, bias=rowB))
                            pb.s(lambda c=c, b2=b2: SC.activation(
                                c, b2, AF.Exp, scale=LN2,
                                bias=consts[:, C_M24:C_M24 + 1]))
                            pb.v(lambda a=a, b2=b2: V.tensor_mul(b2, zfb[:], a))
                            pb.v(lambda b2=b2, d=d: V.tensor_single_scalar(
                                d, b2, -0.5, OP.add))
                            pb.v(lambda d=d, ii2=ii2: V.tensor_copy(ii2, d))
                            pb.v(lambda d=d, ii2=ii2: V.tensor_copy(d, ii2))
                            pb.v(lambda a=a, b2=b2, d=d: V.scalar_tensor_tensor(
                                a, d, -1.0, b2, OP.mult, OP.add))
                            pb.v(lambda a=a: V.tensor_single_scalar(
                                a, a, 0.5, OP.is_ge))
                            pb.v(lambda a=a, d=d: V.scalar_tensor_tensor(
                                d, a, 1.0, d, OP.mult, OP.add))
                            pb.v(lambda d=d, c=c, qi=qi: V.tensor_mul(
                                pfb[qi][:], d, c))
                        for sub in range(4):
                            trange = 512 * (4 * band + sub)
                            pb.dma(lambda trange=trange: nc.sync.dma_start(
                                out=vcv,
                                in_=v_s[trange:trange + 512, :].rearrange(
                                    "(tt p) h -> p tt h", p=128)))
                            for qi in range(GE):
                                for tt_ in range(4):
                                    tglob = 4 * (4 * band + sub) + tt_
                                    pb.t(lambda qi=qi, sub=sub, tt_=tt_:
                                         T.transpose(
                                             psTT2[:],
                                             pfb[qi][:, 512 * sub + 128 * tt_:
                                                     512 * sub + 128 * (tt_ + 1)],
                                             identH[:]))
                                    pb.v(lambda: V.tensor_copy(pTt[:], psTT2[:]))
                                    for hh in range(2):
                                        pb.t(lambda qi=qi, tt_=tt_, hh=hh,
                                             tglob=tglob: T.matmul(
                                                 psO[qi][:, 512 * hh:
                                                         512 * (hh + 1)],
                                                 pTt[:],
                                                 vc[:, H * tt_ + 512 * hh:
                                                    H * tt_ + 512 * (hh + 1)],
                                                 start=(tglob == 0),
                                                 stop=(tglob == 63)))
                    for qi in range(GE):
                        q = GE * grp + qi
                        pb.v(lambda qi=qi: V.tensor_copy(oeb[:], psO[qi][:]))
                        pb.dma(lambda q=q: nc.sync.dma_start(
                            out=attn_s[128 * q:128 * (q + 1), :], in_=oeb[:]))

        if "E" in PHASES:
            _phE()
        def _phF():
            with ExitStack() as sF:
                xdT_hi = sb("xdT_hi", [128, HT * EPAD], BF16, stack=sF)
                xdT_lo = sb("xdT_lo", [128, HT * EPAD], BF16, stack=sF)
                yT_hi = sb("yT_hi", [128, FT * EPAD], BF16, stack=sF)
                yT_lo = sb("yT_lo", [128, FT * EPAD], BF16, stack=sF)
                xtile = sb("xtileF", [128, H], stack=sF)
                xntile = sb("xntileF", [128, H], stack=sF)
                f32c = sb("f32cF", [128, 128], stack=sF)
                wcf = sb("wcf", [128, HT * 128], stack=sF)
                w1h = sb("w1h", [128, HT * 128], BF16, stack=sF)
                w1l = sb("w1l", [128, HT * 128], BF16, stack=sF)
                w3h = sb("w3h", [128, HT * 128], BF16, stack=sF)
                w3l = sb("w3l", [128, HT * 128], BF16, stack=sF)
                silb = sb("silb", [128, EPAD], stack=sF)
                h3b = sb("h3b", [128, EPAD], stack=sF)
                w2h = sb("w2h", [128, 512], BF16, stack=sF)
                w2l = sb("w2l", [128, 512], BF16, stack=sF)
                oeb = sb("oebF", [128, 512], stack=sF)

                for e in range(E):
                    pb.force_hz = True
                    for j in range(EPT):
                        pb.v(lambda: V.memset(xtile[:], 0.0))
                        pb.gdma(lambda e=e, j=j: G.indirect_dma_start(
                            xtile[:], None, xb[:, :],
                            bass.IndirectOffsetOnAxis(
                                didxS[:, EPT * e + j:EPT * e + j + 1], 0)))
                        layernorm(xtile[:], xntile[:])
                        for k in range(HT):
                            hi = xdT_hi[:, EPAD * k + 128 * j:
                                        EPAD * k + 128 * (j + 1)]
                            lo = xdT_lo[:, EPAD * k + 128 * j:
                                        EPAD * k + 128 * (j + 1)]
                            pb.t(lambda k=k: T.transpose(
                                psT[:], xntile[:, 128 * k:128 * (k + 1)],
                                ident[:]))
                            pb.v(lambda: V.tensor_copy(f32c[:], psT[:]))
                            pb.v(lambda hi=hi: V.tensor_copy(hi, f32c[:]))
                            pb.v(lambda hi=hi, lo=lo: V.scalar_tensor_tensor(
                                lo, hi, -1.0, f32c[:], OP.mult, OP.add))
                    pb.force_hz = False
                    with ExitStack() as sFh:
                        for ftile in range(FT):
                            for (wmat, whi, wlo) in ((w1v, w1h, w1l),
                                                     (w3v, w3h, w3l)):
                                pb.dma(lambda wmat=wmat, e=e, ftile=ftile:
                                       nc.sync.dma_start(
                                           out=wcf.rearrange(
                                               "p (ht f) -> p ht f", f=128),
                                           in_=wmat[H * e:H * (e + 1),
                                                    128 * ftile:128 * (ftile + 1)]
                                           .rearrange("(ht p) f -> p ht f",
                                                      p=128)))
                                pb.v(lambda whi=whi: V.tensor_copy(
                                    whi[:], wcf[:]))
                                pb.v(lambda whi=whi, wlo=wlo:
                                     V.scalar_tensor_tensor(
                                         wlo[:], whi[:], -1.0, wcf[:],
                                         OP.mult, OP.add))
                            for (whi, wlo, issil) in ((w1h, w1l, True),
                                                      (w3h, w3l, False)):
                                for (s0, sw, ps) in ((0, 512, psH),
                                                     (512, 128, psHb)):
                                    nmm = 0
                                    for ht in range(HT):
                                        for (wa, xa) in ((whi, xdT_hi),
                                                         (whi, xdT_lo),
                                                         (wlo, xdT_hi)):
                                            pb.t(lambda wa=wa, xa=xa, ht=ht,
                                                 s0=s0, sw=sw, ps=ps, nmm=nmm:
                                                 T.matmul(
                                                     ps[:, 0:sw],
                                                     wa[:, 128 * ht:
                                                        128 * (ht + 1)],
                                                     xa[:, EPAD * ht + s0:
                                                        EPAD * ht + s0 + sw],
                                                     start=(nmm == 0),
                                                     stop=(nmm == 23)))
                                            nmm += 1
                                if issil:
                                    pb.s(lambda psH=psH: SC.activation(
                                        silb[:, 0:512], psH[:], AF.Silu))
                                    pb.s(lambda psHb=psHb: SC.activation(
                                        silb[:, 512:640], psHb[:], AF.Silu))
                                else:
                                    pb.v(lambda psH=psH: V.tensor_copy(
                                        h3b[:, 0:512], psH[:]))
                                    pb.v(lambda psHb=psHb: V.tensor_copy(
                                        h3b[:, 512:640], psHb[:]))
                            pb.v(lambda: V.tensor_mul(silb[:], silb[:], h3b[:]))
                            yh = yT_hi[:, EPAD * ftile:EPAD * (ftile + 1)]
                            yl = yT_lo[:, EPAD * ftile:EPAD * (ftile + 1)]
                            pb.v(lambda yh=yh: V.tensor_copy(yh, silb[:]))
                            pb.v(lambda yh=yh, yl=yl: V.scalar_tensor_tensor(
                                yl, yh, -1.0, silb[:], OP.mult, OP.add))
                    with ExitStack() as sFw:
                        for hh in range(2):
                            for ft in range(FT):
                                pb.dma(lambda e=e, ft=ft, hh=hh:
                                       nc.sync.dma_start(
                                           out=wcf[:, 0:512],
                                           in_=w2v[FF * e + 128 * ft:
                                                   FF * e + 128 * (ft + 1),
                                                   512 * hh:512 * (hh + 1)]))
                                pb.v(lambda: V.tensor_copy(w2h[:], wcf[:, 0:512]))
                                pb.v(lambda: V.scalar_tensor_tensor(
                                    w2l[:], w2h[:], -1.0, wcf[:, 0:512],
                                    OP.mult, OP.add))
                                for j in range(EPT):
                                    for nmm3, (ya, wa) in enumerate(
                                            ((yT_hi, w2h), (yT_hi, w2l),
                                             (yT_lo, w2h))):
                                        pb.t(lambda ya=ya, wa=wa, ft=ft, j=j,
                                             nmm3=nmm3: T.matmul(
                                                 psO2[j][:],
                                                 ya[:, EPAD * ft + 128 * j:
                                                    EPAD * ft + 128 * (j + 1)],
                                                 wa[:],
                                                 start=(ft == 0 and nmm3 == 0),
                                                 stop=(ft == FT - 1
                                                       and nmm3 == 2)))
                            for j in range(EPT):
                                pb.v(lambda j=j, psO2=psO2: V.tensor_copy(
                                    oeb[:], psO2[j][:]))
                                pb.dma(lambda e=e, j=j, hh=hh: nc.sync.dma_start(
                                    out=moe_s[EPAD * e + 128 * j:
                                              EPAD * e + 128 * (j + 1),
                                              512 * hh:512 * (hh + 1)],
                                    in_=oeb[:]))

        if "F" in PHASES:
            _phF()
        def _phG():
            with ExitStack() as sG:
                r1b = sb("r1b", [128, H], stack=sG)
                r2b = sb("r2b", [128, H], stack=sG)
                xtile = sb("xtileG", [128, H], stack=sG)
                atile = sb("atileG", [128, H], stack=sG)
                ob = sb("ob", [128, H], stack=sG)

                pb.force_hz = True
                for t in range(QT):
                    pb.v(lambda: V.memset(r1b[:], 0.0))
                    pb.v(lambda: V.memset(r2b[:], 0.0))
                    if G_GATHERS:
                        pb.gdma(lambda t=t: G.indirect_dma_start(
                            r1b[:], None, moe_s[:, :],
                            bass.IndirectOffsetOnAxis(sidxG1[:, t:t + 1], 0)))
                        pb.gdma(lambda t=t: G.indirect_dma_start(
                            r2b[:], None, moe_s[:, :],
                            bass.IndirectOffsetOnAxis(sidxG2[:, t:t + 1], 0)))
                    pb.dma(lambda t=t: nc.sync.dma_start(
                        out=xtile[:], in_=xq[128 * t:128 * (t + 1), :]))
                    pb.dma(lambda t=t: nc.sync.dma_start(
                        out=atile[:], in_=attn_s[128 * t:128 * (t + 1), :]))
                    pb.v(lambda: V.tensor_add(ob[:], xtile[:], atile[:]))
                    pb.v(lambda t=t: V.scalar_tensor_tensor(
                        ob[:], r1b[:], pmS[:, t:t + 1], ob[:], OP.mult, OP.add))
                    pb.v(lambda t=t: V.scalar_tensor_tensor(
                        ob[:], r2b[:], pmS[:, QT + t:QT + t + 1], ob[:],
                        OP.mult, OP.add))
                    pb.dma(lambda t=t: nc.sync.dma_start(
                        out=out[128 * t:128 * (t + 1), :], in_=ob[:]))

        if "G" in PHASES:
            _phG()
        pb.finish(block)
    return nc


_NC_CACHE = []


def kernel(L0_attn_x, L0_attn_Wq, L0_attn_bq, L0_attn_Wk, L0_attn_bk,
           L0_attn_Wv, L0_attn_bv, L0_ffn_gate_w, L0_ffn_w1, L0_ffn_w2,
           L0_ffn_w3):
    x = np.ascontiguousarray(np.asarray(L0_attn_x, dtype=np.float32))
    shared = {
        "wq": np.ascontiguousarray(L0_attn_Wq, dtype=np.float32),
        "wk": np.ascontiguousarray(L0_attn_Wk, dtype=np.float32),
        "wv": np.ascontiguousarray(L0_attn_Wv, dtype=np.float32),
        "bq": np.ascontiguousarray(L0_attn_bq, dtype=np.float32),
        "bk": np.ascontiguousarray(L0_attn_bk, dtype=np.float32),
        "bv": np.ascontiguousarray(L0_attn_bv, dtype=np.float32),
        "gw": np.ascontiguousarray(L0_ffn_gate_w, dtype=np.float32),
        "w1": np.ascontiguousarray(L0_ffn_w1, dtype=np.float32),
        "w2": np.ascontiguousarray(L0_ffn_w2, dtype=np.float32),
        "w3": np.ascontiguousarray(L0_ffn_w3, dtype=np.float32),
    }
    in_maps = []
    for c in range(8):
        b, q0 = c // 4, SH * (c % 4)
        tokids = np.ascontiguousarray(
            (q0 + np.arange(SH, dtype=np.int32)).reshape(QT, 128).T)
        m = dict(shared)
        m["xb"] = x[b]
        m["xq"] = np.ascontiguousarray(x[b, q0:q0 + SH])
        m["tok"] = tokids
        in_maps.append(m)

    if not _NC_CACHE:
        _NC_CACHE.append(build_program())
    nc = _NC_CACHE[0]
    res = run_bass_kernel_spmd(nc, in_maps, core_ids=list(range(8)))
    full = np.empty((B, S, H), dtype=np.float32)
    for c in range(8):
        b, q0 = c // 4, SH * (c % 4)
        full[b, q0:q0 + SH] = np.asarray(res.results[c]["out"])
    return full


# version 06 (final)
# speedup vs baseline: 2.2319x; 1.5880x over previous
import numpy as np
from contextlib import ExitStack

import concourse.bass as bass
import concourse.mybir as mybir
from concourse.bass_utils import run_bass_kernel_spmd

F32 = mybir.dt.float32
F16 = mybir.dt.float16
BF16 = mybir.dt.bfloat16
I32 = mybir.dt.int32
AF = mybir.ActivationFunctionType
OP = mybir.AluOpType
AX = mybir.AxisListType.X

B, S, H, FF, E = 2, 8192, 1024, 3584, 8
CAP = 2560
SH = 2048
NT = S // 128
NG = NT // 4
QT = SH // 128
EPAD = 640
EPT = EPAD // 128
HT = H // 128
FT = FF // 128
LN2 = float(np.log(2.0))
INV_LN2 = float(1.0 / np.log(2.0))
C1 = 0.5 * INV_LN2
BIGF = float(1 << 16)
PHASES = set("ABCDEFG")
G_GATHERS = True
DEBUG_OUTS = False


class PB:

    ENG = ("sync", "tensor", "vector", "scalar", "gpsimd")

    def __init__(self, nc, sem, gsem):
        self.force_hz = True
        self.nc = nc
        self.sem = sem
        self.gsem = gsem
        self.scnt = {}
        self.pending = []
        self.cnt = 0
        self.gcnt = 0
        self.run_eng = None
        self.run_ops = []
        self.run_inc = 0
        self.run_wait = None
        self.run_extra = []
        self.ops = {e: [] for e in self.ENG}

    def _flush(self):
        if not self.run_ops:
            return
        ops = self.run_ops
        inc = self.run_inc
        wait_at, gwait_at = self.run_wait
        extra = self.run_extra
        sem, gsem = self.sem, self.gsem
        last = len(ops) - 1

        def closure(engine, ops=ops, inc=inc, wait_at=wait_at,
                    gwait_at=gwait_at, last=last, extra=extra):
            for i, fn in enumerate(ops):
                if i == 0:
                    for es, ev in extra:
                        engine.wait_ge(es, ev)
                    if gwait_at:
                        engine.wait_ge(gsem, gwait_at)
                    if wait_at:
                        engine.wait_ge(sem, wait_at)
                inst = fn()
                if i == last:
                    inst.then_inc(sem, inc)

        self.ops[self.run_eng].append(closure)
        self.run_ops = []
        self.run_inc = 0
        self.run_eng = None
        self.run_wait = None
        self.run_extra = []

    def _emit(self, eng, fn, hazard=False):
        if eng != self.run_eng or self.pending:
            self._flush()
            self.run_eng = eng
            self.run_wait = (self.cnt, self.gcnt)
            self.run_extra = self.pending
            self.pending = []
        self.run_ops.append(fn)
        self.run_inc += 1
        self.cnt += 1
        if hazard or self.force_hz:
            self._flush()

    def _emit_dma(self, eng, fn, ginc, stream=None):
        self._flush()
        wait_at, gwait_at = self.cnt, self.gcnt
        extra = self.pending
        self.pending = []
        sem, gsem = self.sem, self.gsem
        tsem = stream if stream is not None else (gsem if ginc else sem)

        def closure(engine, fn=fn, wait_at=wait_at, gwait_at=gwait_at,
                    tsem=tsem, extra=extra):
            for es, ev in extra:
                engine.wait_ge(es, ev)
            if gwait_at:
                engine.wait_ge(gsem, gwait_at)
            if wait_at:
                engine.wait_ge(sem, wait_at)
            fn().then_inc(tsem, 16)

        self.ops[eng].append(closure)
        if stream is not None:
            self.scnt[stream] = self.scnt.get(stream, 0) + 16
        elif ginc:
            self.gcnt += 16
        else:
            self.cnt += 16

    def sdma(self, stream, fn, eng="sync"):
        self._emit_dma(eng, fn, 0, stream=stream)

    def swait(self, stream):
        v = self.scnt.get(stream, 0)
        if v:
            self.pending.append((stream, v))

    def swait_at(self, stream, val):
        if val:
            self.pending.append((stream, val))

    def t(self, fn, hz=False):
        self._emit("tensor", fn, hz)

    def v(self, fn, hz=False):
        self._emit("vector", fn, hz)

    def s(self, fn, hz=False):
        self._emit("scalar", fn, hz)

    def g(self, fn, hz=False):
        self._emit("gpsimd", fn, hz)

    def dma(self, fn):
        self._emit_dma("sync", fn, 0)

    def gdma(self, fn):
        self._emit_dma("gpsimd", fn, 1)

    def finish_waits(self):
        return list(self.scnt.items())

    def finish(self, block):
        self._flush()
        total = self.cnt
        gtotal = self.gcnt
        drains = self.finish_waits()
        sem, gsem = self.sem, self.gsem
        for name in self.ENG:
            ops = self.ops[name]

            def run(engine, ops=ops, name=name):
                for op in ops:
                    op(engine)
                if name == "sync":
                    if gtotal:
                        engine.wait_ge(gsem, gtotal)
                    engine.wait_ge(sem, total)
                    for ds, dv in drains:
                        engine.wait_ge(ds, dv)

            getattr(block, name)(run)


def build_program():
    nc = bass.Bass("TRN2", target_bir_lowering=False,
                   detect_race_conditions=False)

    xb = nc.dram_tensor("xb", [S, H], F32, kind="ExternalInput").ap()
    xq = nc.dram_tensor("xq", [SH, H], F32, kind="ExternalInput").ap()
    tok = nc.dram_tensor("tok", [128, QT], I32, kind="ExternalInput").ap()
    wqd = nc.dram_tensor("wq", [H, H], F32, kind="ExternalInput").ap()
    wkd = nc.dram_tensor("wk", [H, H], F32, kind="ExternalInput").ap()
    wvd = nc.dram_tensor("wv", [H, H], F32, kind="ExternalInput").ap()
    bqv = nc.dram_tensor("bq", [H], F32, kind="ExternalInput").ap()
    bkv = nc.dram_tensor("bk", [H], F32, kind="ExternalInput").ap()
    bvv = nc.dram_tensor("bv", [H], F32, kind="ExternalInput").ap()
    gw = nc.dram_tensor("gw", [E, H], F32, kind="ExternalInput").ap()
    w1 = nc.dram_tensor("w1", [E, H, FF], F32, kind="ExternalInput").ap()
    w2 = nc.dram_tensor("w2", [E, FF, H], F32, kind="ExternalInput").ap()
    w3 = nc.dram_tensor("w3", [E, H, FF], F32, kind="ExternalInput").ap()
    out = nc.dram_tensor("out", [SH, H], F32, kind="ExternalOutput").ap()

    DK = "ExternalOutput" if DEBUG_OUTS else "Internal"
    kT_s = nc.dram_tensor("kT_s", [H, S], F16, kind=DK).ap()
    v_s = nc.dram_tensor("v_s", [S, H], F16, kind=DK).ap()
    attn_s = nc.dram_tensor("attn_s", [SH, H], F32, kind=DK).ap()
    moe_s = nc.dram_tensor("moe_s", [E * EPAD, H], F32, kind=DK).ap()
    didx_s = nc.dram_tensor("didx_s", [E * EPAD, 1], I32, kind=DK).ap()
    rec_s = nc.dram_tensor("rec_s", [S, 4], F32, kind=DK).ap()
    maskh = nc.dram_tensor("maskh", [E, 2 * S], F32, kind=DK).ap()
    toth = nc.dram_tensor("toth", [128], F32, kind=DK).ap()
    offh = nc.dram_tensor("offh", [128], F32, kind=DK).ap()
    posh = nc.dram_tensor("posh", [2 * S], F32, kind=DK).ap()

    w1v = w1.rearrange("e h f -> (e h) f")
    w3v = w3.rearrange("e h f -> (e h) f")
    w2v = w2.rearrange("e f h -> (e f) h")

    st = ExitStack()
    with st:
        sem = st.enter_context(nc.semaphore("chain"))
        gsem = st.enter_context(nc.semaphore("gchain"))
        stX = st.enter_context(nc.semaphore("stX"))
        stS = st.enter_context(nc.semaphore("stS"))
        stK = st.enter_context(nc.semaphore("stK"))
        stV = st.enter_context(nc.semaphore("stV"))
        stW = st.enter_context(nc.semaphore("stW"))
        stW2 = st.enter_context(nc.semaphore("stW2"))
        stG = st.enter_context(nc.semaphore("stG"))
        block = st.enter_context(nc.Block())
        pb = PB(nc, sem, gsem)
        V, SC, T, G = nc.vector, nc.scalar, nc.tensor, nc.gpsimd

        def sb(name, shape, dt=F32, stack=None):
            return (stack or st).enter_context(nc.sbuf_tensor(name, shape, dt))

        def psum(name, shape, stack=None, dt=F32):
            return (stack or st).enter_context(
                nc.psum_tensor(name, shape, dt))

        PSF = st.enter_context(nc.psum_tensor("PSF", [128, 3584], F32))
        PS16 = st.enter_context(nc.psum_tensor("PS16", [128, 1024], F16))
        psA = PSF[:, 0:512]
        psT = PSF[:, 3072:3200]
        psT0 = psT
        psR = PSF[0:8, 0:1024]
        psRs = PSF[0:8, 1024:1152]
        psTT = PSF[:, 3200:3208]
        psD = PSF[:, 2048:2560]
        psMT = PSF[0:8, 3072:3200]
        psP = PSF[0:16, 0:1024]
        psB8 = PSF[:, 3200:3208]
        psE = PSF[:, 2048:2560]
        psTT2 = PS16[:, 0:128]
        psO = [PSF[:, 0:1024], PSF[:, 1024:2048]]
        psH = PSF[:, 2560:3072]
        psHb = PSF[:, 3072:3200]
        psO2 = [PSF[:, 0:512], PSF[:, 512:1024], PSF[:, 1024:1536],
                PSF[:, 1536:2048], PSF[:, 2048:2560]]

        ident = sb("ident", [128, 128])
        identH = sb("identH", [128, 128], F16)
        identI = sb("identI", [128, 128], I32)
        ones1 = sb("ones1", [1, 128])
        consts = sb("consts", [128, 8])
        C_EPS, C_135, C_N505, C_SM, C_SP, C_LNEPS, C_M24 = range(7)
        biasq = sb("biasq", [128, HT])
        biask = sb("biask", [128, HT])
        bvB = sb("bvB", [128, H])
        gwT1 = sb("gwT1", [128, HT * E], BF16)
        gwT2 = sb("gwT2", [128, HT * E], BF16)
        tokS = sb("tokS", [128, QT], I32)
        didxS = sb("didxS", [128, E * EPT], I32)
        sidxI1 = sb("sidxI1", [128, QT], I32)
        sidxI2 = sb("sidxI2", [128, QT], I32)
        sidxG1 = sb("sidxG1", [128, QT], I32)
        sidxG2 = sb("sidxG2", [128, QT], I32)
        pmS = sb("pmS", [128, 2 * QT])
        rowsc = sb("rowsc", [128, 16])

        PW = 2048
        qs_a = sb("qs_a", [128, PW])
        qs_b = sb("qs_b", [128, PW])
        qs_c = sb("qs_c", [128, PW])
        qs_d = sb("qs_d", [128, PW])
        qs_i = sb("qs_i", [128, PW], I32)

        sBC = ExitStack()
        logP = sb("logP", [128, NT * 72], stack=sBC)
        oh1 = sb("oh1", [128, NT * E], stack=sBC)
        oh2 = sb("oh2", [128, NT * E], stack=sBC)
        lgb = sb("lgb", [128, NT * E], stack=sBC)
        sA5 = sb("sA5", [128, NT * E], stack=sBC)
        sB5 = sb("sB5", [128, NT * E], stack=sBC)
        iotaE = sb("iotaE", [128, NT * E], stack=sBC)
        m1 = sb("m1", [128, NT], stack=sBC)
        m2 = sb("m2", [128, NT], stack=sBC)
        i1f = sb("i1f", [128, NT], stack=sBC)
        i2f = sb("i2f", [128, NT], stack=sBC)
        p1x = sb("p1x", [128, NT], stack=sBC)
        p2x = sb("p2x", [128, NT], stack=sBC)
        sA6 = sb("sA6", [128, NT], stack=sBC)
        sB6 = sb("sB6", [128, NT], stack=sBC)

        def e3(ap):
            return ap.rearrange("p (t e) -> p t e", e=8)

        def quantize(src, dst, width):
            prev = pb.force_hz
            pb.force_hz = False
            a = qs_a[:, 0:width]
            b2 = qs_b[:, 0:width]
            c = qs_c[:, 0:width]
            d = qs_d[:, 0:width]
            ii = qs_i[:, 0:width]
            pb.s(lambda: SC.activation(a, src, AF.Square))
            pb.s(lambda: SC.activation(
                b2, a, AF.Ln, bias=consts[:, C_EPS:C_EPS + 1]))
            pb.s(lambda: SC.activation(
                d, b2, AF.Relu, scale=C1, bias=consts[:, C_135:C_135 + 1]))
            pb.v(lambda: V.tensor_copy(ii, d))
            pb.v(lambda: V.tensor_copy(a, ii))
            pb.v(lambda: V.scalar_tensor_tensor(
                b2, a, -1.0, d, OP.mult, OP.add))
            pb.v(lambda: V.tensor_single_scalar(b2, b2, 0.5, OP.is_ge))
            pb.v(lambda: V.scalar_tensor_tensor(
                a, b2, 1.0, a, OP.mult, OP.add))
            pb.s(lambda: SC.activation(
                b2, a, AF.Exp, scale=-LN2, bias=consts[:, C_SM:C_SM + 1]))
            pb.s(lambda: SC.activation(
                c, a, AF.Exp, scale=LN2, bias=consts[:, C_SP:C_SP + 1]))
            pb.v(lambda: V.tensor_mul(a, src, b2))
            pb.v(lambda: V.tensor_single_scalar(d, a, 4095.5, OP.add))
            pb.v(lambda: V.tensor_copy(ii, d))
            pb.v(lambda: V.tensor_copy(d, ii))
            pb.v(lambda: V.scalar_tensor_tensor(
                b2, d, -1.0, a, OP.mult, OP.add))
            pb.v(lambda: V.tensor_single_scalar(
                b2, b2, -4095.5, OP.is_ge))
            pb.v(lambda: V.scalar_tensor_tensor(
                d, b2, 1.0, d, OP.mult, OP.add))
            pb.v(lambda: V.tensor_single_scalar(d, d, -4096.0, OP.add))
            pb.v(lambda: V.tensor_mul(dst, d, c))
            pb.force_hz = prev

        def layernorm(xt, xnt):
            prev_hz = pb.force_hz
            pb.force_hz = True
            rs = rowsc[:, 0:1]
            ss = rowsc[:, 1:2]
            pb.v(lambda: V.reduce_sum(rs, xt, AX), hz=True)
            pb.v(lambda: V.tensor_single_scalar(rs, rs, 1.0 / H, OP.mult))
            pb.v(lambda: V.tensor_scalar(xnt, xt, rs, None, OP.subtract))
            pb.s(lambda: SC.activation(
                qs_a[:, 0:H], xnt, AF.Square, accum_out=ss), hz=True)
            pb.s(lambda: SC.activation(
                ss, ss, AF.Sqrt, scale=1.0 / H,
                bias=consts[:, C_LNEPS:C_LNEPS + 1]))
            pb.v(lambda: V.reciprocal(ss, ss), hz=True)
            pb.v(lambda: V.tensor_scalar(xnt, xnt, ss, None, OP.mult))
            pb.force_hz = prev_hz

        def ln_transpose_split(xtile, xntile, slc):
            layernorm(xtile, xntile[:])
            prev = pb.force_hz
            pb.force_hz = False
            for k in range(HT):
                pb.t(lambda k=k: T.transpose(
                    PSF[:, 128 * k:128 * (k + 1)],
                    xntile[:, 128 * k:128 * (k + 1)], ident[:]))
            for k in range(HT):
                hi, lo = slc(k)
                ps = PSF[:, 128 * k:128 * (k + 1)]
                pb.v(lambda hi=hi, ps=ps: V.tensor_copy(hi, ps))
                pb.v(lambda hi=hi, lo=lo, ps=ps: V.scalar_tensor_tensor(
                    lo, hi, -1.0, ps, OP.mult, OP.add))
            pb.force_hz = prev

        def prep_wT(wsrc, hi, lo, wst):
            pb.dma(lambda: nc.sync.dma_start(
                out=wst.rearrange("p (ot h) -> p ot h", h=H),
                in_=wsrc.rearrange("(ot p) h -> p ot h", p=128)))
            prev = pb.force_hz
            pb.force_hz = False
            for ot in range(HT):
                for ht in range(HT):
                    pb.t(lambda ot=ot, ht=ht: T.transpose(
                        PSF[:, 128 * ht:128 * (ht + 1)],
                        wst[:, 1024 * ot + 128 * ht:
                            1024 * ot + 128 * (ht + 1)], ident[:]))
                for ht in range(HT):
                    ps = PSF[:, 128 * ht:128 * (ht + 1)]
                    dh = hi[:, 1024 * ht + 128 * ot:1024 * ht + 128 * (ot + 1)]
                    dl = lo[:, 1024 * ht + 128 * ot:1024 * ht + 128 * (ot + 1)]
                    pb.v(lambda dh=dh, ps=ps: V.tensor_copy(dh, ps))
                    pb.v(lambda dh=dh, dl=dl, ps=ps: V.scalar_tensor_tensor(
                        dl, dh, -1.0, ps, OP.mult, OP.add))
            pb.force_hz = prev

        pb.g(lambda: G.iota(identI[:], [[-1, 128]], base=0,
                            channel_multiplier=1))
        pb.v(lambda: V.tensor_copy(ident[:], identI[:]))
        pb.v(lambda: V.tensor_single_scalar(
            ident[:], ident[:], 0.0, OP.is_equal))
        pb.v(lambda: V.tensor_copy(identH[:], ident[:]))
        pb.v(lambda: V.memset(ones1[:], 1.0))
        for idx, val in [(C_EPS, 2.0 ** -40), (C_135, 13.5), (C_N505, -50.5),
                         (C_SM, 24.0 * LN2), (C_SP, -24.0 * LN2),
                         (C_LNEPS, 1e-5), (C_M24, -24.0 * LN2)]:
            pb.v(lambda idx=idx, val=val: V.memset(
                consts[:, idx:idx + 1], float(val)))
        pb.dma(lambda: nc.sync.dma_start(
            out=biasq[:], in_=bqv.rearrange("(j p) -> p j", p=128),
            allow_slow_non_contiguous=True))
        pb.dma(lambda: nc.sync.dma_start(
            out=biask[:], in_=bkv.rearrange("(j p) -> p j", p=128),
            allow_slow_non_contiguous=True))
        pb.dma(lambda: nc.sync.dma_start(out=tokS[:], in_=tok[:]))
        pb.g(lambda: G.iota(qs_i[:, 0:NT * E], [[0, NT], [1, E]],
                            base=0, channel_multiplier=0))
        pb.v(lambda: V.tensor_copy(iotaE[:], qs_i[:, 0:NT * E]))

        with ExitStack() as sA:
            bvrow = sb("bvrow", [1, H], stack=sA)
            gws = sb("gws", [8, H], stack=sA)
            gtf = sb("gtf", [128, 8], stack=sA)
            pb.dma(lambda: nc.sync.dma_start(
                out=bvrow[:], in_=bvv.rearrange("(o h) -> o h", o=1)))
            for half in range(2):
                pb.t(lambda half=half: T.matmul(
                    psA[:], ones1[:], bvrow[:, 512 * half:512 * (half + 1)],
                    start=True, stop=True))
                pb.v(lambda half=half: V.tensor_copy(
                    bvB[:, 512 * half:512 * (half + 1)], psA[:]))
            pb.dma(lambda: nc.sync.dma_start(out=gws[:], in_=gw[:]))
            for k in range(HT):
                pb.t(lambda k=k: T.transpose(
                    psA[:, 0:8], gws[:, 128 * k:128 * (k + 1)],
                    ident[0:8, 0:8]))
                pb.v(lambda: V.tensor_copy(gtf[:], psA[:, 0:8]))
                pb.v(lambda k=k: V.tensor_copy(
                    gwT1[:, 8 * k:8 * (k + 1)], gtf[:]))
                pb.v(lambda k=k: V.scalar_tensor_tensor(
                    gwT2[:, 8 * k:8 * (k + 1)], gwT1[:, 8 * k:8 * (k + 1)],
                    -1.0, gtf[:], OP.mult, OP.add))

        def _phB():
            with ExitStack() as sBo:
                wkT_hi = sb("wkT_hi", [128, H * HT], BF16, stack=sBo)
                wkT_lo = sb("wkT_lo", [128, H * HT], BF16, stack=sBo)
                wvT_hi = sb("wvT_hi", [128, H * HT], BF16, stack=sBo)
                wvT_lo = sb("wvT_lo", [128, H * HT], BF16, stack=sBo)
                with ExitStack() as sBw:
                    wst = sb("wst", [128, H * HT], stack=sBw)
                    prep_wT(wkd, wkT_hi, wkT_lo, wst)
                    prep_wT(wvd, wvT_hi, wvT_lo, wst)
                xcT_hi = sb("xcT_hi", [128, HT * 512], BF16, stack=sBo)
                xcT_lo = sb("xcT_lo", [128, HT * 512], BF16, stack=sBo)
                xtl = sb("xtile", [128, 2 * H], stack=sBo)
                xts = [xtl[:, 0:H], xtl[:, H:2 * H]]
                xntile = sb("xntile", [128, H], stack=sBo)
                evb = sb("evb", [128, 2048], stack=sBo)
                hfb = sb("hf16", [128, 2 * 2048], F16, stack=sBo)
                hfs = [hfb[:, 0:2048], hfb[:, 2048:4096]]
                rsc = sb("rsc", [8, 1152], stack=sBo)
                xclaim = [0, 0]
                hclaim = [0, 0]
                kvp = 0

                def xload(i):
                    pb.sdma(stX, lambda i=i: nc.sync.dma_start(
                        out=xts[i % 2],
                        in_=xb[128 * i:128 * (i + 1), :]))
                    xclaim[i % 2] = pb.scnt[stX]

                xload(0)
                for g in range(NG):
                    pb.force_hz = True
                    for ii in range(4):
                        i = 4 * g + ii
                        if i + 1 < 4 * NG:
                            xload(i + 1)
                        pb.swait_at(stX, xclaim[i % 2])
                        ln_transpose_split(
                            xts[i % 2], xntile,
                            lambda k, ii=ii: (
                                xcT_hi[:, 512 * k + 128 * ii:
                                       512 * k + 128 * (ii + 1)],
                                xcT_lo[:, 512 * k + 128 * ii:
                                       512 * k + 128 * (ii + 1)]))
                        pb.force_hz = False
                        for k in range(HT):
                            pb.t(lambda k=k, ii=ii: T.matmul(
                                psR[:, 128 * k:128 * (k + 1)],
                                gwT1[:, 8 * k:8 * (k + 1)],
                                xcT_hi[:, 512 * k + 128 * ii:
                                       512 * k + 128 * (ii + 1)],
                                start=True, stop=True))
                        nsm = 0
                        for k in range(HT):
                            for (ga, xa) in ((gwT2, xcT_hi), (gwT1, xcT_lo),
                                             (gwT2, xcT_lo)):
                                pb.t(lambda k=k, ii=ii, ga=ga, xa=xa, nsm=nsm:
                                     T.matmul(
                                         psRs[:], ga[:, 8 * k:8 * (k + 1)],
                                         xa[:, 512 * k + 128 * ii:
                                            512 * k + 128 * (ii + 1)],
                                         start=(nsm == 0), stop=(nsm == 23)))
                                nsm += 1
                        pb.v(lambda: V.tensor_copy(rsc[:, 0:1024], psR[:]))
                        pb.v(lambda: V.tensor_copy(rsc[:, 1024:1152], psRs[:]))
                        for pl in range(9):
                            pb.t(lambda pl=pl: T.transpose(
                                PSF[:, 3200 + 8 * pl:3208 + 8 * pl],
                                rsc[:, 128 * pl:128 * (pl + 1)],
                                ident[0:8, 0:8]))
                        for pl in range(9):
                            pb.v(lambda i=i, pl=pl: V.tensor_copy(
                                logP[:, 72 * i + 8 * pl:72 * i + 8 * (pl + 1)],
                                PSF[:, 3200 + 8 * pl:3208 + 8 * pl]))
                        pb.force_hz = True
                    pb.force_hz = False
                    for oth in range(2):
                        for otl in range(4):
                            ot = 4 * oth + otl
                            nmm = 0
                            for ht in range(HT):
                                for (wa, xa) in ((wkT_hi, xcT_hi),
                                                 (wkT_hi, xcT_lo),
                                                 (wkT_lo, xcT_hi)):
                                    pb.t(lambda ot=ot, ht=ht, wa=wa, xa=xa,
                                         nmm=nmm: T.matmul(
                                             psD[:],
                                             wa[:, 1024 * ht + 128 * ot:
                                                1024 * ht + 128 * (ot + 1)],
                                             xa[:, 512 * ht:512 * (ht + 1)],
                                             start=(nmm == 0), stop=(nmm == 23)))
                                    nmm += 1
                            pb.v(lambda ot=ot, otl=otl: V.tensor_scalar(
                                evb[:, 512 * otl:512 * (otl + 1)], psD[:],
                                biask[:, ot:ot + 1], None, OP.add))
                        hb = hfs[kvp % 2]
                        pb.swait_at(stS, hclaim[kvp % 2])
                        quantize(evb[:, 0:2048], hb, 2048)
                        pb.sdma(stS, lambda g=g, oth=oth, hb=hb:
                                nc.sync.dma_start(
                                    out=kT_s.rearrange(
                                        "(ot p) s -> p ot s", p=128)[
                                        :, 4 * oth:4 * (oth + 1),
                                        512 * g:512 * (g + 1)],
                                    in_=hb.rearrange(
                                        "p (ot s) -> p ot s", s=512)))
                        hclaim[kvp % 2] = pb.scnt[stS]
                        kvp += 1
                    for iih in range(2):
                        for iil in range(2):
                            ii = 2 * iih + iil
                            for oc in range(2):
                                nmm = 0
                                for ht in range(HT):
                                    for (xa, wa) in ((xcT_hi, wvT_hi),
                                                     (xcT_hi, wvT_lo),
                                                     (xcT_lo, wvT_hi)):
                                        pb.t(lambda ii=ii, oc=oc, ht=ht, xa=xa,
                                             wa=wa, nmm=nmm: T.matmul(
                                                 psD[:],
                                                 xa[:, 512 * ht + 128 * ii:
                                                    512 * ht + 128 * (ii + 1)],
                                                 wa[:, 1024 * ht + 512 * oc:
                                                    1024 * ht + 512 * (oc + 1)],
                                                 start=(nmm == 0),
                                                 stop=(nmm == 23)))
                                        nmm += 1
                                pb.v(lambda iil=iil, oc=oc: V.tensor_add(
                                    evb[:, 1024 * iil + 512 * oc:
                                        1024 * iil + 512 * (oc + 1)],
                                    psD[:], bvB[:, 512 * oc:512 * (oc + 1)]))
                        hb = hfs[kvp % 2]
                        pb.swait_at(stS, hclaim[kvp % 2])
                        quantize(evb[:, 0:2048], hb, 2048)
                        pb.sdma(stS, lambda g=g, iih=iih, hb=hb:
                                nc.sync.dma_start(
                                    out=v_s[512 * g + 256 * iih:
                                            512 * g + 256 * (iih + 1), :]
                                    .rearrange("(ii p) h -> p ii h", p=128),
                                    in_=hb.rearrange(
                                        "p (ii h) -> p ii h", h=1024)))
                        hclaim[kvp % 2] = pb.scnt[stS]
                        kvp += 1

        if "B" in PHASES:
            _phB()
        def _phC():
            pb.force_hz = True
            with ExitStack() as sC:
                eftS = sb("eftS", [128, NT * E], stack=sC)
                eftC = sb("eftC", [128, NT * E], stack=sC)
                eftX = sb("eftX", [128, NT * E], stack=sC)
                eftT = sb("eftT", [128, NT * E], stack=sC)
                eftZ = sb("eftZ", [128, NT * E], stack=sC)
                mbuf = sb("mbuf", [8, 2048], stack=sC)
                m8 = sb("m8", [8, 16], stack=sC)
                scanL = sb("scanL", [128, 1024], stack=sC)
                cg = sb("cg", [128, 1024], stack=sC)
                cmb = sb("cmb", [128, 1024], stack=sC)
                tot8 = sb("tot8", [8, 16], stack=sC)
                offs8 = sb("offs8", [8, 16], stack=sC)
                offCol = sb("offCol", [128, 1], stack=sC)
                selM = sb("selM", [128, 16], stack=sC)
                pos16 = sb("pos16", [16, 1024], stack=sC)
                posTok = sb("posTok", [128, 128], stack=sC)
                baseB = sb("baseB", [128, 32], stack=sC)
                bRow = sb("bRow", [1, 8], stack=sC)
                recS = sb("recS", [128, NT * 4], stack=sC)
                recAll = sb("recAll", [128, QT * 4], stack=sC)
                initI = sb("initI", [128, E * EPT], I32, stack=sC)
                iotaPf = sb("iotaPf", [128, 1], stack=sC)
                iota16 = sb("iota16", [128, 16], stack=sC)

                logv = logP.rearrange("p (t pl e) -> p t pl e", pl=9, e=8)

                pb.v(lambda: V.tensor_copy(e3(eftS[:, :]), logv[:, :, 0, :]))
                pb.v(lambda: V.memset(eftC[:], 0.0))
                for pl in range(1, 9):
                    pb.v(lambda pl=pl: V.tensor_copy(
                        e3(eftX[:, :]), logv[:, :, pl, :]))
                    pb.v(lambda: V.tensor_add(eftT[:], eftS[:], eftX[:]))
                    pb.v(lambda: V.tensor_sub(eftZ[:], eftT[:], eftS[:]))
                    pb.v(lambda: V.tensor_sub(sA5[:], eftT[:], eftZ[:]))
                    pb.v(lambda: V.tensor_sub(sA5[:], eftS[:], sA5[:]))
                    pb.v(lambda: V.tensor_add(eftC[:], eftC[:], sA5[:]))
                    pb.v(lambda: V.tensor_sub(eftZ[:], eftX[:], eftZ[:]))
                    pb.v(lambda: V.tensor_add(eftC[:], eftC[:], eftZ[:]))
                    pb.v(lambda: V.tensor_copy(eftS[:], eftT[:]))
                pb.v(lambda: V.tensor_add(lgb[:], eftS[:], eftC[:]))

                bc = [128, NT, 8]
                pb.v(lambda: V.reduce_max(m1[:].unsqueeze(2), e3(lgb[:, :]), AX), hz=True)
                pb.v(lambda: V.tensor_tensor(
                    e3(sA5[:, :]), e3(lgb[:, :]),
                    m1[:].unsqueeze(2).broadcast_to(bc), OP.is_ge))
                pb.v(lambda: V.tensor_single_scalar(
                    sB5[:], iotaE[:], -999.0, OP.add))
                pb.v(lambda: V.tensor_mul(sA5[:], sA5[:], sB5[:]))
                pb.v(lambda: V.tensor_single_scalar(sA5[:], sA5[:], 999.0, OP.add))
                pb.v(lambda: V.tensor_reduce(
                    i1f[:].unsqueeze(2), e3(sA5[:, :]), AX, OP.min), hz=True)
                pb.v(lambda: V.tensor_tensor(
                    e3(oh1[:, :]), e3(iotaE[:, :]),
                    i1f[:].unsqueeze(2).broadcast_to(bc), OP.is_equal))
                pb.v(lambda: V.scalar_tensor_tensor(
                    sB5[:], oh1[:], -1e30, lgb[:], OP.mult, OP.add))
                pb.v(lambda: V.reduce_max(m2[:].unsqueeze(2), e3(sB5[:, :]), AX), hz=True)
                pb.v(lambda: V.tensor_tensor(
                    e3(sA5[:, :]), e3(sB5[:, :]),
                    m2[:].unsqueeze(2).broadcast_to(bc), OP.is_ge))
                pb.v(lambda: V.tensor_single_scalar(
                    sB5[:], iotaE[:], -999.0, OP.add))
                pb.v(lambda: V.tensor_mul(sA5[:], sA5[:], sB5[:]))
                pb.v(lambda: V.tensor_single_scalar(sA5[:], sA5[:], 999.0, OP.add))
                pb.v(lambda: V.tensor_reduce(
                    i2f[:].unsqueeze(2), e3(sA5[:, :]), AX, OP.min), hz=True)
                pb.v(lambda: V.tensor_tensor(
                    e3(oh2[:, :]), e3(iotaE[:, :]),
                    i2f[:].unsqueeze(2).broadcast_to(bc), OP.is_equal))
                pb.v(lambda: V.tensor_tensor(
                    e3(sA5[:, :]), e3(lgb[:, :]),
                    m1[:].unsqueeze(2).broadcast_to(bc), OP.subtract))
                pb.s(lambda: SC.activation(sA5[:], sA5[:], AF.Exp))
                pb.v(lambda: V.reduce_sum(sA6[:].unsqueeze(2), e3(sA5[:, :]), AX), hz=True)
                pb.v(lambda: V.reciprocal(sA6[:], sA6[:]), hz=True)
                pb.v(lambda: V.tensor_tensor(
                    e3(sA5[:, :]), e3(sA5[:, :]),
                    sA6[:].unsqueeze(2).broadcast_to(bc), OP.mult))
                pb.v(lambda: V.tensor_mul(sB5[:], sA5[:], oh1[:]))
                pb.v(lambda: V.reduce_sum(p1x[:].unsqueeze(2), e3(sB5[:, :]), AX), hz=True)
                pb.v(lambda: V.tensor_mul(sB5[:], sA5[:], oh2[:]))
                pb.v(lambda: V.reduce_sum(p2x[:].unsqueeze(2), e3(sB5[:, :]), AX), hz=True)

                mb3 = mbuf.rearrange("e (w k) -> e w k", k=2)
                for grp in range(8):
                    for iw in range(8):
                        i = 8 * grp + iw
                        for k, oh in ((0, oh1), (1, oh2)):
                            pb.t(lambda i=i, oh=oh: T.transpose(
                                psMT[:], oh[:, 8 * i:8 * (i + 1)], ident[:]))
                            pb.v(lambda iw=iw, k=k: V.tensor_copy(
                                mb3[:, 128 * iw:128 * (iw + 1), k:k + 1],
                                psMT[:].unsqueeze(2)))
                    pb.dma(lambda grp=grp: nc.sync.dma_start(
                        out=maskh[:, 2048 * grp:2048 * (grp + 1)], in_=mbuf[:]))
                pb.dma(lambda: nc.sync.dma_start(
                    out=cmb[:], in_=maskh.rearrange("e (c w) -> (e c) w", c=16)))
                pb.v(lambda: V.tensor_tensor_scan(
                    scanL[:], cmb[:], cmb[:], 0.0, OP.add, OP.bypass), hz=True)
                pb.dma(lambda: nc.sync.dma_start(
                    out=toth.rearrange("(p o) -> p o", o=1),
                    in_=scanL[:, 1023:1024]))
                pb.dma(lambda: nc.sync.dma_start(
                    out=tot8[:], in_=toth.rearrange("(e c) -> e c", c=16)))
                pb.v(lambda: V.tensor_copy(m8[:], tot8[:]))
                pb.v(lambda: V.tensor_tensor_scan(
                    tot8[:], m8[:], m8[:], 0.0, OP.add, OP.bypass), hz=True)
                pb.v(lambda: V.memset(offs8[:, 0:1], 0.0))
                pb.v(lambda: V.tensor_copy(offs8[:, 1:16], tot8[:, 0:15]))
                pb.dma(lambda: nc.sync.dma_start(
                    out=offh.rearrange("(e c) -> e c", c=16), in_=offs8[:]))
                pb.dma(lambda: nc.sync.dma_start(
                    out=offCol[:], in_=offh.rearrange("(p o) -> p o", o=1)))
                pb.v(lambda: V.tensor_scalar(
                    cg[:], scanL[:], offCol[:, 0:1], None, OP.add))
                pb.v(lambda: V.tensor_mul(cg[:], cg[:], cmb[:]))
                pb.g(lambda: G.iota(qs_i[:, 0:1], [[0, 1]], base=0,
                                    channel_multiplier=1))
                pb.v(lambda: V.tensor_copy(iotaPf[:], qs_i[:, 0:1]))
                pb.g(lambda: G.iota(qs_i[:, 0:16], [[1, 16]], base=0,
                                    channel_multiplier=0))
                pb.v(lambda: V.tensor_copy(iota16[:], qs_i[:, 0:16]))
                t0, t1 = rowsc[:, 8:9], rowsc[:, 9:10]
                pb.v(lambda: V.tensor_scalar(
                    t0, iotaPf[:], 1.0 / 16.0, -0.5, OP.mult, OP.add))
                pb.v(lambda: V.tensor_copy(qs_i[:, 0:1], t0))
                pb.v(lambda: V.tensor_copy(t1, qs_i[:, 0:1]))
                pb.v(lambda: V.tensor_sub(t0, t0, t1))
                pb.v(lambda: V.tensor_single_scalar(t0, t0, 0.5, OP.is_ge))
                pb.v(lambda: V.scalar_tensor_tensor(
                    t1, t0, 1.0, t1, OP.mult, OP.add))
                pb.v(lambda: V.scalar_tensor_tensor(
                    t0, t1, -16.0, iotaPf[:], OP.mult, OP.add))
                pb.v(lambda: V.tensor_scalar(
                    selM[:], iota16[:], t0, None, OP.is_equal))
                for j in range(2):
                    pb.t(lambda j=j: T.matmul(
                        psP[:, 512 * j:512 * (j + 1)], selM[:],
                        cg[:, 512 * j:512 * (j + 1)], start=True, stop=True))
                pb.v(lambda: V.tensor_copy(pos16[:], psP[:]))
                pb.dma(lambda: nc.sync.dma_start(
                    out=posh.rearrange("(c w) -> c w", w=1024), in_=pos16[:]))
                pb.dma(lambda: nc.sync.dma_start(
                    out=posTok.rearrange("p (t k) -> p t k", k=2),
                    in_=posh.rearrange("(t p k) -> p t k", p=128, k=2),
                    allow_slow_non_contiguous=True))
                for q in range(4):
                    pb.t(lambda q=q: T.transpose(
                        psMT[0:1, 0:8], offs8[:, 4 * q:4 * q + 1],
                        ident[0:8, 0:8]))
                    pb.v(lambda: V.tensor_copy(bRow[:], psMT[0:1, 0:8]))
                    pb.t(lambda: T.matmul(
                        psB8[:], ones1[:], bRow[:], start=True, stop=True))
                    pb.v(lambda q=q: V.tensor_copy(
                        baseB[:, 8 * q:8 * (q + 1)], psB8[:]))
                posv = posTok.rearrange("p (t k) -> p t k", k=2)
                recv = recS.rearrange("p (t r) -> p t r", r=4)
                for k, ohk, ikf, pkx in ((0, oh1, i1f, p1x), (1, oh2, i2f, p2x)):
                    for q in range(4):
                        pb.v(lambda q=q, ohk=ohk: V.tensor_tensor(
                            e3(sA5[:, :])[:, 16 * q:16 * (q + 1), :],
                            e3(ohk[:, :])[:, 16 * q:16 * (q + 1), :],
                            baseB[:, 8 * q:8 * (q + 1)].unsqueeze(1)
                            .broadcast_to([128, 16, 8]), OP.mult))
                    pb.v(lambda: V.reduce_sum(
                        sA6[:].unsqueeze(2), e3(sA5[:, :]), AX), hz=True)
                    pb.v(lambda k=k: V.tensor_tensor(
                        sB6[:].unsqueeze(2), posv[:, :, k:k + 1],
                        sA6[:].unsqueeze(2), OP.subtract))
                    pb.v(lambda ikf=ikf: V.tensor_scalar(
                        sA6[:], ikf[:], 640.0, -1.0, OP.mult, OP.add))
                    pb.v(lambda: V.tensor_add(sA6[:], sA6[:], sB6[:]))
                    pb.v(lambda k=k: V.tensor_copy(
                        m2[:].unsqueeze(2), posv[:, :, k:k + 1]))
                    pb.v(lambda: V.tensor_single_scalar(
                        m2[:], m2[:], CAP + 0.5, OP.is_le))
                    pb.v(lambda: V.tensor_single_scalar(
                        sB6[:], sB6[:], EPAD + 0.5, OP.is_le))
                    pb.v(lambda: V.tensor_mul(m2[:], m2[:], sB6[:]))
                    pb.v(lambda: V.tensor_single_scalar(
                        sA6[:], sA6[:], BIGF, OP.add))
                    pb.v(lambda: V.scalar_tensor_tensor(
                        sA6[:], m2[:], -BIGF, sA6[:], OP.mult, OP.add))
                    pb.v(lambda pkx=pkx: V.tensor_mul(pkx[:], pkx[:], m2[:]))
                    pb.v(lambda k=k: V.tensor_copy(
                        recv[:, :, k:k + 1], sA6[:].unsqueeze(2)))
                    pb.v(lambda k=k, pkx=pkx: V.tensor_copy(
                        recv[:, :, 2 + k:3 + k], pkx[:].unsqueeze(2)))
                pb.dma(lambda: nc.sync.dma_start(
                    out=rec_s.rearrange("(t p) r -> p t r", p=128),
                    in_=recS.rearrange("p (t r) -> p t r", r=4)))
                pb.v(lambda: V.memset(initI[:], 0))
                pb.dma(lambda: nc.sync.dma_start(
                    out=didx_s.rearrange("(g p) o -> p g o", p=128),
                    in_=initI[:].unsqueeze(2),
                    allow_slow_non_contiguous=True))
                for t in range(QT):
                    pb.gdma(lambda t=t: G.indirect_dma_start(
                        recAll[:, 4 * t:4 * (t + 1)], None, rec_s[:, :],
                        bass.IndirectOffsetOnAxis(tokS[:, t:t + 1], 0)))
                recav = recAll.rearrange("p (t r) -> p t r", r=4)
                pb.v(lambda: V.tensor_copy(
                    sidxI1[:].unsqueeze(2), recav[:, :, 0:1]))
                pb.v(lambda: V.tensor_copy(
                    sidxI2[:].unsqueeze(2), recav[:, :, 1:2]))
                pb.v(lambda: V.tensor_copy(
                    pmS[:, 0:QT].unsqueeze(2), recav[:, :, 2:3]))
                pb.v(lambda: V.tensor_copy(
                    pmS[:, QT:2 * QT].unsqueeze(2), recav[:, :, 3:4]))
                for (pmsl, rsl, dst) in ((0, 0, sidxG1), (QT, 1, sidxG2)):
                    pb.v(lambda pmsl=pmsl: V.tensor_single_scalar(
                        qs_a[:, 0:QT], pmS[:, pmsl:pmsl + QT], 0.0, OP.is_gt))
                    pb.v(lambda rsl=rsl: V.tensor_tensor(
                        qs_b[:, 0:QT].unsqueeze(2), recav[:, :, rsl:rsl + 1],
                        qs_a[:, 0:QT].unsqueeze(2), OP.mult))
                    pb.v(lambda: V.tensor_scalar(
                        qs_b[:, 0:QT], qs_b[:, 0:QT], 0.0,
                        float(E * EPAD - 1), OP.max, OP.min))
                    pb.v(lambda dst=dst: V.tensor_copy(dst[:], qs_b[:, 0:QT]))
                for t in range(QT):
                    for sidx in (sidxI1, sidxI2):
                        pb.gdma(lambda t=t, sidx=sidx: G.indirect_dma_start(
                            didx_s[:, :],
                            bass.IndirectOffsetOnAxis(sidx[:, t:t + 1], 0),
                            tokS[:, t:t + 1], None,
                            bounds_check=E * EPAD - 1, oob_is_err=False))
                pb.dma(lambda: nc.sync.dma_start(
                    out=didxS[:],
                    in_=didx_s.rearrange("(g p) o -> p (g o)", p=128),
                    allow_slow_non_contiguous=True))

        if "C" in PHASES:
            _phC()
            pb.force_hz = False
        sBC.close()

        qTt = sb("qT", [128, HT * SH], F16)
        def _phD():
            with ExitStack() as sD:
                wqT_hi = sb("wqT_hi", [128, H * HT], BF16, stack=sD)
                wqT_lo = sb("wqT_lo", [128, H * HT], BF16, stack=sD)
                with ExitStack() as sDw:
                    wst = sb("wstD", [128, H * HT], stack=sDw)
                    prep_wT(wqd, wqT_hi, wqT_lo, wst)
                xcT_hi = sb("xcT_hiD", [128, HT * 512], BF16, stack=sD)
                xcT_lo = sb("xcT_loD", [128, HT * 512], BF16, stack=sD)
                xtl = sb("xtileD", [128, 2 * H], stack=sD)
                xts = [xtl[:, 0:H], xtl[:, H:2 * H]]
                xntile = sb("xntileD", [128, H], stack=sD)
                evb = sb("evbD", [128, 2048], stack=sD)
                xclaim = [0, 0]

                def xload(i):
                    pb.sdma(stX, lambda i=i: nc.sync.dma_start(
                        out=xts[i % 2],
                        in_=xq[128 * i:128 * (i + 1), :]))
                    xclaim[i % 2] = pb.scnt[stX]

                xload(0)
                for g in range(4):
                    pb.force_hz = True
                    for ii in range(4):
                        i = 4 * g + ii
                        if i + 1 < 16:
                            xload(i + 1)
                        pb.swait_at(stX, xclaim[i % 2])
                        ln_transpose_split(
                            xts[i % 2], xntile,
                            lambda k, ii=ii: (
                                xcT_hi[:, 512 * k + 128 * ii:
                                       512 * k + 128 * (ii + 1)],
                                xcT_lo[:, 512 * k + 128 * ii:
                                       512 * k + 128 * (ii + 1)]))
                    pb.force_hz = False
                    for oth in range(2):
                        for otl in range(4):
                            ot = 4 * oth + otl
                            nmm = 0
                            for ht in range(HT):
                                for (wa, xa) in ((wqT_hi, xcT_hi),
                                                 (wqT_hi, xcT_lo),
                                                 (wqT_lo, xcT_hi)):
                                    pb.t(lambda ot=ot, ht=ht, wa=wa, xa=xa,
                                         nmm=nmm: T.matmul(
                                             psD[:],
                                             wa[:, 1024 * ht + 128 * ot:
                                                1024 * ht + 128 * (ot + 1)],
                                             xa[:, 512 * ht:512 * (ht + 1)],
                                             start=(nmm == 0), stop=(nmm == 23)))
                                    nmm += 1
                            pb.v(lambda ot=ot, otl=otl: V.tensor_scalar(
                                evb[:, 512 * otl:512 * (otl + 1)], psD[:],
                                biasq[:, ot:ot + 1], None, OP.add))
                        for otl in range(4):
                            ot = 4 * oth + otl
                            quantize(evb[:, 512 * otl:512 * (otl + 1)],
                                     qTt[:, SH * ot + 512 * g:
                                         SH * ot + 512 * (g + 1)], 512)

        if "D" in PHASES:
            _phD()
        def _phE():
            with ExitStack() as sE:
                GE = 2
                sq = [sb(f"sq{i}", [128, S], F16, stack=sE) for i in range(GE)]
                sfb = [sb(f"sfb{i}", [128, 2048], stack=sE) for i in range(GE)]
                kc = sb("kc", [128, 2 * HT * 512], F16, stack=sE)
                kcs = [kc[:, 0:4096], kc[:, 4096:8192]]
                vc = sb("vc", [128, 2 * 4 * H], F16, stack=sE)
                vcs = [vc[:, 0:4096], vc[:, 4096:8192]]
                zfb = sb("zfb", [128, 2048], stack=sE)
                pfb = [sb(f"pfb{i}", [128, 2048], F16, stack=sE)
                       for i in range(GE)]
                pTt4 = sb("pTt4", [128, 512], F16, stack=sE)
                sums = sb("sums", [128, 2 * GE], stack=sE)
                oeb = sb("oeb", [128, 2 * H], stack=sE)
                oebs = [oeb[:, 0:H], oeb[:, H:2 * H]]
                kclaim = [0, 0]
                vclaim = [0, 0]
                oclaim = [0, 0]
                NGRP = QT // GE

                def kload(tc):
                    par = tc % 2
                    pb.sdma(stK, lambda tc=tc, par=par: nc.sync.dma_start(
                        out=kcs[par].rearrange("p (ot s) -> p ot s", s=512),
                        in_=kT_s.rearrange("(ot p) s -> p ot s", p=128)[
                            :, :, 512 * tc:512 * (tc + 1)]))
                    kclaim[par] = pb.scnt[stK]

                def vload(ls):
                    par = ls % 2
                    pb.sdma(stV, lambda ls=ls, par=par: nc.sync.dma_start(
                        out=vcs[par].rearrange("p (tt h) -> p tt h", h=H),
                        in_=v_s[512 * ls:512 * (ls + 1), :].rearrange(
                            "(tt p) h -> p tt h", p=128)))
                    vclaim[par] = pb.scnt[stV]

                pb.swait(stS)
                kload(0)
                vload(0)
                for grp in range(NGRP):
                    pb.force_hz = False
                    for band in range(4):
                        for tc4 in range(4):
                            tc = 4 * band + tc4
                            par = tc % 2
                            if not (grp == NGRP - 1 and tc == 15):
                                kload((tc + 1) % 16)
                            pb.swait_at(stK, kclaim[par])
                            for qi in range(GE):
                                q = GE * grp + qi
                                for ot in range(HT):
                                    pb.t(lambda q=q, ot=ot, par=par: T.matmul(
                                        psE[:],
                                        qTt[:, SH * ot + 128 * q:
                                            SH * ot + 128 * (q + 1)],
                                        kcs[par][:, 512 * ot:512 * (ot + 1)],
                                        start=(ot == 0), stop=(ot == 7)))
                                pb.s(lambda qi=qi, tc4=tc4: SC.activation(
                                    sfb[qi][:, 512 * tc4:512 * (tc4 + 1)],
                                    psE[:], AF.Copy, scale=1.0 / 32.0))
                        for qi in range(GE):
                            quantize(sfb[qi][:, 0:2048],
                                     sq[qi][:, 2048 * band:2048 * (band + 1)],
                                     2048)
                    pb.force_hz = True
                    for qi in range(GE):
                        mq = rowsc[:, 2 + qi:3 + qi]
                        negm = rowsc[:, 4 + qi:5 + qi]
                        rowA = rowsc[:, 8 + qi:9 + qi]
                        rowB = rowsc[:, 12 + qi:13 + qi]
                        pb.v(lambda qi=qi, mq=mq: V.reduce_max(
                            mq, sq[qi][:, :], AX), hz=True)
                        pb.v(lambda mq=mq, negm=negm: V.tensor_single_scalar(
                            negm, mq, -1.0, OP.mult))
                        for band in range(4):
                            pb.s(lambda qi=qi, band=band, negm=negm:
                                 SC.activation(
                                     qs_a[:, 0:2048],
                                     sq[qi][:, 2048 * band:2048 * (band + 1)],
                                     AF.Exp, bias=negm,
                                     accum_out=qs_b[:, band:band + 1]),
                                 hz=True)
                        pb.v(lambda qi=qi: V.reduce_sum(
                            sums[:, qi:qi + 1], qs_b[:, 0:4], AX), hz=True)
                        pb.v(lambda qi=qi: V.reciprocal(
                            sums[:, qi:qi + 1], sums[:, qi:qi + 1]), hz=True)
                        pb.s(lambda qi=qi: SC.activation(
                            sums[:, GE + qi:GE + qi + 1], sums[:, qi:qi + 1],
                            AF.Ln))
                        pb.v(lambda qi=qi, rowA=rowA: V.tensor_scalar(
                            rowA, sums[:, GE + qi:GE + qi + 1], INV_LN2, 64.0,
                            OP.mult, OP.add))
                        pb.v(lambda mq=mq, rowA=rowA: V.scalar_tensor_tensor(
                            rowA, mq, -INV_LN2, rowA, OP.mult, OP.add))
                        pb.v(lambda qi=qi, rowB=rowB: V.tensor_single_scalar(
                            rowB, sums[:, GE + qi:GE + qi + 1], 24.0 * LN2,
                            OP.add))
                    pb.force_hz = False
                    for band in range(4):
                        for qi in range(GE):
                            negm = rowsc[:, 4 + qi:5 + qi]
                            rowA = rowsc[:, 8 + qi:9 + qi]
                            rowB = rowsc[:, 12 + qi:13 + qi]
                            sqs = sq[qi][:, 2048 * band:2048 * (band + 1)]
                            a, b2 = qs_a[:, :], qs_b[:, :]
                            c, d, ii2 = qs_c[:, :], qs_d[:, :], qs_i[:, :]
                            pb.s(lambda sqs=sqs, negm=negm: SC.activation(
                                zfb[:], sqs, AF.Exp, bias=negm))
                            pb.v(lambda sqs=sqs, rowA=rowA, a=a: V.tensor_scalar(
                                a, sqs, INV_LN2, rowA, OP.mult, OP.add))
                            pb.s(lambda a=a, d=d: SC.activation(
                                d, a, AF.Relu, bias=consts[:, C_N505:C_N505 + 1]))
                            pb.v(lambda d=d, ii2=ii2: V.tensor_copy(ii2, d))
                            pb.v(lambda b2=b2, ii2=ii2: V.tensor_copy(b2, ii2))
                            pb.v(lambda a=a, b2=b2, d=d: V.scalar_tensor_tensor(
                                a, b2, -1.0, d, OP.mult, OP.add))
                            pb.v(lambda a=a: V.tensor_single_scalar(
                                a, a, 0.5, OP.is_ge))
                            pb.v(lambda a=a, b2=b2: V.scalar_tensor_tensor(
                                b2, a, 1.0, b2, OP.mult, OP.add))
                            pb.s(lambda a=a, b2=b2, rowB=rowB: SC.activation(
                                a, b2, AF.Exp, scale=-LN2, bias=rowB))
                            pb.s(lambda c=c, b2=b2: SC.activation(
                                c, b2, AF.Exp, scale=LN2,
                                bias=consts[:, C_M24:C_M24 + 1]))
                            pb.v(lambda a=a, b2=b2: V.tensor_mul(b2, zfb[:], a))
                            pb.v(lambda b2=b2, d=d: V.tensor_single_scalar(
                                d, b2, -0.5, OP.add))
                            pb.v(lambda d=d, ii2=ii2: V.tensor_copy(ii2, d))
                            pb.v(lambda d=d, ii2=ii2: V.tensor_copy(d, ii2))
                            pb.v(lambda a=a, b2=b2, d=d: V.scalar_tensor_tensor(
                                a, d, -1.0, b2, OP.mult, OP.add))
                            pb.v(lambda a=a: V.tensor_single_scalar(
                                a, a, 0.5, OP.is_ge))
                            pb.v(lambda a=a, d=d: V.scalar_tensor_tensor(
                                d, a, 1.0, d, OP.mult, OP.add))
                            pb.v(lambda d=d, c=c, qi=qi: V.tensor_mul(
                                pfb[qi][:], d, c))
                        for sub in range(4):
                            ls = 4 * band + sub
                            vpar = ls % 2
                            if not (grp == NGRP - 1 and ls == 15):
                                vload((ls + 1) % 16)
                            for qi in range(GE):
                                for tt_ in range(4):
                                    pb.t(lambda qi=qi, sub=sub, tt_=tt_:
                                         T.transpose(
                                             PS16[:, 128 * tt_:128 * (tt_ + 1)],
                                             pfb[qi][:, 512 * sub + 128 * tt_:
                                                     512 * sub + 128 * (tt_ + 1)],
                                             identH[:]))
                                pb.v(lambda: V.tensor_copy(
                                    pTt4[:], PS16[:, 0:512]))
                                if qi == 0:
                                    pb.swait_at(stV, vclaim[vpar])
                                for tt_ in range(4):
                                    tglob = 4 * ls + tt_
                                    for hh in range(2):
                                        pb.t(lambda qi=qi, tt_=tt_, hh=hh,
                                             tglob=tglob, vpar=vpar: T.matmul(
                                                 psO[qi][:, 512 * hh:
                                                         512 * (hh + 1)],
                                                 pTt4[:, 128 * tt_:
                                                      128 * (tt_ + 1)],
                                                 vcs[vpar][:,
                                                           H * tt_ + 512 * hh:
                                                           H * tt_ + 512 * (hh + 1)],
                                                 start=(tglob == 0),
                                                 stop=(tglob == 63)))
                    for qi in range(GE):
                        q = GE * grp + qi
                        pb.swait_at(stS, oclaim[qi])
                        pb.v(lambda qi=qi: V.tensor_copy(oebs[qi], psO[qi][:]))
                        pb.sdma(stS, lambda q=q, qi=qi: nc.sync.dma_start(
                            out=attn_s[128 * q:128 * (q + 1), :],
                            in_=oebs[qi]))
                        oclaim[qi] = pb.scnt[stS]

        if "E" in PHASES:
            _phE()
        def _phF():
            with ExitStack() as sF:
                xdT_hi = sb("xdT_hi", [128, HT * EPAD], BF16, stack=sF)
                xdT_lo = sb("xdT_lo", [128, HT * EPAD], BF16, stack=sF)
                yT_hi = sb("yT_hi", [128, FT * EPAD], BF16, stack=sF)
                yT_lo = sb("yT_lo", [128, FT * EPAD], BF16, stack=sF)
                xg = sb("xtileF", [128, 2 * H], stack=sF)
                xgs = [xg[:, 0:H], xg[:, H:2 * H]]
                xntile = sb("xntileF", [128, H], stack=sF)
                wcf = sb("wcf", [128, HT * 128], stack=sF)
                wcf3 = sb("wcf3", [128, HT * 128], stack=sF)
                w1h = sb("w1h", [128, HT * 128], BF16, stack=sF)
                w1l = sb("w1l", [128, HT * 128], BF16, stack=sF)
                w3h = sb("w3h", [128, HT * 128], BF16, stack=sF)
                w3l = sb("w3l", [128, HT * 128], BF16, stack=sF)
                silb = sb("silb", [128, EPAD], stack=sF)
                h3b = sb("h3b", [128, EPAD], stack=sF)
                wcf2 = sb("wcf2", [128, 512], stack=sF)
                w2h = sb("w2h", [128, 512], BF16, stack=sF)
                w2l = sb("w2l", [128, 512], BF16, stack=sF)
                gclaim = [0, 0]
                wclaim = [0, 0]
                w2claim = [0]
                osb = [silb[:, 0:512], h3b[:, 0:512]]
                oclaimF = [0, 0]

                def gload(jj):
                    par = jj % 2
                    pb.v(lambda par=par: V.memset(xgs[par], 0.0), hz=True)
                    pb.sdma(stG, lambda jj=jj, par=par: G.indirect_dma_start(
                        xgs[par], None, xb[:, :],
                        bass.IndirectOffsetOnAxis(
                            didxS[:, jj:jj + 1], 0)), eng="gpsimd")
                    gclaim[par] = pb.scnt[stG]

                def w13load(fidx):
                    e, ftile = divmod(fidx, FT)
                    for wi, (wmat, dst) in enumerate(((w1v, wcf),
                                                      (w3v, wcf3))):
                        pb.sdma(stW, lambda wmat=wmat, e=e, ftile=ftile,
                                dst=dst: nc.sync.dma_start(
                                    out=dst.rearrange(
                                        "p (ht f) -> p ht f", f=128),
                                    in_=wmat[H * e:H * (e + 1),
                                             128 * ftile:128 * (ftile + 1)]
                                    .rearrange("(ht p) f -> p ht f", p=128)))
                        wclaim[wi] = pb.scnt[stW]

                def w2load(widx):
                    e, r = divmod(widx, 2 * FT)
                    hh, ft = divmod(r, FT)
                    pb.sdma(stW2, lambda e=e, ft=ft, hh=hh:
                            nc.sync.dma_start(
                                out=wcf2[:],
                                in_=w2v[FF * e + 128 * ft:
                                        FF * e + 128 * (ft + 1),
                                        512 * hh:512 * (hh + 1)]))
                    w2claim[0] = pb.scnt[stW2]

                gload(0)
                w13load(0)
                w2load(0)
                for e in range(E):
                    pb.force_hz = True
                    for j in range(EPT):
                        jj = EPT * e + j
                        if jj + 1 < E * EPT:
                            gload(jj + 1)
                        pb.swait_at(stG, gclaim[jj % 2])
                        ln_transpose_split(
                            xgs[jj % 2], xntile,
                            lambda k, j=j: (
                                xdT_hi[:, EPAD * k + 128 * j:
                                       EPAD * k + 128 * (j + 1)],
                                xdT_lo[:, EPAD * k + 128 * j:
                                       EPAD * k + 128 * (j + 1)]))
                    pb.force_hz = False
                    with ExitStack() as sFh:
                        for ftile in range(FT):
                            if ftile == 0:
                                pb.swait_at(stS, oclaimF[0])
                                pb.swait_at(stS, oclaimF[1])
                            for wi, (whi, wlo, wsrc) in enumerate(
                                    ((w1h, w1l, wcf), (w3h, w3l, wcf3))):
                                pb.swait_at(stW, wclaim[wi])
                                pb.v(lambda whi=whi, wsrc=wsrc: V.tensor_copy(
                                    whi[:], wsrc[:]))
                                pb.v(lambda whi=whi, wlo=wlo, wsrc=wsrc:
                                     V.scalar_tensor_tensor(
                                         wlo[:], whi[:], -1.0, wsrc[:],
                                         OP.mult, OP.add))
                            fidx = e * FT + ftile
                            if fidx + 1 < E * FT:
                                w13load(fidx + 1)
                            for (whi, wlo, issil) in ((w1h, w1l, True),
                                                      (w3h, w3l, False)):
                                for (s0, sw, ps) in ((0, 512, psH),
                                                     (512, 128, psHb)):
                                    nmm = 0
                                    for ht in range(HT):
                                        for (wa, xa) in ((whi, xdT_hi),
                                                         (whi, xdT_lo),
                                                         (wlo, xdT_hi)):
                                            pb.t(lambda wa=wa, xa=xa, ht=ht,
                                                 s0=s0, sw=sw, ps=ps, nmm=nmm:
                                                 T.matmul(
                                                     ps[:, 0:sw],
                                                     wa[:, 128 * ht:
                                                        128 * (ht + 1)],
                                                     xa[:, EPAD * ht + s0:
                                                        EPAD * ht + s0 + sw],
                                                     start=(nmm == 0),
                                                     stop=(nmm == 23)))
                                            nmm += 1
                                if issil:
                                    pb.s(lambda psH=psH: SC.activation(
                                        silb[:, 0:512], psH[:], AF.Silu))
                                    pb.s(lambda psHb=psHb: SC.activation(
                                        silb[:, 512:640], psHb[:], AF.Silu))
                                else:
                                    pb.v(lambda psH=psH: V.tensor_copy(
                                        h3b[:, 0:512], psH[:]))
                                    pb.v(lambda psHb=psHb: V.tensor_copy(
                                        h3b[:, 512:640], psHb[:]))
                            pb.v(lambda: V.tensor_mul(silb[:], silb[:], h3b[:]))
                            yh = yT_hi[:, EPAD * ftile:EPAD * (ftile + 1)]
                            yl = yT_lo[:, EPAD * ftile:EPAD * (ftile + 1)]
                            pb.v(lambda yh=yh: V.tensor_copy(yh, silb[:]))
                            pb.v(lambda yh=yh, yl=yl: V.scalar_tensor_tensor(
                                yl, yh, -1.0, silb[:], OP.mult, OP.add))
                    with ExitStack() as sFw:
                        for hh in range(2):
                            for ft in range(FT):
                                pb.swait_at(stW2, w2claim[0])
                                pb.v(lambda: V.tensor_copy(w2h[:], wcf2[:]))
                                pb.v(lambda: V.scalar_tensor_tensor(
                                    w2l[:], w2h[:], -1.0, wcf2[:],
                                    OP.mult, OP.add))
                                widx = (2 * e + hh) * FT + ft
                                if widx + 1 < E * 2 * FT:
                                    w2load(widx + 1)
                                for j in range(EPT):
                                    for nmm3, (ya, wa) in enumerate(
                                            ((yT_hi, w2h), (yT_hi, w2l),
                                             (yT_lo, w2h))):
                                        pb.t(lambda ya=ya, wa=wa, ft=ft, j=j,
                                             nmm3=nmm3: T.matmul(
                                                 psO2[j][:],
                                                 ya[:, EPAD * ft + 128 * j:
                                                    EPAD * ft + 128 * (j + 1)],
                                                 wa[:],
                                                 start=(ft == 0 and nmm3 == 0),
                                                 stop=(ft == FT - 1
                                                       and nmm3 == 2)))
                            for j in range(EPT):
                                pb.swait_at(stS, oclaimF[j % 2])
                                pb.v(lambda j=j, psO2=psO2: V.tensor_copy(
                                    osb[j % 2], psO2[j][:]))
                                pb.sdma(stS, lambda e=e, j=j, hh=hh:
                                        nc.sync.dma_start(
                                            out=moe_s[EPAD * e + 128 * j:
                                                      EPAD * e + 128 * (j + 1),
                                                      512 * hh:512 * (hh + 1)],
                                            in_=osb[j % 2]))
                                oclaimF[j % 2] = pb.scnt[stS]

        if "F" in PHASES:
            _phF()
        def _phG():
            with ExitStack() as sG:
                r1b = sb("r1b", [128, H], stack=sG)
                r2b = sb("r2b", [128, H], stack=sG)
                xtile = sb("xtileG", [128, H], stack=sG)
                atile = sb("atileG", [128, H], stack=sG)
                ob = sb("ob", [128, H], stack=sG)

                pb.force_hz = True
                pb.swait(stS)
                for t in range(QT):
                    pb.v(lambda: V.memset(r1b[:], 0.0))
                    pb.v(lambda: V.memset(r2b[:], 0.0))
                    if G_GATHERS:
                        pb.gdma(lambda t=t: G.indirect_dma_start(
                            r1b[:], None, moe_s[:, :],
                            bass.IndirectOffsetOnAxis(sidxG1[:, t:t + 1], 0)))
                        pb.gdma(lambda t=t: G.indirect_dma_start(
                            r2b[:], None, moe_s[:, :],
                            bass.IndirectOffsetOnAxis(sidxG2[:, t:t + 1], 0)))
                    pb.dma(lambda t=t: nc.sync.dma_start(
                        out=xtile[:], in_=xq[128 * t:128 * (t + 1), :]))
                    pb.dma(lambda t=t: nc.sync.dma_start(
                        out=atile[:], in_=attn_s[128 * t:128 * (t + 1), :]))
                    pb.v(lambda: V.tensor_add(ob[:], xtile[:], atile[:]))
                    pb.v(lambda t=t: V.scalar_tensor_tensor(
                        ob[:], r1b[:], pmS[:, t:t + 1], ob[:], OP.mult, OP.add))
                    pb.v(lambda t=t: V.scalar_tensor_tensor(
                        ob[:], r2b[:], pmS[:, QT + t:QT + t + 1], ob[:],
                        OP.mult, OP.add))
                    pb.dma(lambda t=t: nc.sync.dma_start(
                        out=out[128 * t:128 * (t + 1), :], in_=ob[:]))

        if "G" in PHASES:
            _phG()
        pb.finish(block)
    return nc


_NC_CACHE = []


def kernel(L0_attn_x, L0_attn_Wq, L0_attn_bq, L0_attn_Wk, L0_attn_bk,
           L0_attn_Wv, L0_attn_bv, L0_ffn_gate_w, L0_ffn_w1, L0_ffn_w2,
           L0_ffn_w3):
    x = np.ascontiguousarray(np.asarray(L0_attn_x, dtype=np.float32))
    shared = {
        "wq": np.ascontiguousarray(L0_attn_Wq, dtype=np.float32),
        "wk": np.ascontiguousarray(L0_attn_Wk, dtype=np.float32),
        "wv": np.ascontiguousarray(L0_attn_Wv, dtype=np.float32),
        "bq": np.ascontiguousarray(L0_attn_bq, dtype=np.float32),
        "bk": np.ascontiguousarray(L0_attn_bk, dtype=np.float32),
        "bv": np.ascontiguousarray(L0_attn_bv, dtype=np.float32),
        "gw": np.ascontiguousarray(L0_ffn_gate_w, dtype=np.float32),
        "w1": np.ascontiguousarray(L0_ffn_w1, dtype=np.float32),
        "w2": np.ascontiguousarray(L0_ffn_w2, dtype=np.float32),
        "w3": np.ascontiguousarray(L0_ffn_w3, dtype=np.float32),
    }
    in_maps = []
    for c in range(8):
        b, q0 = c // 4, SH * (c % 4)
        tokids = np.ascontiguousarray(
            (q0 + np.arange(SH, dtype=np.int32)).reshape(QT, 128).T)
        m = dict(shared)
        m["xb"] = x[b]
        m["xq"] = np.ascontiguousarray(x[b, q0:q0 + SH])
        m["tok"] = tokids
        in_maps.append(m)

    if not _NC_CACHE:
        _NC_CACHE.append(build_program())
    nc = _NC_CACHE[0]
    res = run_bass_kernel_spmd(nc, in_maps, core_ids=list(range(8)))
    full = np.empty((B, S, H), dtype=np.float32)
    for c in range(8):
        b, q0 = c // 4, SH * (c % 4)
        full[b, q0:q0 + SH] = np.asarray(res.results[c]["out"])
    return full


# version 07
# speedup vs baseline: 2.2476x; 1.0071x over previous
import numpy as np
from contextlib import ExitStack

import concourse.bass as bass
import concourse.mybir as mybir
from concourse.bass_utils import run_bass_kernel_spmd

F32 = mybir.dt.float32
F16 = mybir.dt.float16
BF16 = mybir.dt.bfloat16
I32 = mybir.dt.int32
AF = mybir.ActivationFunctionType
OP = mybir.AluOpType
AX = mybir.AxisListType.X

B, S, H, FF, E = 2, 8192, 1024, 3584, 8
CAP = 2560
SH = 2048
NT = S // 128
NG = NT // 4
QT = SH // 128
EPAD = 640
EPT = EPAD // 128
HT = H // 128
FT = FF // 128
LN2 = float(np.log(2.0))
INV_LN2 = float(1.0 / np.log(2.0))
C1 = 0.5 * INV_LN2
BIGF = float(1 << 16)
PHASES = set("ABCDEFG")
G_GATHERS = True
DEBUG_OUTS = False


class PB:

    ENG = ("sync", "tensor", "vector", "scalar", "gpsimd")

    def __init__(self, nc, sem, gsem):
        self.force_hz = True
        self.nc = nc
        self.sem = sem
        self.gsem = gsem
        self.scnt = {}
        self.pending = []
        self.cnt = 0
        self.gcnt = 0
        self.run_eng = None
        self.run_ops = []
        self.run_inc = 0
        self.run_wait = None
        self.run_extra = []
        self.ops = {e: [] for e in self.ENG}

    def _flush(self):
        if not self.run_ops:
            return
        ops = self.run_ops
        inc = self.run_inc
        wait_at, gwait_at = self.run_wait
        extra = self.run_extra
        sem, gsem = self.sem, self.gsem
        last = len(ops) - 1

        def closure(engine, ops=ops, inc=inc, wait_at=wait_at,
                    gwait_at=gwait_at, last=last, extra=extra):
            for i, fn in enumerate(ops):
                if i == 0:
                    for es, ev in extra:
                        engine.wait_ge(es, ev)
                    if gwait_at:
                        engine.wait_ge(gsem, gwait_at)
                    if wait_at:
                        engine.wait_ge(sem, wait_at)
                inst = fn()
                if i == last:
                    inst.then_inc(sem, inc)

        self.ops[self.run_eng].append(closure)
        self.run_ops = []
        self.run_inc = 0
        self.run_eng = None
        self.run_wait = None
        self.run_extra = []

    def _emit(self, eng, fn, hazard=False):
        if eng != self.run_eng or self.pending:
            self._flush()
            self.run_eng = eng
            self.run_wait = (self.cnt, self.gcnt)
            self.run_extra = self.pending
            self.pending = []
        self.run_ops.append(fn)
        self.run_inc += 1
        self.cnt += 1
        if hazard or self.force_hz:
            self._flush()

    def _emit_dma(self, eng, fn, ginc, stream=None):
        self._flush()
        wait_at, gwait_at = self.cnt, self.gcnt
        extra = self.pending
        self.pending = []
        sem, gsem = self.sem, self.gsem
        tsem = stream if stream is not None else (gsem if ginc else sem)

        def closure(engine, fn=fn, wait_at=wait_at, gwait_at=gwait_at,
                    tsem=tsem, extra=extra):
            for es, ev in extra:
                engine.wait_ge(es, ev)
            if gwait_at:
                engine.wait_ge(gsem, gwait_at)
            if wait_at:
                engine.wait_ge(sem, wait_at)
            fn().then_inc(tsem, 16)

        self.ops[eng].append(closure)
        if stream is not None:
            self.scnt[stream] = self.scnt.get(stream, 0) + 16
        elif ginc:
            self.gcnt += 16
        else:
            self.cnt += 16

    def sdma(self, stream, fn, eng="sync"):
        self._emit_dma(eng, fn, 0, stream=stream)

    def swait(self, stream):
        v = self.scnt.get(stream, 0)
        if v:
            self.pending.append((stream, v))

    def swait_at(self, stream, val):
        if val:
            self.pending.append((stream, val))

    def t(self, fn, hz=False):
        self._emit("tensor", fn, hz)

    def v(self, fn, hz=False):
        self._emit("vector", fn, hz)

    def s(self, fn, hz=False):
        self._emit("scalar", fn, hz)

    def g(self, fn, hz=False):
        self._emit("gpsimd", fn, hz)

    def dma(self, fn):
        self._emit_dma("sync", fn, 0)

    def gdma(self, fn):
        self._emit_dma("gpsimd", fn, 1)

    def finish_waits(self):
        return list(self.scnt.items())

    def finish(self, block):
        self._flush()
        total = self.cnt
        gtotal = self.gcnt
        drains = self.finish_waits()
        sem, gsem = self.sem, self.gsem
        for name in self.ENG:
            ops = self.ops[name]

            def run(engine, ops=ops, name=name):
                for op in ops:
                    op(engine)
                if name == "sync":
                    if gtotal:
                        engine.wait_ge(gsem, gtotal)
                    engine.wait_ge(sem, total)
                    for ds, dv in drains:
                        engine.wait_ge(ds, dv)

            getattr(block, name)(run)


def build_program():
    nc = bass.Bass("TRN2", target_bir_lowering=False,
                   detect_race_conditions=False)

    xb = nc.dram_tensor("xb", [S, H], F32, kind="ExternalInput").ap()
    xq = nc.dram_tensor("xq", [SH, H], F32, kind="ExternalInput").ap()
    tok = nc.dram_tensor("tok", [128, QT], I32, kind="ExternalInput").ap()
    wqd = nc.dram_tensor("wq", [H, H], F32, kind="ExternalInput").ap()
    wkd = nc.dram_tensor("wk", [H, H], F32, kind="ExternalInput").ap()
    wvd = nc.dram_tensor("wv", [H, H], F32, kind="ExternalInput").ap()
    bqv = nc.dram_tensor("bq", [H], F32, kind="ExternalInput").ap()
    bkv = nc.dram_tensor("bk", [H], F32, kind="ExternalInput").ap()
    bvv = nc.dram_tensor("bv", [H], F32, kind="ExternalInput").ap()
    gw = nc.dram_tensor("gw", [E, H], F32, kind="ExternalInput").ap()
    w1 = nc.dram_tensor("w1", [E, H, FF], F32, kind="ExternalInput").ap()
    w2 = nc.dram_tensor("w2", [E, FF, H], F32, kind="ExternalInput").ap()
    w3 = nc.dram_tensor("w3", [E, H, FF], F32, kind="ExternalInput").ap()
    out = nc.dram_tensor("out", [SH, H], F32, kind="ExternalOutput").ap()

    DK = "ExternalOutput" if DEBUG_OUTS else "Internal"
    kT_s = nc.dram_tensor("kT_s", [H, S], F16, kind=DK).ap()
    v_s = nc.dram_tensor("v_s", [S, H], F16, kind=DK).ap()
    attn_s = nc.dram_tensor("attn_s", [SH, H], F32, kind=DK).ap()
    moe_s = nc.dram_tensor("moe_s", [E * EPAD, H], F32, kind=DK).ap()
    didx_s = nc.dram_tensor("didx_s", [E * EPAD, 1], I32, kind=DK).ap()
    rec_s = nc.dram_tensor("rec_s", [S, 4], F32, kind=DK).ap()
    maskh = nc.dram_tensor("maskh", [E, 2 * S], F32, kind=DK).ap()
    toth = nc.dram_tensor("toth", [128], F32, kind=DK).ap()
    offh = nc.dram_tensor("offh", [128], F32, kind=DK).ap()
    posh = nc.dram_tensor("posh", [2 * S], F32, kind=DK).ap()

    w1v = w1.rearrange("e h f -> (e h) f")
    w3v = w3.rearrange("e h f -> (e h) f")
    w2v = w2.rearrange("e f h -> (e f) h")

    st = ExitStack()
    with st:
        sem = st.enter_context(nc.semaphore("chain"))
        gsem = st.enter_context(nc.semaphore("gchain"))
        stX = st.enter_context(nc.semaphore("stX"))
        stS = st.enter_context(nc.semaphore("stS"))
        stK = st.enter_context(nc.semaphore("stK"))
        stV = st.enter_context(nc.semaphore("stV"))
        stW = st.enter_context(nc.semaphore("stW"))
        stW2 = st.enter_context(nc.semaphore("stW2"))
        stG = st.enter_context(nc.semaphore("stG"))
        block = st.enter_context(nc.Block())
        pb = PB(nc, sem, gsem)
        V, SC, T, G = nc.vector, nc.scalar, nc.tensor, nc.gpsimd

        def sb(name, shape, dt=F32, stack=None):
            return (stack or st).enter_context(nc.sbuf_tensor(name, shape, dt))

        def psum(name, shape, stack=None, dt=F32):
            return (stack or st).enter_context(
                nc.psum_tensor(name, shape, dt))

        PSF = st.enter_context(nc.psum_tensor("PSF", [128, 3584], F32))
        PS16 = st.enter_context(nc.psum_tensor("PS16", [128, 1024], F16))
        psA = PSF[:, 0:512]
        psT = PSF[:, 3072:3200]
        psT0 = psT
        psR = PSF[0:8, 0:1024]
        psRs = PSF[0:8, 1024:1152]
        psTT = PSF[:, 3200:3208]
        psD = PSF[:, 2048:2560]
        psMT = PSF[0:8, 3072:3200]
        psP = PSF[0:16, 0:1024]
        psB8 = PSF[:, 3200:3208]
        psE = PSF[:, 2048:2560]
        psTT2 = PS16[:, 0:128]
        psO = [PSF[:, 0:1024], PSF[:, 1024:2048]]
        psH = PSF[:, 2560:3072]
        psHb = PSF[:, 3072:3200]
        psO2 = [PSF[:, 0:512], PSF[:, 512:1024], PSF[:, 1024:1536],
                PSF[:, 1536:2048], PSF[:, 2048:2560]]

        ident = sb("ident", [128, 128])
        identH = sb("identH", [128, 128], F16)
        identI = sb("identI", [128, 128], I32)
        ones1 = sb("ones1", [1, 128])
        consts = sb("consts", [128, 8])
        C_EPS, C_135, C_N505, C_SM, C_SP, C_LNEPS, C_M24 = range(7)
        biasq = sb("biasq", [128, HT])
        biask = sb("biask", [128, HT])
        bvB = sb("bvB", [128, H])
        gwT1 = sb("gwT1", [128, HT * E], BF16)
        gwT2 = sb("gwT2", [128, HT * E], BF16)
        tokS = sb("tokS", [128, QT], I32)
        didxS = sb("didxS", [128, E * EPT], I32)
        sidxI1 = sb("sidxI1", [128, QT], I32)
        sidxI2 = sb("sidxI2", [128, QT], I32)
        sidxG1 = sb("sidxG1", [128, QT], I32)
        sidxG2 = sb("sidxG2", [128, QT], I32)
        pmS = sb("pmS", [128, 2 * QT])
        rowsc = sb("rowsc", [128, 16])

        PW = 2048
        qs_a = sb("qs_a", [128, PW])
        qs_b = sb("qs_b", [128, PW])
        qs_c = sb("qs_c", [128, PW])
        qs_d = sb("qs_d", [128, PW])
        qs_i = sb("qs_i", [128, PW], I32)

        sBC = ExitStack()
        logP = sb("logP", [128, NT * 72], stack=sBC)
        oh1 = sb("oh1", [128, NT * E], stack=sBC)
        oh2 = sb("oh2", [128, NT * E], stack=sBC)
        lgb = sb("lgb", [128, NT * E], stack=sBC)
        sA5 = sb("sA5", [128, NT * E], stack=sBC)
        sB5 = sb("sB5", [128, NT * E], stack=sBC)
        iotaE = sb("iotaE", [128, NT * E], stack=sBC)
        m1 = sb("m1", [128, NT], stack=sBC)
        m2 = sb("m2", [128, NT], stack=sBC)
        i1f = sb("i1f", [128, NT], stack=sBC)
        i2f = sb("i2f", [128, NT], stack=sBC)
        p1x = sb("p1x", [128, NT], stack=sBC)
        p2x = sb("p2x", [128, NT], stack=sBC)
        sA6 = sb("sA6", [128, NT], stack=sBC)
        sB6 = sb("sB6", [128, NT], stack=sBC)

        def e3(ap):
            return ap.rearrange("p (t e) -> p t e", e=8)

        def quantize(src, dst, width):
            prev = pb.force_hz
            pb.force_hz = False
            a = qs_a[:, 0:width]
            b2 = qs_b[:, 0:width]
            c = qs_c[:, 0:width]
            d = qs_d[:, 0:width]
            ii = qs_i[:, 0:width]
            pb.s(lambda: SC.activation(a, src, AF.Square))
            pb.s(lambda: SC.activation(
                b2, a, AF.Ln, bias=consts[:, C_EPS:C_EPS + 1]))
            pb.s(lambda: SC.activation(
                d, b2, AF.Relu, scale=C1, bias=consts[:, C_135:C_135 + 1]))
            pb.v(lambda: V.tensor_copy(ii, d))
            pb.v(lambda: V.tensor_copy(a, ii))
            pb.v(lambda: V.scalar_tensor_tensor(
                b2, a, -1.0, d, OP.mult, OP.add))
            pb.v(lambda: V.tensor_single_scalar(b2, b2, 0.5, OP.is_ge))
            pb.v(lambda: V.scalar_tensor_tensor(
                a, b2, 1.0, a, OP.mult, OP.add))
            pb.s(lambda: SC.activation(
                b2, a, AF.Exp, scale=-LN2, bias=consts[:, C_SM:C_SM + 1]))
            pb.s(lambda: SC.activation(
                c, a, AF.Exp, scale=LN2, bias=consts[:, C_SP:C_SP + 1]))
            pb.v(lambda: V.tensor_mul(a, src, b2))
            pb.v(lambda: V.tensor_single_scalar(d, a, 4095.5, OP.add))
            pb.v(lambda: V.tensor_copy(ii, d))
            pb.v(lambda: V.tensor_copy(d, ii))
            pb.v(lambda: V.scalar_tensor_tensor(
                b2, d, -1.0, a, OP.mult, OP.add))
            pb.v(lambda: V.tensor_single_scalar(
                b2, b2, -4095.5, OP.is_ge))
            pb.v(lambda: V.scalar_tensor_tensor(
                d, b2, 1.0, d, OP.mult, OP.add))
            pb.v(lambda: V.tensor_single_scalar(d, d, -4096.0, OP.add))
            pb.v(lambda: V.tensor_mul(dst, d, c))
            pb.force_hz = prev

        def layernorm(xt, xnt):
            prev_hz = pb.force_hz
            pb.force_hz = True
            rs = rowsc[:, 0:1]
            ss = rowsc[:, 1:2]
            pb.v(lambda: V.reduce_sum(rs, xt, AX), hz=True)
            pb.v(lambda: V.tensor_single_scalar(rs, rs, 1.0 / H, OP.mult))
            pb.v(lambda: V.tensor_scalar(xnt, xt, rs, None, OP.subtract))
            pb.s(lambda: SC.activation(
                qs_a[:, 0:H], xnt, AF.Square, accum_out=ss), hz=True)
            pb.s(lambda: SC.activation(
                ss, ss, AF.Sqrt, scale=1.0 / H,
                bias=consts[:, C_LNEPS:C_LNEPS + 1]))
            pb.v(lambda: V.reciprocal(ss, ss), hz=True)
            pb.v(lambda: V.tensor_scalar(xnt, xnt, ss, None, OP.mult))
            pb.force_hz = prev_hz

        def ln_transpose_split(xtile, xntile, slc):
            layernorm(xtile, xntile[:])
            prev = pb.force_hz
            pb.force_hz = False
            for k in range(HT):
                pb.t(lambda k=k: T.transpose(
                    PSF[:, 128 * k:128 * (k + 1)],
                    xntile[:, 128 * k:128 * (k + 1)], ident[:]))
            for k in range(HT):
                hi, lo = slc(k)
                ps = PSF[:, 128 * k:128 * (k + 1)]
                pb.v(lambda hi=hi, ps=ps: V.tensor_copy(hi, ps))
                pb.v(lambda hi=hi, lo=lo, ps=ps: V.scalar_tensor_tensor(
                    lo, hi, -1.0, ps, OP.mult, OP.add))
            pb.force_hz = prev

        def prep_wT(wsrc, hi, lo, wst):
            pb.dma(lambda: nc.sync.dma_start(
                out=wst.rearrange("p (ot h) -> p ot h", h=H),
                in_=wsrc.rearrange("(ot p) h -> p ot h", p=128)))
            prev = pb.force_hz
            pb.force_hz = False
            for ot in range(HT):
                for ht in range(HT):
                    pb.t(lambda ot=ot, ht=ht: T.transpose(
                        PSF[:, 128 * ht:128 * (ht + 1)],
                        wst[:, 1024 * ot + 128 * ht:
                            1024 * ot + 128 * (ht + 1)], ident[:]))
                for ht in range(HT):
                    ps = PSF[:, 128 * ht:128 * (ht + 1)]
                    dh = hi[:, 1024 * ht + 128 * ot:1024 * ht + 128 * (ot + 1)]
                    dl = lo[:, 1024 * ht + 128 * ot:1024 * ht + 128 * (ot + 1)]
                    pb.v(lambda dh=dh, ps=ps: V.tensor_copy(dh, ps))
                    pb.v(lambda dh=dh, dl=dl, ps=ps: V.scalar_tensor_tensor(
                        dl, dh, -1.0, ps, OP.mult, OP.add))
            pb.force_hz = prev

        pb.g(lambda: G.iota(identI[:], [[-1, 128]], base=0,
                            channel_multiplier=1))
        pb.v(lambda: V.tensor_copy(ident[:], identI[:]))
        pb.v(lambda: V.tensor_single_scalar(
            ident[:], ident[:], 0.0, OP.is_equal))
        pb.v(lambda: V.tensor_copy(identH[:], ident[:]))
        pb.v(lambda: V.memset(ones1[:], 1.0))
        for idx, val in [(C_EPS, 2.0 ** -40), (C_135, 13.5), (C_N505, -50.5),
                         (C_SM, 24.0 * LN2), (C_SP, -24.0 * LN2),
                         (C_LNEPS, 1e-5), (C_M24, -24.0 * LN2)]:
            pb.v(lambda idx=idx, val=val: V.memset(
                consts[:, idx:idx + 1], float(val)))
        pb.dma(lambda: nc.sync.dma_start(
            out=biasq[:], in_=bqv.rearrange("(j p) -> p j", p=128),
            allow_slow_non_contiguous=True))
        pb.dma(lambda: nc.sync.dma_start(
            out=biask[:], in_=bkv.rearrange("(j p) -> p j", p=128),
            allow_slow_non_contiguous=True))
        pb.dma(lambda: nc.sync.dma_start(out=tokS[:], in_=tok[:]))
        pb.g(lambda: G.iota(qs_i[:, 0:NT * E], [[0, NT], [1, E]],
                            base=0, channel_multiplier=0))
        pb.v(lambda: V.tensor_copy(iotaE[:], qs_i[:, 0:NT * E]))

        with ExitStack() as sA:
            bvrow = sb("bvrow", [1, H], stack=sA)
            gws = sb("gws", [8, H], stack=sA)
            gtf = sb("gtf", [128, 8], stack=sA)
            pb.dma(lambda: nc.sync.dma_start(
                out=bvrow[:], in_=bvv.rearrange("(o h) -> o h", o=1)))
            for half in range(2):
                pb.t(lambda half=half: T.matmul(
                    psA[:], ones1[:], bvrow[:, 512 * half:512 * (half + 1)],
                    start=True, stop=True))
                pb.v(lambda half=half: V.tensor_copy(
                    bvB[:, 512 * half:512 * (half + 1)], psA[:]))
            pb.dma(lambda: nc.sync.dma_start(out=gws[:], in_=gw[:]))
            for k in range(HT):
                pb.t(lambda k=k: T.transpose(
                    psA[:, 0:8], gws[:, 128 * k:128 * (k + 1)],
                    ident[0:8, 0:8]))
                pb.v(lambda: V.tensor_copy(gtf[:], psA[:, 0:8]))
                pb.v(lambda k=k: V.tensor_copy(
                    gwT1[:, 8 * k:8 * (k + 1)], gtf[:]))
                pb.v(lambda k=k: V.scalar_tensor_tensor(
                    gwT2[:, 8 * k:8 * (k + 1)], gwT1[:, 8 * k:8 * (k + 1)],
                    -1.0, gtf[:], OP.mult, OP.add))

        def _phB():
            with ExitStack() as sBo:
                wkT_hi = sb("wkT_hi", [128, H * HT], BF16, stack=sBo)
                wkT_lo = sb("wkT_lo", [128, H * HT], BF16, stack=sBo)
                wvT_hi = sb("wvT_hi", [128, H * HT], BF16, stack=sBo)
                wvT_lo = sb("wvT_lo", [128, H * HT], BF16, stack=sBo)
                with ExitStack() as sBw:
                    wst = sb("wst", [128, H * HT], stack=sBw)
                    prep_wT(wkd, wkT_hi, wkT_lo, wst)
                    prep_wT(wvd, wvT_hi, wvT_lo, wst)
                xcT_hi = sb("xcT_hi", [128, HT * 512], BF16, stack=sBo)
                xcT_lo = sb("xcT_lo", [128, HT * 512], BF16, stack=sBo)
                xtl = sb("xtile", [128, 2 * H], stack=sBo)
                xts = [xtl[:, 0:H], xtl[:, H:2 * H]]
                xntile = sb("xntile", [128, H], stack=sBo)
                evb = sb("evb", [128, 2048], stack=sBo)
                hfb = sb("hf16", [128, 2 * 2048], F16, stack=sBo)
                hfs = [hfb[:, 0:2048], hfb[:, 2048:4096]]
                rsc = sb("rsc", [8, 1152], stack=sBo)
                xclaim = [0, 0]
                hclaim = [0, 0]
                kvp = 0

                def xload(i):
                    pb.sdma(stX, lambda i=i: nc.sync.dma_start(
                        out=xts[i % 2],
                        in_=xb[128 * i:128 * (i + 1), :]))
                    xclaim[i % 2] = pb.scnt[stX]

                xload(0)
                for g in range(NG):
                    pb.force_hz = True
                    for ii in range(4):
                        i = 4 * g + ii
                        if i + 1 < 4 * NG:
                            xload(i + 1)
                        pb.swait_at(stX, xclaim[i % 2])
                        ln_transpose_split(
                            xts[i % 2], xntile,
                            lambda k, ii=ii: (
                                xcT_hi[:, 512 * k + 128 * ii:
                                       512 * k + 128 * (ii + 1)],
                                xcT_lo[:, 512 * k + 128 * ii:
                                       512 * k + 128 * (ii + 1)]))
                        pb.force_hz = False
                        for k in range(HT):
                            pb.t(lambda k=k, ii=ii: T.matmul(
                                psR[:, 128 * k:128 * (k + 1)],
                                gwT1[:, 8 * k:8 * (k + 1)],
                                xcT_hi[:, 512 * k + 128 * ii:
                                       512 * k + 128 * (ii + 1)],
                                start=True, stop=True))
                        nsm = 0
                        for k in range(HT):
                            for (ga, xa) in ((gwT2, xcT_hi), (gwT1, xcT_lo),
                                             (gwT2, xcT_lo)):
                                pb.t(lambda k=k, ii=ii, ga=ga, xa=xa, nsm=nsm:
                                     T.matmul(
                                         psRs[:], ga[:, 8 * k:8 * (k + 1)],
                                         xa[:, 512 * k + 128 * ii:
                                            512 * k + 128 * (ii + 1)],
                                         start=(nsm == 0), stop=(nsm == 23)))
                                nsm += 1
                        pb.v(lambda: V.tensor_copy(rsc[:, 0:1024], psR[:]))
                        pb.v(lambda: V.tensor_copy(rsc[:, 1024:1152], psRs[:]))
                        for pl in range(9):
                            pb.t(lambda pl=pl: T.transpose(
                                PSF[:, 3200 + 8 * pl:3208 + 8 * pl],
                                rsc[:, 128 * pl:128 * (pl + 1)],
                                ident[0:8, 0:8]))
                        for pl in range(9):
                            pb.v(lambda i=i, pl=pl: V.tensor_copy(
                                logP[:, 72 * i + 8 * pl:72 * i + 8 * (pl + 1)],
                                PSF[:, 3200 + 8 * pl:3208 + 8 * pl]))
                        pb.force_hz = True
                    pb.force_hz = False
                    for oth in range(2):
                        for otl in range(4):
                            ot = 4 * oth + otl
                            nmm = 0
                            for ht in range(HT):
                                for (wa, xa) in ((wkT_hi, xcT_hi),
                                                 (wkT_hi, xcT_lo),
                                                 (wkT_lo, xcT_hi)):
                                    pb.t(lambda ot=ot, ht=ht, wa=wa, xa=xa,
                                         nmm=nmm: T.matmul(
                                             psD[:],
                                             wa[:, 1024 * ht + 128 * ot:
                                                1024 * ht + 128 * (ot + 1)],
                                             xa[:, 512 * ht:512 * (ht + 1)],
                                             start=(nmm == 0), stop=(nmm == 23)))
                                    nmm += 1
                            pb.v(lambda ot=ot, otl=otl: V.tensor_scalar(
                                evb[:, 512 * otl:512 * (otl + 1)], psD[:],
                                biask[:, ot:ot + 1], None, OP.add))
                        hb = hfs[kvp % 2]
                        pb.swait_at(stS, hclaim[kvp % 2])
                        quantize(evb[:, 0:2048], hb, 2048)
                        pb.sdma(stS, lambda g=g, oth=oth, hb=hb:
                                nc.sync.dma_start(
                                    out=kT_s.rearrange(
                                        "(ot p) s -> p ot s", p=128)[
                                        :, 4 * oth:4 * (oth + 1),
                                        512 * g:512 * (g + 1)],
                                    in_=hb.rearrange(
                                        "p (ot s) -> p ot s", s=512)))
                        hclaim[kvp % 2] = pb.scnt[stS]
                        kvp += 1
                    for iih in range(2):
                        for iil in range(2):
                            ii = 2 * iih + iil
                            for oc in range(2):
                                nmm = 0
                                for ht in range(HT):
                                    for (xa, wa) in ((xcT_hi, wvT_hi),
                                                     (xcT_hi, wvT_lo),
                                                     (xcT_lo, wvT_hi)):
                                        pb.t(lambda ii=ii, oc=oc, ht=ht, xa=xa,
                                             wa=wa, nmm=nmm: T.matmul(
                                                 psD[:],
                                                 xa[:, 512 * ht + 128 * ii:
                                                    512 * ht + 128 * (ii + 1)],
                                                 wa[:, 1024 * ht + 512 * oc:
                                                    1024 * ht + 512 * (oc + 1)],
                                                 start=(nmm == 0),
                                                 stop=(nmm == 23)))
                                        nmm += 1
                                pb.v(lambda iil=iil, oc=oc: V.tensor_add(
                                    evb[:, 1024 * iil + 512 * oc:
                                        1024 * iil + 512 * (oc + 1)],
                                    psD[:], bvB[:, 512 * oc:512 * (oc + 1)]))
                        hb = hfs[kvp % 2]
                        pb.swait_at(stS, hclaim[kvp % 2])
                        quantize(evb[:, 0:2048], hb, 2048)
                        pb.sdma(stS, lambda g=g, iih=iih, hb=hb:
                                nc.sync.dma_start(
                                    out=v_s[512 * g + 256 * iih:
                                            512 * g + 256 * (iih + 1), :]
                                    .rearrange("(ii p) h -> p ii h", p=128),
                                    in_=hb.rearrange(
                                        "p (ii h) -> p ii h", h=1024)))
                        hclaim[kvp % 2] = pb.scnt[stS]
                        kvp += 1

        if "B" in PHASES:
            _phB()
        def _phC():
            pb.force_hz = True
            with ExitStack() as sC:
                eftS = sb("eftS", [128, NT * E], stack=sC)
                eftC = sb("eftC", [128, NT * E], stack=sC)
                eftX = sb("eftX", [128, NT * E], stack=sC)
                eftT = sb("eftT", [128, NT * E], stack=sC)
                eftZ = sb("eftZ", [128, NT * E], stack=sC)
                mbuf = sb("mbuf", [8, 2048], stack=sC)
                m8 = sb("m8", [8, 16], stack=sC)
                scanL = sb("scanL", [128, 1024], stack=sC)
                cg = sb("cg", [128, 1024], stack=sC)
                cmb = sb("cmb", [128, 1024], stack=sC)
                tot8 = sb("tot8", [8, 16], stack=sC)
                offs8 = sb("offs8", [8, 16], stack=sC)
                offCol = sb("offCol", [128, 1], stack=sC)
                selM = sb("selM", [128, 16], stack=sC)
                pos16 = sb("pos16", [16, 1024], stack=sC)
                posTok = sb("posTok", [128, 128], stack=sC)
                baseB = sb("baseB", [128, 32], stack=sC)
                bRow = sb("bRow", [1, 8], stack=sC)
                recS = sb("recS", [128, NT * 4], stack=sC)
                recAll = sb("recAll", [128, QT * 4], stack=sC)
                initI = sb("initI", [128, E * EPT], I32, stack=sC)
                iotaPf = sb("iotaPf", [128, 1], stack=sC)
                iota16 = sb("iota16", [128, 16], stack=sC)

                logv = logP.rearrange("p (t pl e) -> p t pl e", pl=9, e=8)

                pb.v(lambda: V.tensor_copy(e3(eftS[:, :]), logv[:, :, 0, :]))
                pb.v(lambda: V.memset(eftC[:], 0.0))
                for pl in range(1, 9):
                    pb.v(lambda pl=pl: V.tensor_copy(
                        e3(eftX[:, :]), logv[:, :, pl, :]))
                    pb.v(lambda: V.tensor_add(eftT[:], eftS[:], eftX[:]))
                    pb.v(lambda: V.tensor_sub(eftZ[:], eftT[:], eftS[:]))
                    pb.v(lambda: V.tensor_sub(sA5[:], eftT[:], eftZ[:]))
                    pb.v(lambda: V.tensor_sub(sA5[:], eftS[:], sA5[:]))
                    pb.v(lambda: V.tensor_add(eftC[:], eftC[:], sA5[:]))
                    pb.v(lambda: V.tensor_sub(eftZ[:], eftX[:], eftZ[:]))
                    pb.v(lambda: V.tensor_add(eftC[:], eftC[:], eftZ[:]))
                    pb.v(lambda: V.tensor_copy(eftS[:], eftT[:]))
                pb.v(lambda: V.tensor_add(lgb[:], eftS[:], eftC[:]))

                bc = [128, NT, 8]
                pb.v(lambda: V.reduce_max(m1[:].unsqueeze(2), e3(lgb[:, :]), AX), hz=True)
                pb.v(lambda: V.tensor_tensor(
                    e3(sA5[:, :]), e3(lgb[:, :]),
                    m1[:].unsqueeze(2).broadcast_to(bc), OP.is_ge))
                pb.v(lambda: V.tensor_single_scalar(
                    sB5[:], iotaE[:], -999.0, OP.add))
                pb.v(lambda: V.tensor_mul(sA5[:], sA5[:], sB5[:]))
                pb.v(lambda: V.tensor_single_scalar(sA5[:], sA5[:], 999.0, OP.add))
                pb.v(lambda: V.tensor_reduce(
                    i1f[:].unsqueeze(2), e3(sA5[:, :]), AX, OP.min), hz=True)
                pb.v(lambda: V.tensor_tensor(
                    e3(oh1[:, :]), e3(iotaE[:, :]),
                    i1f[:].unsqueeze(2).broadcast_to(bc), OP.is_equal))
                pb.v(lambda: V.scalar_tensor_tensor(
                    sB5[:], oh1[:], -1e30, lgb[:], OP.mult, OP.add))
                pb.v(lambda: V.reduce_max(m2[:].unsqueeze(2), e3(sB5[:, :]), AX), hz=True)
                pb.v(lambda: V.tensor_tensor(
                    e3(sA5[:, :]), e3(sB5[:, :]),
                    m2[:].unsqueeze(2).broadcast_to(bc), OP.is_ge))
                pb.v(lambda: V.tensor_single_scalar(
                    sB5[:], iotaE[:], -999.0, OP.add))
                pb.v(lambda: V.tensor_mul(sA5[:], sA5[:], sB5[:]))
                pb.v(lambda: V.tensor_single_scalar(sA5[:], sA5[:], 999.0, OP.add))
                pb.v(lambda: V.tensor_reduce(
                    i2f[:].unsqueeze(2), e3(sA5[:, :]), AX, OP.min), hz=True)
                pb.v(lambda: V.tensor_tensor(
                    e3(oh2[:, :]), e3(iotaE[:, :]),
                    i2f[:].unsqueeze(2).broadcast_to(bc), OP.is_equal))
                pb.v(lambda: V.tensor_tensor(
                    e3(sA5[:, :]), e3(lgb[:, :]),
                    m1[:].unsqueeze(2).broadcast_to(bc), OP.subtract))
                pb.s(lambda: SC.activation(sA5[:], sA5[:], AF.Exp))
                pb.v(lambda: V.reduce_sum(sA6[:].unsqueeze(2), e3(sA5[:, :]), AX), hz=True)
                pb.v(lambda: V.reciprocal(sA6[:], sA6[:]), hz=True)
                pb.v(lambda: V.tensor_tensor(
                    e3(sA5[:, :]), e3(sA5[:, :]),
                    sA6[:].unsqueeze(2).broadcast_to(bc), OP.mult))
                pb.v(lambda: V.tensor_mul(sB5[:], sA5[:], oh1[:]))
                pb.v(lambda: V.reduce_sum(p1x[:].unsqueeze(2), e3(sB5[:, :]), AX), hz=True)
                pb.v(lambda: V.tensor_mul(sB5[:], sA5[:], oh2[:]))
                pb.v(lambda: V.reduce_sum(p2x[:].unsqueeze(2), e3(sB5[:, :]), AX), hz=True)

                mb3 = mbuf.rearrange("e (w k) -> e w k", k=2)
                for grp in range(8):
                    for iw in range(8):
                        i = 8 * grp + iw
                        for k, oh in ((0, oh1), (1, oh2)):
                            pb.t(lambda i=i, oh=oh: T.transpose(
                                psMT[:], oh[:, 8 * i:8 * (i + 1)], ident[:]))
                            pb.v(lambda iw=iw, k=k: V.tensor_copy(
                                mb3[:, 128 * iw:128 * (iw + 1), k:k + 1],
                                psMT[:].unsqueeze(2)))
                    pb.dma(lambda grp=grp: nc.sync.dma_start(
                        out=maskh[:, 2048 * grp:2048 * (grp + 1)], in_=mbuf[:]))
                pb.dma(lambda: nc.sync.dma_start(
                    out=cmb[:], in_=maskh.rearrange("e (c w) -> (e c) w", c=16)))
                pb.v(lambda: V.tensor_tensor_scan(
                    scanL[:], cmb[:], cmb[:], 0.0, OP.add, OP.bypass), hz=True)
                pb.dma(lambda: nc.sync.dma_start(
                    out=toth.rearrange("(p o) -> p o", o=1),
                    in_=scanL[:, 1023:1024]))
                pb.dma(lambda: nc.sync.dma_start(
                    out=tot8[:], in_=toth.rearrange("(e c) -> e c", c=16)))
                pb.v(lambda: V.tensor_copy(m8[:], tot8[:]))
                pb.v(lambda: V.tensor_tensor_scan(
                    tot8[:], m8[:], m8[:], 0.0, OP.add, OP.bypass), hz=True)
                pb.v(lambda: V.memset(offs8[:, 0:1], 0.0))
                pb.v(lambda: V.tensor_copy(offs8[:, 1:16], tot8[:, 0:15]))
                pb.dma(lambda: nc.sync.dma_start(
                    out=offh.rearrange("(e c) -> e c", c=16), in_=offs8[:]))
                pb.dma(lambda: nc.sync.dma_start(
                    out=offCol[:], in_=offh.rearrange("(p o) -> p o", o=1)))
                pb.v(lambda: V.tensor_scalar(
                    cg[:], scanL[:], offCol[:, 0:1], None, OP.add))
                pb.v(lambda: V.tensor_mul(cg[:], cg[:], cmb[:]))
                pb.g(lambda: G.iota(qs_i[:, 0:1], [[0, 1]], base=0,
                                    channel_multiplier=1))
                pb.v(lambda: V.tensor_copy(iotaPf[:], qs_i[:, 0:1]))
                pb.g(lambda: G.iota(qs_i[:, 0:16], [[1, 16]], base=0,
                                    channel_multiplier=0))
                pb.v(lambda: V.tensor_copy(iota16[:], qs_i[:, 0:16]))
                t0, t1 = rowsc[:, 8:9], rowsc[:, 9:10]
                pb.v(lambda: V.tensor_scalar(
                    t0, iotaPf[:], 1.0 / 16.0, -0.5, OP.mult, OP.add))
                pb.v(lambda: V.tensor_copy(qs_i[:, 0:1], t0))
                pb.v(lambda: V.tensor_copy(t1, qs_i[:, 0:1]))
                pb.v(lambda: V.tensor_sub(t0, t0, t1))
                pb.v(lambda: V.tensor_single_scalar(t0, t0, 0.5, OP.is_ge))
                pb.v(lambda: V.scalar_tensor_tensor(
                    t1, t0, 1.0, t1, OP.mult, OP.add))
                pb.v(lambda: V.scalar_tensor_tensor(
                    t0, t1, -16.0, iotaPf[:], OP.mult, OP.add))
                pb.v(lambda: V.tensor_scalar(
                    selM[:], iota16[:], t0, None, OP.is_equal))
                for j in range(2):
                    pb.t(lambda j=j: T.matmul(
                        psP[:, 512 * j:512 * (j + 1)], selM[:],
                        cg[:, 512 * j:512 * (j + 1)], start=True, stop=True))
                pb.v(lambda: V.tensor_copy(pos16[:], psP[:]))
                pb.dma(lambda: nc.sync.dma_start(
                    out=posh.rearrange("(c w) -> c w", w=1024), in_=pos16[:]))
                pb.dma(lambda: nc.sync.dma_start(
                    out=posTok.rearrange("p (t k) -> p t k", k=2),
                    in_=posh.rearrange("(t p k) -> p t k", p=128, k=2),
                    allow_slow_non_contiguous=True))
                for q in range(4):
                    pb.t(lambda q=q: T.transpose(
                        psMT[0:1, 0:8], offs8[:, 4 * q:4 * q + 1],
                        ident[0:8, 0:8]))
                    pb.v(lambda: V.tensor_copy(bRow[:], psMT[0:1, 0:8]))
                    pb.t(lambda: T.matmul(
                        psB8[:], ones1[:], bRow[:], start=True, stop=True))
                    pb.v(lambda q=q: V.tensor_copy(
                        baseB[:, 8 * q:8 * (q + 1)], psB8[:]))
                posv = posTok.rearrange("p (t k) -> p t k", k=2)
                recv = recS.rearrange("p (t r) -> p t r", r=4)
                for k, ohk, ikf, pkx in ((0, oh1, i1f, p1x), (1, oh2, i2f, p2x)):
                    for q in range(4):
                        pb.v(lambda q=q, ohk=ohk: V.tensor_tensor(
                            e3(sA5[:, :])[:, 16 * q:16 * (q + 1), :],
                            e3(ohk[:, :])[:, 16 * q:16 * (q + 1), :],
                            baseB[:, 8 * q:8 * (q + 1)].unsqueeze(1)
                            .broadcast_to([128, 16, 8]), OP.mult))
                    pb.v(lambda: V.reduce_sum(
                        sA6[:].unsqueeze(2), e3(sA5[:, :]), AX), hz=True)
                    pb.v(lambda k=k: V.tensor_tensor(
                        sB6[:].unsqueeze(2), posv[:, :, k:k + 1],
                        sA6[:].unsqueeze(2), OP.subtract))
                    pb.v(lambda ikf=ikf: V.tensor_scalar(
                        sA6[:], ikf[:], 640.0, -1.0, OP.mult, OP.add))
                    pb.v(lambda: V.tensor_add(sA6[:], sA6[:], sB6[:]))
                    pb.v(lambda k=k: V.tensor_copy(
                        m2[:].unsqueeze(2), posv[:, :, k:k + 1]))
                    pb.v(lambda: V.tensor_single_scalar(
                        m2[:], m2[:], CAP + 0.5, OP.is_le))
                    pb.v(lambda: V.tensor_single_scalar(
                        sB6[:], sB6[:], EPAD + 0.5, OP.is_le))
                    pb.v(lambda: V.tensor_mul(m2[:], m2[:], sB6[:]))
                    pb.v(lambda: V.tensor_single_scalar(
                        sA6[:], sA6[:], BIGF, OP.add))
                    pb.v(lambda: V.scalar_tensor_tensor(
                        sA6[:], m2[:], -BIGF, sA6[:], OP.mult, OP.add))
                    pb.v(lambda pkx=pkx: V.tensor_mul(pkx[:], pkx[:], m2[:]))
                    pb.v(lambda k=k: V.tensor_copy(
                        recv[:, :, k:k + 1], sA6[:].unsqueeze(2)))
                    pb.v(lambda k=k, pkx=pkx: V.tensor_copy(
                        recv[:, :, 2 + k:3 + k], pkx[:].unsqueeze(2)))
                pb.dma(lambda: nc.sync.dma_start(
                    out=rec_s.rearrange("(t p) r -> p t r", p=128),
                    in_=recS.rearrange("p (t r) -> p t r", r=4)))
                pb.v(lambda: V.memset(initI[:], 0))
                pb.dma(lambda: nc.sync.dma_start(
                    out=didx_s.rearrange("(g p) o -> p g o", p=128),
                    in_=initI[:].unsqueeze(2),
                    allow_slow_non_contiguous=True))
                for t in range(QT):
                    pb.gdma(lambda t=t: G.indirect_dma_start(
                        recAll[:, 4 * t:4 * (t + 1)], None, rec_s[:, :],
                        bass.IndirectOffsetOnAxis(tokS[:, t:t + 1], 0)))
                recav = recAll.rearrange("p (t r) -> p t r", r=4)
                pb.v(lambda: V.tensor_copy(
                    sidxI1[:].unsqueeze(2), recav[:, :, 0:1]))
                pb.v(lambda: V.tensor_copy(
                    sidxI2[:].unsqueeze(2), recav[:, :, 1:2]))
                pb.v(lambda: V.tensor_copy(
                    pmS[:, 0:QT].unsqueeze(2), recav[:, :, 2:3]))
                pb.v(lambda: V.tensor_copy(
                    pmS[:, QT:2 * QT].unsqueeze(2), recav[:, :, 3:4]))
                for (pmsl, rsl, dst) in ((0, 0, sidxG1), (QT, 1, sidxG2)):
                    pb.v(lambda pmsl=pmsl: V.tensor_single_scalar(
                        qs_a[:, 0:QT], pmS[:, pmsl:pmsl + QT], 0.0, OP.is_gt))
                    pb.v(lambda rsl=rsl: V.tensor_tensor(
                        qs_b[:, 0:QT].unsqueeze(2), recav[:, :, rsl:rsl + 1],
                        qs_a[:, 0:QT].unsqueeze(2), OP.mult))
                    pb.v(lambda: V.tensor_scalar(
                        qs_b[:, 0:QT], qs_b[:, 0:QT], 0.0,
                        float(E * EPAD - 1), OP.max, OP.min))
                    pb.v(lambda dst=dst: V.tensor_copy(dst[:], qs_b[:, 0:QT]))
                for t in range(QT):
                    for sidx in (sidxI1, sidxI2):
                        pb.gdma(lambda t=t, sidx=sidx: G.indirect_dma_start(
                            didx_s[:, :],
                            bass.IndirectOffsetOnAxis(sidx[:, t:t + 1], 0),
                            tokS[:, t:t + 1], None,
                            bounds_check=E * EPAD - 1, oob_is_err=False))
                pb.dma(lambda: nc.sync.dma_start(
                    out=didxS[:],
                    in_=didx_s.rearrange("(g p) o -> p (g o)", p=128),
                    allow_slow_non_contiguous=True))

        if "C" in PHASES:
            _phC()
            pb.force_hz = False
        sBC.close()

        qTt = sb("qT", [128, HT * SH], F16)
        def _phD():
            with ExitStack() as sD:
                wqT_hi = sb("wqT_hi", [128, H * HT], BF16, stack=sD)
                wqT_lo = sb("wqT_lo", [128, H * HT], BF16, stack=sD)
                with ExitStack() as sDw:
                    wst = sb("wstD", [128, H * HT], stack=sDw)
                    prep_wT(wqd, wqT_hi, wqT_lo, wst)
                xcT_hi = sb("xcT_hiD", [128, HT * 512], BF16, stack=sD)
                xcT_lo = sb("xcT_loD", [128, HT * 512], BF16, stack=sD)
                xtl = sb("xtileD", [128, 2 * H], stack=sD)
                xts = [xtl[:, 0:H], xtl[:, H:2 * H]]
                xntile = sb("xntileD", [128, H], stack=sD)
                evb = sb("evbD", [128, 2048], stack=sD)
                xclaim = [0, 0]

                def xload(i):
                    pb.sdma(stX, lambda i=i: nc.sync.dma_start(
                        out=xts[i % 2],
                        in_=xq[128 * i:128 * (i + 1), :]))
                    xclaim[i % 2] = pb.scnt[stX]

                xload(0)
                for g in range(4):
                    pb.force_hz = True
                    for ii in range(4):
                        i = 4 * g + ii
                        if i + 1 < 16:
                            xload(i + 1)
                        pb.swait_at(stX, xclaim[i % 2])
                        ln_transpose_split(
                            xts[i % 2], xntile,
                            lambda k, ii=ii: (
                                xcT_hi[:, 512 * k + 128 * ii:
                                       512 * k + 128 * (ii + 1)],
                                xcT_lo[:, 512 * k + 128 * ii:
                                       512 * k + 128 * (ii + 1)]))
                    pb.force_hz = False
                    for oth in range(2):
                        for otl in range(4):
                            ot = 4 * oth + otl
                            nmm = 0
                            for ht in range(HT):
                                for (wa, xa) in ((wqT_hi, xcT_hi),
                                                 (wqT_hi, xcT_lo),
                                                 (wqT_lo, xcT_hi)):
                                    pb.t(lambda ot=ot, ht=ht, wa=wa, xa=xa,
                                         nmm=nmm: T.matmul(
                                             psD[:],
                                             wa[:, 1024 * ht + 128 * ot:
                                                1024 * ht + 128 * (ot + 1)],
                                             xa[:, 512 * ht:512 * (ht + 1)],
                                             start=(nmm == 0), stop=(nmm == 23)))
                                    nmm += 1
                            pb.v(lambda ot=ot, otl=otl: V.tensor_scalar(
                                evb[:, 512 * otl:512 * (otl + 1)], psD[:],
                                biasq[:, ot:ot + 1], None, OP.add))
                        for otl in range(4):
                            ot = 4 * oth + otl
                            quantize(evb[:, 512 * otl:512 * (otl + 1)],
                                     qTt[:, SH * ot + 512 * g:
                                         SH * ot + 512 * (g + 1)], 512)

        if "D" in PHASES:
            _phD()
        def _phE():
            with ExitStack() as sE:
                GE = 2
                sq = [sb(f"sq{i}", [128, S], F16, stack=sE) for i in range(GE)]
                sfb = [sb(f"sfb{i}", [128, 2048], stack=sE) for i in range(GE)]
                kc = sb("kc", [128, 2 * HT * 512], F16, stack=sE)
                kcs = [kc[:, 0:4096], kc[:, 4096:8192]]
                vc = sb("vc", [128, 2 * 4 * H], F16, stack=sE)
                vcs = [vc[:, 0:4096], vc[:, 4096:8192]]
                zfb = sb("zfb", [128, 2048], stack=sE)
                pfb = [sb(f"pfb{i}", [128, 2048], F16, stack=sE)
                       for i in range(GE)]
                pTt4 = sb("pTt4", [128, 512], F16, stack=sE)
                sums = sb("sums", [128, 2 * GE], stack=sE)
                oeb = sb("oeb", [128, 2 * H], stack=sE)
                oebs = [oeb[:, 0:H], oeb[:, H:2 * H]]
                kclaim = [0, 0]
                vclaim = [0, 0]
                oclaim = [0, 0]
                NGRP = QT // GE

                def kload(tc):
                    par = tc % 2
                    pb.sdma(stK, lambda tc=tc, par=par: nc.sync.dma_start(
                        out=kcs[par].rearrange("p (ot s) -> p ot s", s=512),
                        in_=kT_s.rearrange("(ot p) s -> p ot s", p=128)[
                            :, :, 512 * tc:512 * (tc + 1)]))
                    kclaim[par] = pb.scnt[stK]

                def vload(ls):
                    par = ls % 2
                    pb.sdma(stV, lambda ls=ls, par=par: nc.sync.dma_start(
                        out=vcs[par].rearrange("p (tt h) -> p tt h", h=H),
                        in_=v_s[512 * ls:512 * (ls + 1), :].rearrange(
                            "(tt p) h -> p tt h", p=128)))
                    vclaim[par] = pb.scnt[stV]

                pb.swait(stS)
                kload(0)
                vload(0)
                for grp in range(NGRP):
                    pb.force_hz = False
                    for band in range(4):
                        for tc4 in range(4):
                            tc = 4 * band + tc4
                            par = tc % 2
                            if not (grp == NGRP - 1 and tc == 15):
                                kload((tc + 1) % 16)
                            pb.swait_at(stK, kclaim[par])
                            for qi in range(GE):
                                q = GE * grp + qi
                                for ot in range(HT):
                                    pb.t(lambda q=q, ot=ot, par=par: T.matmul(
                                        psE[:],
                                        qTt[:, SH * ot + 128 * q:
                                            SH * ot + 128 * (q + 1)],
                                        kcs[par][:, 512 * ot:512 * (ot + 1)],
                                        start=(ot == 0), stop=(ot == 7)))
                                pb.s(lambda qi=qi, tc4=tc4: SC.activation(
                                    sfb[qi][:, 512 * tc4:512 * (tc4 + 1)],
                                    psE[:], AF.Copy, scale=1.0 / 32.0))
                        for qi in range(GE):
                            quantize(sfb[qi][:, 0:2048],
                                     sq[qi][:, 2048 * band:2048 * (band + 1)],
                                     2048)
                    pb.force_hz = True
                    for qi in range(GE):
                        mq = rowsc[:, 2 + qi:3 + qi]
                        negm = rowsc[:, 4 + qi:5 + qi]
                        rowA = rowsc[:, 8 + qi:9 + qi]
                        rowB = rowsc[:, 12 + qi:13 + qi]
                        pb.v(lambda qi=qi, mq=mq: V.reduce_max(
                            mq, sq[qi][:, :], AX), hz=True)
                        pb.v(lambda mq=mq, negm=negm: V.tensor_single_scalar(
                            negm, mq, -1.0, OP.mult))
                        for band in range(4):
                            pb.s(lambda qi=qi, band=band, negm=negm:
                                 SC.activation(
                                     qs_a[:, 0:2048],
                                     sq[qi][:, 2048 * band:2048 * (band + 1)],
                                     AF.Exp, bias=negm,
                                     accum_out=qs_b[:, band:band + 1]),
                                 hz=True)
                        pb.v(lambda qi=qi: V.reduce_sum(
                            sums[:, qi:qi + 1], qs_b[:, 0:4], AX), hz=True)
                        pb.v(lambda qi=qi: V.reciprocal(
                            sums[:, qi:qi + 1], sums[:, qi:qi + 1]), hz=True)
                        pb.s(lambda qi=qi: SC.activation(
                            sums[:, GE + qi:GE + qi + 1], sums[:, qi:qi + 1],
                            AF.Ln))
                        pb.v(lambda qi=qi, rowA=rowA: V.tensor_scalar(
                            rowA, sums[:, GE + qi:GE + qi + 1], INV_LN2, 64.0,
                            OP.mult, OP.add))
                        pb.v(lambda mq=mq, rowA=rowA: V.scalar_tensor_tensor(
                            rowA, mq, -INV_LN2, rowA, OP.mult, OP.add))
                        pb.v(lambda qi=qi, rowB=rowB: V.tensor_single_scalar(
                            rowB, sums[:, GE + qi:GE + qi + 1], 24.0 * LN2,
                            OP.add))
                    pb.force_hz = False
                    for band in range(4):
                        for qi in range(GE):
                            negm = rowsc[:, 4 + qi:5 + qi]
                            rowA = rowsc[:, 8 + qi:9 + qi]
                            rowB = rowsc[:, 12 + qi:13 + qi]
                            sqs = sq[qi][:, 2048 * band:2048 * (band + 1)]
                            a, b2 = qs_a[:, :], qs_b[:, :]
                            c, d, ii2 = qs_c[:, :], qs_d[:, :], qs_i[:, :]
                            pb.s(lambda sqs=sqs, negm=negm: SC.activation(
                                zfb[:], sqs, AF.Exp, bias=negm))
                            pb.v(lambda sqs=sqs, rowA=rowA, a=a: V.tensor_scalar(
                                a, sqs, INV_LN2, rowA, OP.mult, OP.add))
                            pb.s(lambda a=a, d=d: SC.activation(
                                d, a, AF.Relu, bias=consts[:, C_N505:C_N505 + 1]))
                            pb.v(lambda d=d, ii2=ii2: V.tensor_copy(ii2, d))
                            pb.v(lambda b2=b2, ii2=ii2: V.tensor_copy(b2, ii2))
                            pb.v(lambda a=a, b2=b2, d=d: V.scalar_tensor_tensor(
                                a, b2, -1.0, d, OP.mult, OP.add))
                            pb.v(lambda a=a: V.tensor_single_scalar(
                                a, a, 0.5, OP.is_ge))
                            pb.v(lambda a=a, b2=b2: V.scalar_tensor_tensor(
                                b2, a, 1.0, b2, OP.mult, OP.add))
                            pb.s(lambda a=a, b2=b2, rowB=rowB: SC.activation(
                                a, b2, AF.Exp, scale=-LN2, bias=rowB))
                            pb.s(lambda c=c, b2=b2: SC.activation(
                                c, b2, AF.Exp, scale=LN2,
                                bias=consts[:, C_M24:C_M24 + 1]))
                            pb.v(lambda a=a, b2=b2: V.tensor_mul(b2, zfb[:], a))
                            pb.v(lambda b2=b2, d=d: V.tensor_single_scalar(
                                d, b2, -0.5, OP.add))
                            pb.v(lambda d=d, ii2=ii2: V.tensor_copy(ii2, d))
                            pb.v(lambda d=d, ii2=ii2: V.tensor_copy(d, ii2))
                            pb.v(lambda a=a, b2=b2, d=d: V.scalar_tensor_tensor(
                                a, d, -1.0, b2, OP.mult, OP.add))
                            pb.v(lambda a=a: V.tensor_single_scalar(
                                a, a, 0.5, OP.is_ge))
                            pb.v(lambda a=a, d=d: V.scalar_tensor_tensor(
                                d, a, 1.0, d, OP.mult, OP.add))
                            pb.v(lambda d=d, c=c, qi=qi: V.tensor_mul(
                                pfb[qi][:], d, c))
                        for sub in range(4):
                            ls = 4 * band + sub
                            vpar = ls % 2
                            if not (grp == NGRP - 1 and ls == 15):
                                vload((ls + 1) % 16)
                            for qi in range(GE):
                                for tt_ in range(4):
                                    pb.t(lambda qi=qi, sub=sub, tt_=tt_:
                                         T.transpose(
                                             PS16[:, 128 * tt_:128 * (tt_ + 1)],
                                             pfb[qi][:, 512 * sub + 128 * tt_:
                                                     512 * sub + 128 * (tt_ + 1)],
                                             identH[:]))
                                pb.v(lambda: V.tensor_copy(
                                    pTt4[:], PS16[:, 0:512]))
                                if qi == 0:
                                    pb.swait_at(stV, vclaim[vpar])
                                for tt_ in range(4):
                                    tglob = 4 * ls + tt_
                                    for hh in range(2):
                                        pb.t(lambda qi=qi, tt_=tt_, hh=hh,
                                             tglob=tglob, vpar=vpar: T.matmul(
                                                 psO[qi][:, 512 * hh:
                                                         512 * (hh + 1)],
                                                 pTt4[:, 128 * tt_:
                                                      128 * (tt_ + 1)],
                                                 vcs[vpar][:,
                                                           H * tt_ + 512 * hh:
                                                           H * tt_ + 512 * (hh + 1)],
                                                 start=(tglob == 0),
                                                 stop=(tglob == 63)))
                    for qi in range(GE):
                        q = GE * grp + qi
                        pb.swait_at(stS, oclaim[qi])
                        pb.v(lambda qi=qi: V.tensor_copy(oebs[qi], psO[qi][:]))
                        pb.sdma(stS, lambda q=q, qi=qi: nc.sync.dma_start(
                            out=attn_s[128 * q:128 * (q + 1), :],
                            in_=oebs[qi]))
                        oclaim[qi] = pb.scnt[stS]

        if "E" in PHASES:
            _phE()
        def _phF():
            with ExitStack() as sF:
                xdT_hi = sb("xdT_hi", [128, HT * EPAD], BF16, stack=sF)
                xdT_lo = sb("xdT_lo", [128, HT * EPAD], BF16, stack=sF)
                yT_hi = sb("yT_hi", [128, FT * EPAD], BF16, stack=sF)
                yT_lo = sb("yT_lo", [128, FT * EPAD], BF16, stack=sF)
                xg = sb("xtileF", [128, 2 * H], stack=sF)
                xgs = [xg[:, 0:H], xg[:, H:2 * H]]
                xntile = sb("xntileF", [128, H], stack=sF)
                wcf = sb("wcf", [128, HT * 128], stack=sF)
                wcf3 = sb("wcf3", [128, HT * 128], stack=sF)
                w1h = sb("w1h", [128, HT * 128], BF16, stack=sF)
                w1l = sb("w1l", [128, HT * 128], BF16, stack=sF)
                w3h = sb("w3h", [128, HT * 128], BF16, stack=sF)
                w3l = sb("w3l", [128, HT * 128], BF16, stack=sF)
                silb = sb("silb", [128, EPAD], stack=sF)
                h3b = sb("h3b", [128, EPAD], stack=sF)
                wcf2 = sb("wcf2", [128, 512], stack=sF)
                w2h = sb("w2h", [128, 512], BF16, stack=sF)
                w2l = sb("w2l", [128, 512], BF16, stack=sF)
                gclaim = [0, 0]
                wclaim = [0, 0]
                w2claim = [0]
                osb = [silb[:, 0:512], h3b[:, 0:512]]
                oclaimF = [0, 0]

                def gload(jj):
                    par = jj % 2
                    pb.v(lambda par=par: V.memset(xgs[par], 0.0), hz=True)
                    pb.sdma(stG, lambda jj=jj, par=par: G.indirect_dma_start(
                        xgs[par], None, xb[:, :],
                        bass.IndirectOffsetOnAxis(
                            didxS[:, jj:jj + 1], 0)), eng="gpsimd")
                    gclaim[par] = pb.scnt[stG]

                def w13load(fidx):
                    e, ftile = divmod(fidx, FT)
                    for wi, (wmat, dst) in enumerate(((w1v, wcf),
                                                      (w3v, wcf3))):
                        pb.sdma(stW, lambda wmat=wmat, e=e, ftile=ftile,
                                dst=dst: nc.sync.dma_start(
                                    out=dst.rearrange(
                                        "p (ht f) -> p ht f", f=128),
                                    in_=wmat[H * e:H * (e + 1),
                                             128 * ftile:128 * (ftile + 1)]
                                    .rearrange("(ht p) f -> p ht f", p=128)))
                        wclaim[wi] = pb.scnt[stW]

                def w2load(widx):
                    e, r = divmod(widx, 2 * FT)
                    hh, ft = divmod(r, FT)
                    pb.sdma(stW2, lambda e=e, ft=ft, hh=hh:
                            nc.sync.dma_start(
                                out=wcf2[:],
                                in_=w2v[FF * e + 128 * ft:
                                        FF * e + 128 * (ft + 1),
                                        512 * hh:512 * (hh + 1)]))
                    w2claim[0] = pb.scnt[stW2]

                gload(0)
                w13load(0)
                w2load(0)
                for e in range(E):
                    pb.force_hz = True
                    for j in range(EPT):
                        jj = EPT * e + j
                        if jj + 1 < E * EPT:
                            gload(jj + 1)
                        pb.swait_at(stG, gclaim[jj % 2])
                        ln_transpose_split(
                            xgs[jj % 2], xntile,
                            lambda k, j=j: (
                                xdT_hi[:, EPAD * k + 128 * j:
                                       EPAD * k + 128 * (j + 1)],
                                xdT_lo[:, EPAD * k + 128 * j:
                                       EPAD * k + 128 * (j + 1)]))
                    pb.force_hz = False
                    with ExitStack() as sFh:
                        for ftile in range(FT):
                            if ftile == 0:
                                pb.swait_at(stS, oclaimF[0])
                                pb.swait_at(stS, oclaimF[1])
                            for wi, (whi, wlo, wsrc) in enumerate(
                                    ((w1h, w1l, wcf), (w3h, w3l, wcf3))):
                                pb.swait_at(stW, wclaim[wi])
                                pb.v(lambda whi=whi, wsrc=wsrc: V.tensor_copy(
                                    whi[:], wsrc[:]))
                                pb.v(lambda whi=whi, wlo=wlo, wsrc=wsrc:
                                     V.scalar_tensor_tensor(
                                         wlo[:], whi[:], -1.0, wsrc[:],
                                         OP.mult, OP.add))
                            fidx = e * FT + ftile
                            if fidx + 1 < E * FT:
                                w13load(fidx + 1)
                            for (whi, wlo, issil) in ((w1h, w1l, True),
                                                      (w3h, w3l, False)):
                                for (s0, sw, ps) in ((0, 512, psH),
                                                     (512, 128, psHb)):
                                    nmm = 0
                                    for ht in range(HT):
                                        for (wa, xa) in ((whi, xdT_hi),
                                                         (whi, xdT_lo),
                                                         (wlo, xdT_hi)):
                                            pb.t(lambda wa=wa, xa=xa, ht=ht,
                                                 s0=s0, sw=sw, ps=ps, nmm=nmm:
                                                 T.matmul(
                                                     ps[:, 0:sw],
                                                     wa[:, 128 * ht:
                                                        128 * (ht + 1)],
                                                     xa[:, EPAD * ht + s0:
                                                        EPAD * ht + s0 + sw],
                                                     start=(nmm == 0),
                                                     stop=(nmm == 23)))
                                            nmm += 1
                                if issil:
                                    pb.s(lambda psH=psH: SC.activation(
                                        silb[:, 0:512], psH[:], AF.Silu))
                                    pb.s(lambda psHb=psHb: SC.activation(
                                        silb[:, 512:640], psHb[:], AF.Silu))
                                else:
                                    pb.v(lambda psH=psH: V.tensor_copy(
                                        h3b[:, 0:512], psH[:]))
                                    pb.v(lambda psHb=psHb: V.tensor_copy(
                                        h3b[:, 512:640], psHb[:]))
                            pb.v(lambda: V.tensor_mul(silb[:], silb[:], h3b[:]))
                            yh = yT_hi[:, EPAD * ftile:EPAD * (ftile + 1)]
                            yl = yT_lo[:, EPAD * ftile:EPAD * (ftile + 1)]
                            pb.v(lambda yh=yh: V.tensor_copy(yh, silb[:]))
                            pb.v(lambda yh=yh, yl=yl: V.scalar_tensor_tensor(
                                yl, yh, -1.0, silb[:], OP.mult, OP.add))
                    with ExitStack() as sFw:
                        for hh in range(2):
                            for ft in range(FT):
                                pb.swait_at(stW2, w2claim[0])
                                pb.v(lambda: V.tensor_copy(w2h[:], wcf2[:]))
                                pb.v(lambda: V.scalar_tensor_tensor(
                                    w2l[:], w2h[:], -1.0, wcf2[:],
                                    OP.mult, OP.add))
                                widx = (2 * e + hh) * FT + ft
                                if widx + 1 < E * 2 * FT:
                                    w2load(widx + 1)
                                for j in range(EPT):
                                    for nmm3, (ya, wa) in enumerate(
                                            ((yT_hi, w2h), (yT_hi, w2l),
                                             (yT_lo, w2h))):
                                        pb.t(lambda ya=ya, wa=wa, ft=ft, j=j,
                                             nmm3=nmm3: T.matmul(
                                                 psO2[j][:],
                                                 ya[:, EPAD * ft + 128 * j:
                                                    EPAD * ft + 128 * (j + 1)],
                                                 wa[:],
                                                 start=(ft == 0 and nmm3 == 0),
                                                 stop=(ft == FT - 1
                                                       and nmm3 == 2)))
                            for j in range(EPT):
                                pb.swait_at(stS, oclaimF[j % 2])
                                pb.v(lambda j=j, psO2=psO2: V.tensor_copy(
                                    osb[j % 2], psO2[j][:]))
                                pb.sdma(stS, lambda e=e, j=j, hh=hh:
                                        nc.sync.dma_start(
                                            out=moe_s[EPAD * e + 128 * j:
                                                      EPAD * e + 128 * (j + 1),
                                                      512 * hh:512 * (hh + 1)],
                                            in_=osb[j % 2]))
                                oclaimF[j % 2] = pb.scnt[stS]

        if "F" in PHASES:
            _phF()
        def _phG():
            with ExitStack() as sG:
                r1b = sb("r1b", [128, H], stack=sG)
                r2b = sb("r2b", [128, H], stack=sG)
                xtG = sb("xtileG", [128, 2 * H], stack=sG)
                xts = [xtG[:, 0:H], xtG[:, H:2 * H]]
                atG = sb("atileG", [128, 2 * H], stack=sG)
                ats = [atG[:, 0:H], atG[:, H:2 * H]]
                ob = sb("ob", [128, 2 * H], stack=sG)
                obs = [ob[:, 0:H], ob[:, H:2 * H]]
                xcl = [0, 0]
                acl = [0, 0]

                def gxload(t):
                    pb.sdma(stX, lambda t=t: nc.sync.dma_start(
                        out=xts[t % 2], in_=xq[128 * t:128 * (t + 1), :]))
                    xcl[t % 2] = pb.scnt[stX]
                    pb.sdma(stV, lambda t=t: nc.sync.dma_start(
                        out=ats[t % 2], in_=attn_s[128 * t:128 * (t + 1), :]))
                    acl[t % 2] = pb.scnt[stV]

                pb.force_hz = True
                pb.swait(stS)
                gxload(0)
                for t in range(QT):
                    if t + 1 < QT:
                        gxload(t + 1)
                    pb.v(lambda: V.memset(r1b[:], 0.0))
                    pb.v(lambda: V.memset(r2b[:], 0.0))
                    if G_GATHERS:
                        pb.gdma(lambda t=t: G.indirect_dma_start(
                            r1b[:], None, moe_s[:, :],
                            bass.IndirectOffsetOnAxis(sidxG1[:, t:t + 1], 0)))
                        pb.gdma(lambda t=t: G.indirect_dma_start(
                            r2b[:], None, moe_s[:, :],
                            bass.IndirectOffsetOnAxis(sidxG2[:, t:t + 1], 0)))
                    pb.swait_at(stX, xcl[t % 2])
                    pb.swait_at(stV, acl[t % 2])
                    pb.v(lambda t=t: V.tensor_add(
                        obs[t % 2], xts[t % 2], ats[t % 2]))
                    pb.v(lambda t=t: V.scalar_tensor_tensor(
                        obs[t % 2], r1b[:], pmS[:, t:t + 1], obs[t % 2],
                        OP.mult, OP.add))
                    pb.v(lambda t=t: V.scalar_tensor_tensor(
                        obs[t % 2], r2b[:], pmS[:, QT + t:QT + t + 1],
                        obs[t % 2], OP.mult, OP.add))
                    pb.dma(lambda t=t: nc.sync.dma_start(
                        out=out[128 * t:128 * (t + 1), :], in_=obs[t % 2]))

        if "G" in PHASES:
            _phG()
        pb.finish(block)
    return nc


_NC_CACHE = []


def kernel(L0_attn_x, L0_attn_Wq, L0_attn_bq, L0_attn_Wk, L0_attn_bk,
           L0_attn_Wv, L0_attn_bv, L0_ffn_gate_w, L0_ffn_w1, L0_ffn_w2,
           L0_ffn_w3):
    x = np.ascontiguousarray(np.asarray(L0_attn_x, dtype=np.float32))
    shared = {
        "wq": np.ascontiguousarray(L0_attn_Wq, dtype=np.float32),
        "wk": np.ascontiguousarray(L0_attn_Wk, dtype=np.float32),
        "wv": np.ascontiguousarray(L0_attn_Wv, dtype=np.float32),
        "bq": np.ascontiguousarray(L0_attn_bq, dtype=np.float32),
        "bk": np.ascontiguousarray(L0_attn_bk, dtype=np.float32),
        "bv": np.ascontiguousarray(L0_attn_bv, dtype=np.float32),
        "gw": np.ascontiguousarray(L0_ffn_gate_w, dtype=np.float32),
        "w1": np.ascontiguousarray(L0_ffn_w1, dtype=np.float32),
        "w2": np.ascontiguousarray(L0_ffn_w2, dtype=np.float32),
        "w3": np.ascontiguousarray(L0_ffn_w3, dtype=np.float32),
    }
    in_maps = []
    for c in range(8):
        b, q0 = c // 4, SH * (c % 4)
        tokids = np.ascontiguousarray(
            (q0 + np.arange(SH, dtype=np.int32)).reshape(QT, 128).T)
        m = dict(shared)
        m["xb"] = x[b]
        m["xq"] = np.ascontiguousarray(x[b, q0:q0 + SH])
        m["tok"] = tokids
        in_maps.append(m)

    if not _NC_CACHE:
        _NC_CACHE.append(build_program())
    nc = _NC_CACHE[0]
    res = run_bass_kernel_spmd(nc, in_maps, core_ids=list(range(8)))
    full = np.empty((B, S, H), dtype=np.float32)
    for c in range(8):
        b, q0 = c // 4, SH * (c % 4)
        full[b, q0:q0 + SH] = np.asarray(res.results[c]["out"])
    return full
